# version 06
# baseline (speedup 1.0000x reference)
import contextlib
import math
from collections import defaultdict
import numpy as np
import ml_dtypes
import concourse.bass as bass
import concourse.mybir as mybir
from concourse.bass_utils import run_bass_kernel_spmd







F32 = mybir.dt.float32
BF16 = mybir.dt.bfloat16
AF = mybir.ActivationFunctionType
ALU = mybir.AluOpType
AX = mybir.AxisListType

ENGS = ("pe", "dve", "act", "pool", "sp")
ENG_ATTR = {"pe": "tensor", "dve": "vector", "act": "scalar", "pool": "gpsimd", "sp": "sync"}


def tmax(*toks):
    best = {}
    for t in toks:
        if t is None:
            continue
        if isinstance(t, list):
            for u in t:
                if u is None:
                    continue
                s, v = u
                best[s] = max(best.get(s, 0), v)
            continue
        s, v = t
        best[s] = max(best.get(s, 0), v)
    return [(s, v) for s, v in best.items()]


class Prog:
    def __init__(self, nc):
        self.nc = nc
        self.es = contextlib.ExitStack()
        self.ops = {e: [] for e in ENGS}
        self.sems = {}
        self.cnt = defaultdict(int)
        self.waited = {e: defaultdict(int) for e in ENGS}
        self.dma_sems = set()
        self.n_inst = 0
        self.prefix = ""
        self._phase_mode = False
        self._phase_addr = 0
        self._max_addr = 0
        self._arena_base = None
        self._arena_size = 212832
        self._psum_pool = {}
        self._psum_idx = {}

    def sem(self, name):
        if name not in self.sems:
            self.sems[name] = self.es.enter_context(self.nc.semaphore(name))
        return self.sems[name]

    def begin_phase(self, prefix):
        self.prefix = prefix
        self._phase_mode = True
        self._phase_addr = 0
        self._psum_idx = {}
        if self._arena_base is None:
            backing = self.es.enter_context(self.nc.sbuf_tensor(
                "sb_arena", [self.nc.NUM_PARTITIONS, self._arena_size], mybir.dt.uint8))
            self._arena_base = self.nc.lookup_mloc(backing).addr

    def sbuf(self, name, shape, dtype):
        if not self._phase_mode:
            return self.es.enter_context(self.nc.sbuf_tensor("sb_" + name, shape, dtype))
        per_part = 1
        for s in shape[1:]:
            per_part *= s
        per_part *= mybir.dt.size(dtype)
        addr = self._phase_addr
        self._phase_addr = (addr + per_part + 31) & ~31
        self._max_addr = max(self._max_addr, self._phase_addr)
        assert self._phase_addr <= self._arena_size, (
            "phase %s overflows SBUF arena: %d > %d" %
            (self.prefix, self._phase_addr, self._arena_size))
        return self.nc.alloc_sbuf_tensor_at(
            "sb_" + self.prefix + name, shape, dtype,
            offset=self._arena_base + addr)

    def psum(self, name, shape, dtype=F32):
        if not self._phase_mode:
            return self.es.enter_context(self.nc.psum_tensor(name, shape, dtype))
        key = (tuple(shape), dtype)
        idx = self._psum_idx.get(key, 0)
        self._psum_idx[key] = idx + 1
        pool = self._psum_pool.setdefault(key, [])
        if idx < len(pool):
            return pool[idx]
        t = self.es.enter_context(self.nc.psum_tensor(
            "ps_%d_%s_%d" % (shape[1], str(dtype), idx), shape, dtype))
        pool.append(t)
        return t

    def snapshot(self):
        return [(s, v) for s, v in self.cnt.items() if v > 0]

    def fence(self):
        deps = self.snapshot()
        for eng in ENGS:
            self.on(eng, lambda e: e.nop(), deps=deps, inc=0)

    def cc(self, kind, op, ins, outs, groups, deps, name):
        return self.on("pool", lambda g: g.collective_compute(
            kind, op, replica_groups=groups, ins=ins, outs=outs),
            deps=deps, sem=name, inc=1)

    def on(self, eng, fn, deps=(), sem=None, inc=1):
        sname = self.prefix + (sem or eng)
        waits = []
        for (s, v) in tmax(*deps) if deps else []:
            if s in self.dma_sems and v != self.cnt[s]:
                raise RuntimeError(
                    f"unsafe DMA wait on {s}: want {v}, issued {self.cnt[s]} "
                    f"(DMA completions are unordered; use a dedicated sem per slot)")
            if v > self.waited[eng][s]:
                waits.append((s, v))
                self.waited[eng][s] = v
        tok = None
        if inc:
            self.cnt[sname] += inc
            tok = (sname, self.cnt[sname])
        self.ops[eng].append((waits, fn, sname if inc else None, inc))
        self.n_inst += 1 + len(waits)
        return tok

    def dma(self, eng, out, in_, deps=(), sem=None, **kw):
        sname = sem or ("dma_" + eng)
        self.dma_sems.add(self.prefix + sname)
        return self.on(eng, lambda e: e.dma_start(out=out, in_=in_, **kw),
                       deps=deps, sem=sname, inc=16)

    def finalize(self):
        for s in list(self.cnt.keys()):
            self.sem(s)
        block = self.es.enter_context(self.nc.Block())
        for eng in ENGS:
            ops = self.ops[eng]
            if not ops:
                continue
            def make(eng_ops):
                def body(e):
                    for waits, fn, sname, inc in eng_ops:
                        for (s, v) in waits:
                            e.wait_ge(self.sems[s], v)
                        inst = fn(e)
                        if sname is not None:
                            inst.then_inc(self.sems[sname], inc)
                return body
            getattr(block, ENG_ATTR[eng])(make(ops))

    def close(self):
        self.es.close()









def host_tile_T(W, oc_size=128):
    O, Din = W.shape
    t = W.reshape(O // 128, 128, Din // 128, 128)
    t = t.transpose(0, 3, 2, 1)
    return np.ascontiguousarray(t.reshape(O // 128, 128, Din))


def build_ffn(p, io, T=2048, TT=512):
    nc = p.nc
    D = 2048
    INTER = 8192
    NB = TT // 128
    NTT = T // TT
    NC = D // 128
    NF = INTER // 128
    RT2 = math.sqrt(2.0)

    xr = p.sbuf("xr", [128, NB * D], BF16)
    xn = p.sbuf("xn", [128, D], BF16)
    vA = p.sbuf("vA", [128, NC * TT], BF16)
    vT = p.sbuf("vT", [128, (NC // 2) * TT], BF16)
    y = p.sbuf("y", [128, NF * TT], BF16)
    yT = p.sbuf("yT", [128, (NF // 2) * TT], BF16)
    dn = p.sbuf("dn", [128, NB * D], BF16)
    pr = dn
    wub = p.sbuf("wub", [128, 2 * D], BF16)
    wgb = p.sbuf("wgb", [128, 2 * D], BF16)
    wdb = p.sbuf("wdb", [128, 2 * (INTER // 2)], BF16)
    sg = p.sbuf("sg", [128, 2 * TT], BF16)
    dnf = p.sbuf("dnf", [128, 2 * TT], BF16)
    h128 = p.sbuf("h128", [128, 128], BF16)
    ident = p.sbuf("ident", [128, 128], BF16)
    suug = p.sbuf("suug", [128, NC], F32)
    sudp = p.sbuf("sudp", [128, NF], F32)
    wscu = p.sbuf("wscu", [128, NF], F32)
    wscg = p.sbuf("wscg", [128, NF], F32)
    upmul = p.sbuf("upmul", [128, NF], F32)
    wscd = p.sbuf("wscd", [128, NC], F32)
    bn6 = p.sbuf("bn6", [128, 24], F32)
    bn2 = p.sbuf("bn2", [128, 2], F32)
    rstd = p.sbuf("rstd", [128, 1], F32)
    epsc = p.sbuf("epsc", [128, 1], F32)

    psA = [p.psum("psA0", [128, 512]), p.psum("psA1", [128, 512])]
    psU = [p.psum("psU0", [128, 512]), p.psum("psU1", [128, 512])]
    psG = [p.psum("psG0", [128, 512]), p.psum("psG1", [128, 512])]
    psT = [p.psum("psT0", [128, 1024], BF16), p.psum("psT1", [128, 1024], BF16)]

    t_h = p.dma("pool", h128[:], io["h128"], sem="ldc")
    t_id = p.dma("pool", ident[:], io["ident"], sem="ldc")
    p.dma("pool", suug[:], bass.AP(io["su_ug"].tensor, 0, [[1, 128], [128, NC]]),
          allow_slow_non_contiguous=True, sem="ldc")
    p.dma("pool", sudp[:], bass.AP(io["su_dp"].tensor, 0, [[1, 128], [128, NF]]),
          allow_slow_non_contiguous=True, sem="ldc")
    p.dma("pool", wscu[:], bass.AP(io["wsc_ug"].tensor, 0, [[1, 128], [128, NF]]),
          allow_slow_non_contiguous=True, sem="ldc")
    p.dma("pool", wscg[:], bass.AP(io["wsc_ug"].tensor, INTER, [[1, 128], [128, NF]]),
          allow_slow_non_contiguous=True, sem="ldc")
    t_wd = p.dma("pool", wscd[:], bass.AP(io["wsc_dp"].tensor, 0, [[1, 128], [128, NC]]),
                 allow_slow_non_contiguous=True, sem="ldc")
    t_su = t_sd = t_wu = t_wg = t_h = t_id = t_wd
    t_eps = p.on("pool", lambda g: g.memset(epsc[:], 1.0e-5))
    t_su2 = p.on("dve", lambda v: v.tensor_scalar(suug[:], suug[:], 2.0 ** -11, None, ALU.mult), deps=[t_su])
    t_wg2 = p.on("dve", lambda v: v.tensor_scalar(wscg[:], wscg[:], 64.0 / RT2, None, ALU.mult), deps=[t_wg])
    t_um1 = p.on("dve", lambda v: v.tensor_tensor(upmul[:], wscu[:], sudp[:], ALU.mult), deps=[t_wu, t_sd])
    t_um2 = p.on("dve", lambda v: v.tensor_scalar(upmul[:], upmul[:], 2.0 ** -6 / RT2, None, ALU.mult), deps=[t_um1])
    t_wd2 = p.on("dve", lambda v: v.tensor_scalar(wscd[:], wscd[:], 64.0 / RT2, None, ALU.mult), deps=[t_wd])

    out_toks = []
    xr_free = None
    wu_free = [None, None]
    wg_free = [None, None]
    wd_free = [None, None]
    psA_free = [None, None]
    psU_free = [None, None]
    psG_free = [None, None]
    psT_free = [None, None]
    sg_free = [None, None]
    dnf_free = [None, None]
    vA_free = y_free = dn_free = None
    xn_free = None

    for j in range(NTT):
        row0 = j * TT

        def row_ap(h, r0=row0):
            return bass.AP(h.tensor, h.offset + r0 * D, [[D, 128], [128 * D, NB], [1, D]])
        xr_ap = bass.AP(xr, 0, [[NB * D, 128], [D, NB], [1, D]])
        pr_ap = bass.AP(pr, 0, [[NB * D, 128], [D, NB], [1, D]])

        t_x = p.dma("pool", xr_ap, row_ap(io["x"]), deps=[xr_free], sem="lda")
        t_p1 = p.dma("pool", pr_ap, row_ap(io["pa"]), deps=[dn_free], sem="lda")
        t_a1 = p.on("dve", lambda v: v.tensor_tensor(xr[:], xr[:], pr[:], ALU.add), deps=[t_x, t_p1])
        if io.get("pb") is not None:
            t_p2 = p.dma("pool", pr_ap, row_ap(io["pb"]), deps=[t_a1], sem="lda")
            t_a2 = p.on("dve", lambda v: v.tensor_tensor(xr[:], xr[:], pr[:], ALU.add), deps=[t_p2])
        else:
            t_a2 = t_a1

        tp_evac = []
        for b in range(NB):
            xrb = xr[:, b * D:(b + 1) * D]
            t_bs = None
            for s4 in range(4):
                t_bs = p.on("dve", lambda v, xrb=xrb, s4=s4: v.bn_stats(
                    bn6[:, s4 * 6:(s4 + 1) * 6], bass.AP(xrb.tensor, xrb.offset + s4 * 512, [[xrb.ap[0][0], 128], [1, 512]])),
                    deps=[t_a2, xn_free])
            t_ba = p.on("dve", lambda v: v.bn_aggr(bn2[:], bn6[:]), deps=[t_bs])
            t_sq = p.on("act", lambda s: s.activation(rstd[:], bn2[:, 1:2], AF.Sqrt, bias=epsc[:]), deps=[t_ba, t_eps])
            t_rs = p.on("dve", lambda v: v.reciprocal(rstd[:], rstd[:]), deps=[t_sq])
            t_xn = p.on("dve", lambda v, xrb=xrb: v.tensor_scalar(
                xn[:], xrb, bn2[:, 0:1], rstd[:], ALU.subtract, ALU.mult),
                deps=[t_rs, xn_free])
            tps = []
            for c in range(NC):
                par = (b * NC + c) % 2
                t_tp = p.on("pe", lambda t, c=c, par=par: t.transpose(
                    psT[par][:, 0:128], xn[:, c * 128:(c + 1) * 128], ident[:]),
                    deps=[t_xn, t_id, psT_free[par]])
                t_ev = p.on("dve", lambda v, c=c, b=b, par=par: v.tensor_scalar(
                    vA[:, c * TT + b * 128: c * TT + (b + 1) * 128],
                    psT[par][:, 0:128], suug[:, c:c + 1], None, ALU.mult),
                    deps=[t_tp, t_su2, vA_free])
                psT_free[par] = t_ev
                tps.append(t_tp)
                tp_evac.append(t_ev)
            xn_free = tmax(*tps)

        tA = tmax(*tp_evac)
        stA = []
        for c in range(NC):
            par = c % 2
            t_mm = p.on("pe", lambda t, c=c, par=par: t.matmul(
                psA[par][:], h128[:], vA[:, c * TT:(c + 1) * TT], start=True, stop=True),
                deps=[tA, t_h, psA_free[par]])
            t_ev = p.on("act", lambda s, c=c, par=par: s.copy(
                vA[:, c * TT:(c + 1) * TT], psA[par][:]), deps=[t_mm])
            psA_free[par] = t_ev
            stA.append(t_ev)

        tB = tmax(*stA)
        for h in (1, 2, 4, 8):
            nblk = NC // (2 * h)
            HT = TT // 2
            ev = bass.AP(vA, 0, [[NC * TT, 128], [2 * h * TT, nblk], [TT, h], [1, TT]])
            od = bass.AP(vA, h * TT, [[NC * TT, 128], [2 * h * TT, nblk], [TT, h], [1, TT]])
            tp_ = bass.AP(vT, 0, [[(NC // 2) * TT, 128], [h * TT, nblk], [TT, h], [1, TT]])
            ev0 = bass.AP(vA, 0, [[NC * TT, 128], [2 * h * TT, nblk], [TT, h], [1, HT]])
            ev1 = bass.AP(vA, HT, [[NC * TT, 128], [2 * h * TT, nblk], [TT, h], [1, HT]])
            tp0 = bass.AP(vT, 0, [[(NC // 2) * TT, 128], [h * TT, nblk], [TT, h], [1, HT]])
            tp1 = bass.AP(vT, HT, [[(NC // 2) * TT, 128], [h * TT, nblk], [TT, h], [1, HT]])
            t_c0 = p.on("act", lambda s, ev0=ev0, tp0=tp0: s.copy(tp0, ev0), deps=[tB])
            t_c1 = p.on("pool", lambda g, ev1=ev1, tp1=tp1: g.tensor_copy(tp1, ev1), deps=[tB])
            t_e = p.on("dve", lambda v, ev=ev, od=od, tp_=tp_: v.tensor_tensor(ev, tp_, od, ALU.add), deps=[t_c0, t_c1])
            t_o = p.on("dve", lambda v, od=od, tp_=tp_: v.tensor_tensor(od, tp_, od, ALU.subtract), deps=[t_e])
            tB = tmax(t_e, t_o)
        vtok = tB

        y_toks = []
        for oc in range(NF):
            par = oc % 2
            wu_ap = bass.AP(io["wupT"].tensor, oc * 128 * D, [[D, 128], [1, D]])
            wg_ap = bass.AP(io["wgT"].tensor, oc * 128 * D, [[D, 128], [1, D]])
            wus = wub[:, par * D:(par + 1) * D]
            wgs = wgb[:, par * D:(par + 1) * D]
            t_lu = p.dma("pool", wus, wu_ap, deps=[wu_free[par]], sem="lwu%d" % par)
            t_lg = p.dma("pool", wgs, wg_ap, deps=[wg_free[par]], sem="lwg%d" % par)
            t_u = t_g = None
            for c in range(NC):
                t_u = p.on("pe", lambda t, c=c, par=par, wus=wus: t.matmul(
                    psU[par][:], wus[:, c * 128:(c + 1) * 128], vA[:, c * TT:(c + 1) * TT],
                    start=(c == 0), stop=(c == NC - 1)),
                    deps=[tmax(vtok, t_lu, psU_free[par])] if c == 0 else (),
                    inc=1 if c == NC - 1 else 0)
            for c in range(NC):
                t_g = p.on("pe", lambda t, c=c, par=par, wgs=wgs: t.matmul(
                    psG[par][:], wgs[:, c * 128:(c + 1) * 128], vA[:, c * TT:(c + 1) * TT],
                    start=(c == 0), stop=(c == NC - 1)),
                    deps=[tmax(t_lg, psG_free[par])] if c == 0 else (),
                    inc=1 if c == NC - 1 else 0)
            wu_free[par] = t_u
            wg_free[par] = t_g
            sgs = sg[:, par * TT:(par + 1) * TT]
            t_sg = p.on("act", lambda s, oc=oc, par=par, sgs=sgs: s.activation(
                sgs, psG[par][:], AF.Silu, scale=wscg[:, oc:oc + 1]),
                deps=[t_g, t_wg2, sg_free[par]])
            psG_free[par] = t_sg
            t_y = p.on("dve", lambda v, oc=oc, par=par, sgs=sgs: v.scalar_tensor_tensor(
                y[:, oc * TT:(oc + 1) * TT], psU[par][:], upmul[:, oc:oc + 1], sgs,
                ALU.mult, ALU.mult), deps=[t_u, t_sg, t_um2, y_free])
            sg_free[par] = t_y
            psU_free[par] = t_y
            y_toks.append(t_y)
        vA_free = tmax(*y_toks)

        tA2 = tmax(*y_toks)
        stA2 = []
        for c in range(NF):
            par = c % 2
            t_mm = p.on("pe", lambda t, c=c, par=par: t.matmul(
                psA[par][:], h128[:], y[:, c * TT:(c + 1) * TT], start=True, stop=True),
                deps=[tA2, psA_free[par]])
            t_ev = p.on("act", lambda s, c=c, par=par: s.copy(
                y[:, c * TT:(c + 1) * TT], psA[par][:]), deps=[t_mm])
            psA_free[par] = t_ev
            stA2.append(t_ev)
        tB = tmax(*stA2)
        for h in (1, 2, 4, 8, 16, 32):
            nblk = NF // (2 * h)
            HT = TT // 2
            ev = bass.AP(y, 0, [[NF * TT, 128], [2 * h * TT, nblk], [TT, h], [1, TT]])
            od = bass.AP(y, h * TT, [[NF * TT, 128], [2 * h * TT, nblk], [TT, h], [1, TT]])
            tp_ = bass.AP(yT, 0, [[(NF // 2) * TT, 128], [h * TT, nblk], [TT, h], [1, TT]])
            ev0 = bass.AP(y, 0, [[NF * TT, 128], [2 * h * TT, nblk], [TT, h], [1, HT]])
            ev1 = bass.AP(y, HT, [[NF * TT, 128], [2 * h * TT, nblk], [TT, h], [1, HT]])
            tp0 = bass.AP(yT, 0, [[(NF // 2) * TT, 128], [h * TT, nblk], [TT, h], [1, HT]])
            tp1 = bass.AP(yT, HT, [[(NF // 2) * TT, 128], [h * TT, nblk], [TT, h], [1, HT]])
            t_c0 = p.on("act", lambda s, ev0=ev0, tp0=tp0: s.copy(tp0, ev0), deps=[tB])
            t_c1 = p.on("pool", lambda g, ev1=ev1, tp1=tp1: g.tensor_copy(tp1, ev1), deps=[tB])
            t_e = p.on("dve", lambda v, ev=ev, od=od, tp_=tp_: v.tensor_tensor(ev, tp_, od, ALU.add), deps=[t_c0, t_c1])
            t_o = p.on("dve", lambda v, od=od, tp_=tp_: v.tensor_tensor(od, tp_, od, ALU.subtract), deps=[t_e])
            tB = tmax(t_e, t_o)
        wtok = tB

        dn_toks = []
        down_mm_toks = []
        for oc in range(NC):
            par = oc % 2
            t_d = None
            for fh in range(2):
                parw = (oc * 2 + fh) % 2
                wds = wdb[:, parw * (INTER // 2):(parw + 1) * (INTER // 2)]
                wd_ap = bass.AP(io["wdT"].tensor, oc * 128 * INTER + fh * (INTER // 2),
                                [[INTER, 128], [1, INTER // 2]])
                t_ld = p.dma("pool", wds, wd_ap, deps=[wd_free[parw]], sem="lwd%d" % parw)
                first_dep = tmax(wtok, t_ld, psU_free[par]) if fh == 0 else tmax(t_ld)
                for fc in range(NF // 2):
                    f = fh * (NF // 2) + fc
                    t_d = p.on("pe", lambda t, fc=fc, f=f, par=par, wds=wds: t.matmul(
                        psU[par][:], wds[:, fc * 128:(fc + 1) * 128], y[:, f * TT:(f + 1) * TT],
                        start=(f == 0), stop=(f == NF - 1)),
                        deps=[first_dep] if fc == 0 else (),
                        inc=1 if fc == NF // 2 - 1 else 0)
                wd_free[parw] = t_d if t_d is not None else wd_free[parw]
            down_mm_toks.append(t_d)
            dns = dnf[:, par * TT:(par + 1) * TT]
            t_ev = p.on("act", lambda s, oc=oc, par=par, dns=dns: s.activation(
                dns, psU[par][:], AF.Copy, scale=wscd[:, oc:oc + 1]),
                deps=[t_d, t_wd2, dnf_free[par]])
            psU_free[par] = t_ev
            for b in range(NB):
                tpar = (oc * NB + b) % 2
                t_tp = p.on("pe", lambda t, b=b, par=par, tpar=tpar, dns=dns: t.transpose(
                    psT[tpar][:, 0:128], dns[:, b * 128:(b + 1) * 128], ident[:]),
                    deps=[t_ev, psT_free[tpar]])
                t_cp = p.on("dve", lambda v, b=b, oc=oc, tpar=tpar: v.tensor_copy(
                    dn[:, b * D + oc * 128: b * D + (oc + 1) * 128], psT[tpar][:, 0:128]),
                    deps=[t_tp, t_a2])
                psT_free[tpar] = t_cp
                dn_toks.append(t_cp)
            dnf_free[par] = dn_toks[-1]
        y_free = tmax(*down_mm_toks)

        t_fin = p.on("dve", lambda v: v.tensor_tensor(xr[:], xr[:], dn[:], ALU.add),
                     deps=[tmax(*dn_toks), t_a2])
        dn_free = t_fin
        t_st = p.dma("pool", row_ap(io["out"]), xr_ap, deps=[t_fin], sem="sto")
        xr_free = t_st
        out_toks.append(t_st)

    p.on("sp", lambda e: e.nop(), deps=out_toks, inc=0)
    return out_toks






D = 2048
NQH = 16
NG = 4
HD = 64
QO = 8
KO = 2
VO = 256
VOP = 260
NC = 16


def head_oc_base(h):
    g, j = h // 4, h % 4
    return (g // 2) * 4 + j, (g % 2) * 64


def build_attn0(p, io, S=4096):
    NBLK = S // 512
    xr = p.sbuf("xr", [128, 4 * D], BF16)
    xn = p.sbuf("xn", [128, D], BF16)
    xf = p.sbuf("xf", [128, NC * 512], BF16)
    wq = p.sbuf("wq", [128, QO * D], BF16)
    wk = p.sbuf("wk", [128, KO * D], BF16)
    wv = p.sbuf("wv", [128, NC * VO], BF16)
    wo = p.sbuf("wo", [128, 16 * 1024], BF16)
    qf = p.sbuf("qf", [128, QO * 512], BF16)
    kf = p.sbuf("kf", [128, KO * S], BF16)
    vt = p.sbuf("vt", [128, (S // 128) * VOP], BF16)
    qsq = p.sbuf("qsq", [128, QO * 512], BF16)
    ksq = p.sbuf("ksq", [128, KO * 512], BF16)
    qm = p.sbuf("qm", [1, NQH], F32)
    km = p.sbuf("km", [1, NG], F32)
    kmx = p.sbuf("kmx", [1, NG], F32)
    kmrep = p.sbuf("kmrep", [1, NQH], F32)
    prod = p.sbuf("prod", [1, NQH], F32)
    brow = p.sbuf("brow", [1, NQH], F32)
    bcol = p.sbuf("bcol", [128, NQH], F32)
    ones_r = p.sbuf("ones_r", [1, 128], BF16)
    onecol = p.sbuf("onecol", [128, 1], BF16)
    pbuf = p.sbuf("pbuf", [128, 4 * 512], BF16)
    lrow = p.sbuf("lrow", [1, 2 * 512], F32)
    lsc = p.sbuf("lsc", [128, 512], F32)
    ctxs = p.sbuf("ctxs", [128, 8 * 512], BF16)
    wof = p.sbuf("wof", [128, 2 * 512], BF16)
    part = p.sbuf("part", [128, 4 * D], BF16)
    ident = p.sbuf("ident", [128, 128], BF16)
    ones32r = p.sbuf("ones32r", [1, 128], F32)
    dmask = p.sbuf("dmask", [128, 128], BF16)
    bn6 = p.sbuf("bn6", [128, 24], F32)
    bn2 = p.sbuf("bn2", [128, 2], F32)
    rstd = p.sbuf("rstd", [128, 1], F32)
    epsc = p.sbuf("epsc", [128, 1], F32)

    psP = [p.psum("psP0", [128, 512]), p.psum("psP1", [128, 512])]
    psS = [p.psum("psS0", [128, 512]), p.psum("psS1", [128, 512])]
    psC = [p.psum("psC0", [128, 512]), p.psum("psC1", [128, 512])]

    psT = [p.psum("psT0", [128, 1024], BF16), p.psum("psT1", [128, 1024], BF16)]
    psN = psP[0]

    t_id = p.dma("pool", ident[:], io["ident"], sem="ldc")
    t_dm = p.dma("pool", dmask[:], io["dmaskt"], sem="ldc")
    p.dma("pool", wv[:], bass.AP(io["wvT"].tensor, 0, [[VO, 128], [128 * VO, NC], [1, VO]]), sem="ldc")
    p.dma("pool", wq[:], bass.AP(io["wq_t"].tensor, 0, [[D, 128], [128 * D, QO], [1, D]]), sem="ldc")
    p.dma("pool", wk[:], bass.AP(io["wk_t"].tensor, 0, [[D, 128], [128 * D, KO], [1, D]]), sem="ldc")
    t_wol = p.dma("pool", wo[:], bass.AP(io["wo_t"].tensor, 0, [[1024, 128], [128 * 1024, 16], [1, 1024]]), sem="ldc")
    tc0 = p.on("pool", lambda g: g.memset(ones_r[:], 1.0))
    p.on("pool", lambda g: g.memset(ones32r[:], 1.0))
    tc1 = p.on("pool", lambda g: g.memset(onecol[:], 1.0))
    tc2 = p.on("pool", lambda g: g.memset(km[:], 0.0))
    t_vt1 = p.on("pool", lambda g: g.memset(vt[:], 1.0))
    t_eps = p.on("pool", lambda g: g.memset(epsc[:], 1.0e-5))
    t_const = tmax(t_id, t_dm, t_wol, tc0, tc1, tc2, t_vt1, t_eps)

    out_toks = []
    xr_free = xn_free = xf_free = qf_free = None
    pbuf_free = [None, None, None, None]
    psP_free = [None, None]
    psS_free = [None, None]
    psC_free = [None, None]
    psT_free = [None, None]
    qsq_free = ksq_free = None
    qm_free = kmx_free = bcol_free = None
    lrow_free = lsc_free = None
    ctxs_free = wof_free = part_free = None

    for qb in range(NBLK):
        row0 = qb * 512

        def row_ap(h, r0=row0):
            return bass.AP(h.tensor, h.offset + r0 * D, [[D, 128], [128 * D, 4], [1, D]])
        xr_ap = bass.AP(xr, 0, [[4 * D, 128], [D, 4], [1, D]])

        t_x = p.dma("pool", xr_ap, row_ap(io["x"]), deps=[xr_free], sem="lda")

        tp_evac = []
        for b in range(4):
            xrb = xr[:, b * D:(b + 1) * D]
            t_bs = None
            for s4 in range(4):
                t_bs = p.on("dve", lambda v, xrb=xrb, s4=s4: v.bn_stats(
                    bn6[:, s4 * 6:(s4 + 1) * 6],
                    bass.AP(xrb.tensor, xrb.offset + s4 * 512, [[xrb.ap[0][0], 128], [1, 512]])),
                    deps=[t_x, xn_free])
            t_ba = p.on("dve", lambda v: v.bn_aggr(bn2[:], bn6[:]), deps=[t_bs])
            t_sq = p.on("act", lambda s: s.activation(rstd[:], bn2[:, 1:2], AF.Sqrt, bias=epsc[:]),
                        deps=[t_ba, t_const])
            t_rs = p.on("dve", lambda v: v.reciprocal(rstd[:], rstd[:]), deps=[t_sq])
            t_xn = p.on("dve", lambda v, xrb=xrb: v.tensor_scalar(
                xn[:], xrb, bn2[:, 0:1], rstd[:], ALU.subtract, ALU.mult),
                deps=[t_rs, xn_free])
            tps = []
            for c in range(NC):
                par = (b * NC + c) % 2
                t_tp = p.on("pe", lambda t, c=c, par=par: t.transpose(
                    psT[par][:, 0:128], xn[:, c * 128:(c + 1) * 128], ident[:]),
                    deps=[t_xn, t_const, psT_free[par]])
                t_ev = p.on("dve", lambda v, c=c, b=b, par=par: v.tensor_copy(
                    xf[:, c * 512 + b * 128: c * 512 + (b + 1) * 128], psT[par][:, 0:128]),
                    deps=[t_tp, xf_free])
                psT_free[par] = t_ev
                tps.append(t_tp)
                tp_evac.append(t_ev)
            xn_free = tmax(*tps)
        t_fm = tmax(*tp_evac)

        q_evs = []
        for oc in range(QO):
            par = oc % 2
            t_mm = None
            for c in range(NC):
                t_mm = p.on("pe", lambda t, oc=oc, c=c, par=par: t.matmul(
                    psP[par][:], wq[:, oc * D + c * 128: oc * D + (c + 1) * 128],
                    xf[:, c * 512:(c + 1) * 512], start=(c == 0), stop=(c == NC - 1)),
                    deps=[tmax(t_fm, t_const, psP_free[par])] if c == 0 else (),
                    inc=1 if c == NC - 1 else 0)
            t_ev = p.on("act", lambda s, oc=oc, par=par: s.copy(
                qf[:, oc * 512:(oc + 1) * 512], psP[par][:]), deps=[t_mm, qf_free])
            psP_free[par] = t_ev
            q_evs.append(t_ev)
        k_evs = []
        for oc in range(KO):
            par = oc % 2
            t_mm = None
            for c in range(NC):
                t_mm = p.on("pe", lambda t, oc=oc, c=c, par=par: t.matmul(
                    psP[par][:], wk[:, oc * D + c * 128: oc * D + (c + 1) * 128],
                    xf[:, c * 512:(c + 1) * 512], start=(c == 0), stop=(c == NC - 1)),
                    deps=[tmax(t_fm, psP_free[par])] if c == 0 else (),
                    inc=1 if c == NC - 1 else 0)
            t_ev = p.on("act", lambda s, oc=oc, par=par, qb=qb: s.copy(
                kf[:, oc * S + qb * 512: oc * S + (qb + 1) * 512], psP[par][:]), deps=[t_mm])
            psP_free[par] = t_ev
            k_evs.append(t_ev)
        v_evs = []
        for b in range(4):
            par = b % 2
            t_mm = None
            for c in range(NC):
                t_mm = p.on("pe", lambda t, b=b, c=c, par=par: t.matmul(
                    psP[par][:, 0:VO], xf[:, c * 512 + b * 128: c * 512 + (b + 1) * 128],
                    wv[:, c * VO:(c + 1) * VO], start=(c == 0), stop=(c == NC - 1)),
                    deps=[tmax(t_fm, psP_free[par])] if c == 0 else (),
                    inc=1 if c == NC - 1 else 0)
            sub = qb * 4 + b
            t_ev = p.on("act", lambda s, sub=sub, par=par: s.copy(
                bass.AP(vt, sub * VOP, [[(S // 128) * VOP, 128], [65, 4], [1, 64]]),
                bass.AP(psP[par], 0, [[512, 128], [64, 4], [1, 64]])),
                deps=[t_mm, t_const])
            psP_free[par] = t_ev
            v_evs.append(t_ev)
        t_kv = tmax(*k_evs, *v_evs)
        t_q = tmax(*q_evs)
        xr_free = t_fm

        t_qs2 = p.on("act", lambda s: s.activation(qsq[:], qf[:], AF.Square), deps=[t_q, qsq_free])
        t_ks2 = p.on("act", lambda s, qb=qb: s.activation(
            ksq[:], bass.AP(kf, qb * 512, [[KO * S, 128], [S, KO], [1, 512]]), AF.Square),
            deps=[tmax(*k_evs), ksq_free])
        nm_toks = []
        for h in range(NQH):
            oc, base = head_oc_base(h)
            t_nm = p.on("pe", lambda t, oc=oc, base=base: t.matmul(
                psN[0:1, 0:512], bass.AP(onecol, base, [[1, 64], [0, 1]]),
                bass.AP(qsq, base * QO * 512 + oc * 512, [[QO * 512, 64], [1, 512]]),
                start=True, stop=True),
                deps=[t_qs2, t_const, psP_free[0]])
            t_mx = p.on("dve", lambda v, h=h: v.tensor_reduce(
                qm[:, h:h + 1], psN[0:1, 0:512], AX.X, ALU.max), deps=[t_nm, qm_free])
            psP_free[0] = t_mx
            nm_toks.append(t_mx)
        for g in range(NG):
            oc, base = g // 2, (g % 2) * 64
            t_nm = p.on("pe", lambda t, oc=oc, base=base: t.matmul(
                psN[0:1, 0:512], bass.AP(onecol, base, [[1, 64], [0, 1]]),
                bass.AP(ksq, base * KO * 512 + oc * 512, [[KO * 512, 64], [1, 512]]),
                start=True, stop=True),
                deps=[t_ks2, psP_free[0]])
            t_mx = p.on("dve", lambda v, g=g: v.tensor_reduce(
                kmx[:, g:g + 1], psN[0:1, 0:512], AX.X, ALU.max), deps=[t_nm, kmx_free])
            t_km = p.on("dve", lambda v, g=g: v.tensor_tensor(
                km[:, g:g + 1], km[:, g:g + 1], kmx[:, g:g + 1], ALU.max), deps=[t_mx])
            psP_free[0] = t_mx
            nm_toks.append(t_km)
        t_nm_all = tmax(*nm_toks)
        qsq_free = t_nm_all
        ksq_free = t_nm_all
        kmx_free = t_nm_all
        rep_toks = []
        for g in range(NG):
            rep_toks.append(p.on("dve", lambda v, g=g: v.tensor_scalar(
                kmrep[:, 4 * g:4 * (g + 1)], ones_r[0:1, 0:4], km[:, g:g + 1], None, ALU.mult),
                deps=[t_nm_all]))
        t_pr = p.on("dve", lambda v: v.tensor_tensor(prod[:], qm[:], kmrep[:], ALU.mult),
                    deps=rep_toks)
        qm_free = t_pr
        t_br = p.on("act", lambda s: s.activation(brow[:], prod[:], AF.Sqrt), deps=[t_pr])
        t_bn = p.on("dve", lambda v: v.tensor_scalar(brow[:], brow[:], -0.125, None, ALU.mult), deps=[t_br])
        t_bc0 = p.on("pe", lambda t: t.matmul(psN[:, 0:NQH], ones32r[0:1, :], brow[:],
                                              start=True, stop=True), deps=[t_bn, psP_free[0]])
        t_bc = p.on("dve", lambda v: v.tensor_copy(bcol[:], psN[:, 0:NQH]), deps=[t_bc0, bcol_free])
        psP_free[0] = t_bc

        pair_evac = []
        for pp in range(8):
            if True:
                t_ctx_last = {0: None, 1: None}
                for kc in range(qb + 1):
                    diag = (kc == qb)
                    for hloc in (0, 1):
                        h = (2 * (pp // 4)) * 4 + (pp % 4) + hloc * 4
                        g = h // 4
                        gbase = (g % 2) * 64
                        oc, base = head_oc_base(h)
                        exp_toks = [None] * 4
                        for ks in range(4):
                            q0 = ks * 128 if diag else 0
                            width = 512 - q0
                            spar = ks % 2
                            sps = psS[spar]
                            t_s = p.on("pe", lambda t, oc=oc, base=base, ks=ks, kc=kc, q0=q0, width=width, sps=sps, gbase=gbase, g=g: t.matmul(
                                sps[:, 0:width],
                                bass.AP(kf, gbase * KO * S + (g // 2) * S + kc * 512 + ks * 128, [[KO * S, 64], [1, 128]]),
                                bass.AP(qf, base * QO * 512 + oc * 512 + q0, [[QO * 512, 64], [1, width]]),
                                start=True, stop=True),
                                deps=[tmax(t_q, t_kv, psS_free[spar])])
                            if diag:
                                t_s = p.on("dve", lambda v, sps=sps: v.tensor_tensor(
                                    sps[:, 0:128], sps[:, 0:128], dmask[:], ALU.add),
                                    deps=[t_s, t_const])
                            t_e = p.on("act", lambda s, ks=ks, q0=q0, width=width, sps=sps, h=h: s.activation(
                                pbuf[:, ks * 512 + q0: (ks + 1) * 512], sps[:, 0:width], AF.Exp,
                                bias=bcol[:, h:h + 1], scale=0.125),
                                deps=[t_s, t_bc, pbuf_free[ks]])
                            psS_free[spar] = t_e
                            if diag and ks > 0:
                                t_m = p.on("dve", lambda v, ks=ks, q0=q0: v.memset(
                                    pbuf[:, ks * 512: ks * 512 + q0], 0.0),
                                    deps=[pbuf_free[ks]])
                                t_e = tmax(t_e, t_m)
                            exp_toks[ks] = t_e
                        for ks in range(4):
                            sub = kc * 4 + ks
                            first = (kc == 0 and ks == 0)
                            last = (kc == qb and ks == 3)
                            t_cm = p.on("pe", lambda t, sub=sub, g=g, hloc=hloc, ks=ks, first=first, last=last: t.matmul(
                                psC[hloc][0:65, :],
                                vt[:, sub * VOP + g * 65: sub * VOP + g * 65 + 65],
                                pbuf[:, ks * 512:(ks + 1) * 512], start=first, stop=last),
                                deps=[tmax(exp_toks[ks], t_kv, psC_free[hloc] if first else None)])
                            pbuf_free[ks] = t_cm
                            t_ctx_last[hloc] = t_cm
                cc = pp
                bc_toks = []
                rc_toks = []
                for hloc in (0, 1):
                    t_rc = p.on("dve", lambda v, hloc=hloc: v.reciprocal(
                        lrow[:, hloc * 512:(hloc + 1) * 512], psC[hloc][64:65, 0:512]),
                        deps=[t_ctx_last[hloc], lrow_free])
                    rc_toks.append(t_rc)
                for hloc in (0, 1):
                    t_b1 = p.on("pe", lambda t, hloc=hloc: t.matmul(
                        psN[hloc * 64:(hloc + 1) * 64, :], ones32r[0:1, 0:64],
                        lrow[:, hloc * 512:(hloc + 1) * 512],
                        start=True, stop=True, tile_position=(0, hloc * 64)),
                        deps=[rc_toks[hloc], psP_free[0], t_const])
                    psP_free[0] = t_b1
                    bc_toks.append(t_b1)
                lrow_free = tmax(*bc_toks)
                t_lc = p.on("dve", lambda v: v.tensor_copy(lsc[:], psN[:]),
                            deps=[tmax(*bc_toks), lsc_free])
                psP_free[0] = t_lc
                t_ev0 = p.on("dve", lambda v, cc=cc: v.tensor_tensor(
                    bass.AP(ctxs, cc * 512, [[8 * 512, 64], [1, 512]]),
                    psC[0][0:64, :], lsc[0:64, :], ALU.mult),
                    deps=[t_lc, t_ctx_last[0], ctxs_free])
                t_ev1 = p.on("dve", lambda v, cc=cc: v.tensor_tensor(
                    bass.AP(ctxs, 64 * 8 * 512 + cc * 512, [[8 * 512, 64], [1, 512]]),
                    psC[1][0:64, :], lsc[64:128, :], ALU.mult),
                    deps=[t_lc, t_ctx_last[1], ctxs_free])
                t_ev = tmax(t_ev0, t_ev1)
                lsc_free = t_ev
                psC_free = [t_ev0, t_ev1]
                pair_evac.append(t_ev)
        t_ctx_all = tmax(*pair_evac)
        qf_free = t_ctx_all
        bcol_free = t_ctx_all
        xf_free = tmax(t_q, t_kv)

        st_deps = []
        for oc in range(16):
            par = oc % 2
            t_mm = None
            for cc in range(8):
                t_mm = p.on("pe", lambda t, oc=oc, cc=cc, par=par: t.matmul(
                    psP[par][:], wo[:, oc * 1024 + cc * 128: oc * 1024 + (cc + 1) * 128],
                    ctxs[:, cc * 512:(cc + 1) * 512], start=(cc == 0), stop=(cc == 7)),
                    deps=[tmax(t_ctx_all, psP_free[par])] if cc == 0 else (),
                    inc=1 if cc == 7 else 0)
            wos = wof[:, par * 512:(par + 1) * 512]
            t_ev = p.on("act", lambda s, par=par, wos=wos: s.copy(wos, psP[par][:]),
                        deps=[t_mm, wof_free])
            psP_free[par] = t_ev
            for b in range(4):
                tpar = (oc * 4 + b) % 2
                t_tp = p.on("pe", lambda t, b=b, tpar=tpar, wos=wos: t.transpose(
                    psT[tpar][:, 0:128], wos[:, b * 128:(b + 1) * 128], ident[:]),
                    deps=[t_ev, psT_free[tpar]])
                t_cp = p.on("dve", lambda v, b=b, oc=oc, tpar=tpar: v.tensor_copy(
                    part[:, b * D + oc * 128: b * D + (oc + 1) * 128], psT[tpar][:, 0:128]),
                    deps=[t_tp, part_free])
                psT_free[tpar] = t_cp
                st_deps.append(t_cp)
            wof_free = st_deps[-1]
        ctxs_free = tmax(*[t for t in st_deps])
        t_st = p.dma("pool", row_ap(io["out"]),
                     bass.AP(part, 0, [[4 * D, 128], [D, 4], [1, D]]),
                     deps=[tmax(*st_deps)], sem="sto")
        part_free = t_st
        out_toks.append(t_st)

    p.on("sp", lambda e: e.nop(), deps=out_toks, inc=0)
    return out_toks






D = 2048
NH = 8
HD = 128
NC = 16
MO = NH * HD
MOP = NH * 129
SCALE = float(HD) ** -0.5


def build_attn1(p, io, S=4096):
    NBLK = S // 512

    xr = p.sbuf("xr", [128, 4 * D], BF16)
    xn = p.sbuf("xn", [128, D], BF16)
    xf = p.sbuf("xf", [128, NC * 512], BF16)
    wts = p.sbuf("wts", [128, 2 * NC * 512], BF16)
    qtm = p.sbuf("qtm", [128, 4 * MO], BF16)
    ktm = p.sbuf("ktm", [128, 4 * MO], BF16)
    rtm = p.sbuf("rtm", [128, 4 * 512], BF16)
    qf = p.sbuf("qf", [128, NH * 512], BF16)
    kf = p.sbuf("kf", [128, NH * 1024], BF16)
    vt = p.sbuf("vt", [128, 8 * MOP], BF16)
    cosr = p.sbuf("cosr", [128, 4 * 512], BF16)
    sinr = p.sbuf("sinr", [128, 4 * 512], BF16)
    pbuf = p.sbuf("pbuf", [128, 3 * 512], BF16)
    lrow = p.sbuf("lrow", [1, 512], F32)
    lsc = p.sbuf("lsc", [128, 512], F32)
    ctxs = p.sbuf("ctxs", [128, NH * 512], BF16)
    wof = p.sbuf("wof", [128, 2 * 512], BF16)
    part = p.sbuf("part", [128, 4 * D], BF16)
    ident = p.sbuf("ident", [128, 128], BF16)
    id32 = p.sbuf("id32", [128, 128], F32)
    ones32r = p.sbuf("ones32r", [1, 128], F32)
    dmask = p.sbuf("dmask", [128, 128], BF16)
    amask = p.sbuf("amask", [128, 128], BF16)
    bn6 = p.sbuf("bn6", [128, 24], F32)
    bn2 = p.sbuf("bn2", [128, 2], F32)
    rstd = p.sbuf("rstd", [128, 1], F32)
    epsc = p.sbuf("epsc", [128, 1], F32)

    psP = [p.psum("psP0", [128, 512]), p.psum("psP1", [128, 512])]
    psS = [p.psum("psS0", [128, 512]), p.psum("psS1", [128, 512])]
    psC = [p.psum("psC0", [128, 512]), p.psum("psC1", [128, 512])]
    psN = psP[0]
    psT = [p.psum("psT0", [128, 1024], BF16), p.psum("psT1", [128, 1024], BF16)]

    p.dma("pool", ident[:], io["ident"], sem="ldc")
    p.dma("pool", id32[:], io["ident"], sem="ldc")
    p.dma("pool", dmask[:], io["dmaskt"], sem="ldc")
    t_am = p.dma("pool", amask[:], io["amaskt"], sem="ldc")
    t_on = p.on("pool", lambda g: g.memset(ones32r[:], 1.0))
    t_vt1 = p.on("pool", lambda g: g.memset(vt[:], 1.0))
    t_eps = p.on("pool", lambda g: g.memset(epsc[:], 1.0e-5))
    t_const = tmax(t_am, t_on, t_vt1, t_eps)

    out_toks = []
    xr_free = xn_free = xf_free = None
    wts_free = [None, None]
    psP_free = [None, None]
    psS_free = [None, None]
    psC_free = [None, None]
    psT_free = [None, None]
    qtm_free = ktm_free = rtm_free = None
    qf_free = None
    cos_free = None
    pbuf_free = [None, None, None]
    lrow_free = lsc_free = None
    ctxs_free = wof_free = part_free = None

    for qb in range(NBLK):
        row0 = qb * 512
        slot = qb % 2
        pslot = (qb - 1) % 2

        def row_ap(h, r0=row0):
            return bass.AP(h.tensor, h.offset + r0 * D, [[D, 128], [128 * D, 4], [1, D]])
        xr_ap = bass.AP(xr, 0, [[4 * D, 128], [D, 4], [1, D]])

        t_x = p.dma("pool", xr_ap, row_ap(io["x"]), deps=[xr_free], sem="lda")
        t_cos = p.dma("pool", bass.AP(cosr, 0, [[4 * 512, 128], [512, 4], [1, 512]]),
                      bass.AP(io["cos_rep"].tensor, row0 * 512, [[512, 128], [128 * 512, 4], [1, 512]]),
                      deps=[cos_free], sem="ldr")
        t_sin = p.dma("pool", bass.AP(sinr, 0, [[4 * 512, 128], [512, 4], [1, 512]]),
                      bass.AP(io["sin_rep"].tensor, row0 * 512, [[512, 128], [128 * 512, 4], [1, 512]]),
                      deps=[cos_free], sem="ldr2")

        tp_evac = []
        for b in range(4):
            xrb = xr[:, b * D:(b + 1) * D]
            t_bs = None
            for s4 in range(4):
                t_bs = p.on("dve", lambda v, xrb=xrb, s4=s4: v.bn_stats(
                    bn6[:, s4 * 6:(s4 + 1) * 6],
                    bass.AP(xrb.tensor, xrb.offset + s4 * 512, [[xrb.ap[0][0], 128], [1, 512]])),
                    deps=[t_x, xn_free])
            t_ba = p.on("dve", lambda v: v.bn_aggr(bn2[:], bn6[:]), deps=[t_bs])
            t_sq = p.on("act", lambda s: s.activation(rstd[:], bn2[:, 1:2], AF.Sqrt, bias=epsc[:]),
                        deps=[t_ba, t_const])
            t_rs = p.on("dve", lambda v: v.reciprocal(rstd[:], rstd[:]), deps=[t_sq])
            t_xn = p.on("dve", lambda v, xrb=xrb: v.tensor_scalar(
                xn[:], xrb, bn2[:, 0:1], rstd[:], ALU.subtract, ALU.mult),
                deps=[t_rs, xn_free])
            tps = []
            for c in range(NC):
                par = (b * NC + c) % 2
                t_tp = p.on("pe", lambda t, c=c, par=par: t.transpose(
                    psT[par][:, 0:128], xn[:, c * 128:(c + 1) * 128], ident[:]),
                    deps=[t_xn, t_const, psT_free[par]])
                t_ev = p.on("dve", lambda v, c=c, b=b, par=par: v.tensor_copy(
                    xf[:, c * 512 + b * 128: c * 512 + (b + 1) * 128], psT[par][:, 0:128]),
                    deps=[t_tp, xf_free])
                psT_free[par] = t_ev
                tps.append(t_tp)
                tp_evac.append(t_ev)
            xn_free = tmax(*tps)
        t_fm = tmax(*tp_evac)
        xr_free = t_fm

        def proj_tm(dst, wio, dst_free, tag):
            evs = []
            for osl in range(2):
                par = osl % 2
                ws = wts[:, par * NC * 512:(par + 1) * NC * 512]
                t_lw = p.dma("pool", ws,
                             bass.AP(wio.tensor, osl * 512, [[MO, 128], [128 * MO, NC], [1, 512]]),
                             deps=[wts_free[par]], sem="lw%d%s" % (par, tag))
                for sub in range(4):
                    spar = sub % 2
                    t_mm = None
                    for c in range(NC):
                        t_mm = p.on("pe", lambda t, c=c, sub=sub, spar=spar, ws=ws: t.matmul(
                            psP[spar][:], xf[:, c * 512 + sub * 128: c * 512 + (sub + 1) * 128],
                            ws[:, c * 512:(c + 1) * 512], start=(c == 0), stop=(c == NC - 1)),
                            deps=[tmax(t_fm, t_lw, psP_free[spar])] if c == 0 else (),
                            inc=1 if c == NC - 1 else 0)
                    t_ev = p.on("dve", lambda v, sub=sub, osl=osl, spar=spar, dst=dst: v.tensor_copy(
                        dst[:, sub * MO + osl * 512: sub * MO + (osl + 1) * 512], psP[spar][:]),
                        deps=[t_mm, dst_free])
                    psP_free[spar] = t_ev
                    evs.append(t_ev)
                wts_free[par] = evs[-1]
            return tmax(*evs)

        t_q = proj_tm(qtm, io["wqT"], qtm_free, "q")
        t_k = proj_tm(ktm, io["wkT"], ktm_free, "k")
        v_evs = []
        for osl in range(2):
            par = osl % 2
            ws = wts[:, par * NC * 512:(par + 1) * NC * 512]
            t_lw = p.dma("pool", ws,
                         bass.AP(io["wvT"].tensor, osl * 512, [[MO, 128], [128 * MO, NC], [1, 512]]),
                         deps=[wts_free[par]], sem="lw%dv" % par)
            for sub in range(4):
                spar = sub % 2
                t_mm = None
                for c in range(NC):
                    t_mm = p.on("pe", lambda t, c=c, sub=sub, spar=spar, ws=ws: t.matmul(
                        psP[spar][:], xf[:, c * 512 + sub * 128: c * 512 + (sub + 1) * 128],
                        ws[:, c * 512:(c + 1) * 512], start=(c == 0), stop=(c == NC - 1)),
                        deps=[tmax(t_fm, t_lw, psP_free[spar])] if c == 0 else (),
                        inc=1 if c == NC - 1 else 0)
                rsub = (qb * 4 + sub) % 8
                t_ev = p.on("dve", lambda v, rsub=rsub, osl=osl, spar=spar: v.tensor_copy(
                    bass.AP(vt, rsub * MOP + osl * 4 * 129, [[8 * MOP, 128], [129, 4], [65, 2], [1, 64]]),
                    bass.AP(psP[spar], 0, [[512, 128], [128, 4], [64, 2], [1, 64]])),
                    deps=[t_mm, t_const])
                psP_free[spar] = t_ev
                v_evs.append(t_ev)
            wts_free[par] = v_evs[-1]
        t_v = tmax(*v_evs)

        def rope(dst, t_src, rfree):
            prev = rfree
            toks = []
            for sub in range(4):
                base = sub * MO
                x1 = bass.AP(dst, base, [[4 * MO, 128], [HD, NH], [2, 64]])
                x2 = bass.AP(dst, base + 1, [[4 * MO, 128], [HD, NH], [2, 64]])
                cs = bass.AP(cosr, sub * 512, [[4 * 512, 128], [64, NH], [1, 64]])
                sn = bass.AP(sinr, sub * 512, [[4 * 512, 128], [64, NH], [1, 64]])
                def rt(k):
                    return bass.AP(rtm, k * 512, [[4 * 512, 128], [64, NH], [1, 64]])
                t1 = p.on("dve", lambda v, x1=x1, cs=cs: v.tensor_tensor(rt(0), x1, cs, ALU.mult),
                          deps=[t_src, t_cos, prev])
                t2 = p.on("pool", lambda g, x1=x1, sn=sn: g.tensor_tensor(rt(1), x1, sn, ALU.mult),
                          deps=[t_src, t_sin, prev])
                t3 = p.on("dve", lambda v, x2=x2, sn=sn: v.tensor_tensor(rt(2), x2, sn, ALU.mult),
                          deps=[t_src, t_sin, prev])
                t4 = p.on("pool", lambda g, x2=x2, cs=cs: g.tensor_tensor(rt(3), x2, cs, ALU.mult),
                          deps=[t_src, t_cos, prev])
                t5 = p.on("dve", lambda v, x1=x1: v.tensor_tensor(x1, rt(0), rt(2), ALU.subtract),
                          deps=[t1, t3, t4])
                t6 = p.on("pool", lambda g, x2=x2: g.tensor_tensor(x2, rt(1), rt(3), ALU.add),
                          deps=[t2, t4, t5])
                prev = tmax(t5, t6)
                toks.append(prev)
            return tmax(*toks)

        t_qr = rope(qtm, t_q, rtm_free)
        t_kr = rope(ktm, t_k, t_qr)
        rtm_free = t_kr

        rtm_free = tmax(t_qr, t_kr)
        t_nb = tmax(t_qr, t_kr)

        qf_toks = []
        for sub in range(4):
            for h in range(NH):
                par = (sub * NH + h) % 2
                t_tp = p.on("pe", lambda t, sub=sub, h=h, par=par: t.transpose(
                    psT[par][:, 0:128], qtm[:, sub * MO + h * HD: sub * MO + (h + 1) * HD],
                    ident[:]), deps=[t_qr, t_nb, psT_free[par]])
                t_ev = p.on("dve", lambda v, sub=sub, h=h, par=par: v.tensor_copy(
                    qf[:, h * 512 + sub * 128: h * 512 + (sub + 1) * 128], psT[par][:, 0:128]),
                    deps=[t_tp, qf_free])
                psT_free[par] = t_ev
                qf_toks.append(t_ev)
        t_qf = tmax(*qf_toks)
        qtm_free = t_qf
        kf_toks = []
        for sub in range(4):
            for h in range(NH):
                par = (sub * NH + h) % 2
                t_tp = p.on("pe", lambda t, sub=sub, h=h, par=par: t.transpose(
                    psT[par][:, 0:128], ktm[:, sub * MO + h * HD: sub * MO + (h + 1) * HD],
                    ident[:]), deps=[t_kr, t_nb, psT_free[par]])
                t_ev = p.on("dve", lambda v, sub=sub, h=h, par=par, slot=slot: v.tensor_copy(
                    kf[:, h * 1024 + slot * 512 + sub * 128: h * 1024 + slot * 512 + (sub + 1) * 128],
                    psT[par][:, 0:128]), deps=[t_tp])
                psT_free[par] = t_ev
                kf_toks.append(t_ev)
        t_kf2 = tmax(*kf_toks)
        ktm_free = t_kf2

        ev_toks_all = []
        for h in range(NH):
            chunks = []
            if qb > 0:
                for kt in range(4):
                    chunks.append((True, kt))
            for kt in range(4):
                chunks.append((False, kt))
            nch = len(chunks)
            t_c0_last = t_c1_last = None
            for ci, (isprev, kt) in enumerate(chunks):
                if isprev:
                    q0 = 0
                    width = (kt + 1) * 128
                    koff = pslot * 512 + kt * 128
                    gblk = qb - 1
                else:
                    q0 = kt * 128
                    width = 512 - q0
                    koff = slot * 512 + kt * 128
                    gblk = qb
                rsub = (gblk * 4 + kt) % 8
                spar = ci % 2
                sps = psS[spar]
                t_s = p.on("pe", lambda t, h=h, koff=koff, q0=q0, width=width, sps=sps: t.matmul(
                    sps[:, 0:width],
                    bass.AP(kf, h * 1024 + koff, [[NH * 1024, 128], [1, 128]]),
                    bass.AP(qf, h * 512 + q0, [[NH * 512, 128], [1, width]]),
                    start=True, stop=True),
                    deps=[tmax(t_qf, t_kf2, psS_free[spar])])
                if isprev:
                    mq0 = kt * 128
                    t_s = p.on("dve", lambda v, sps=sps, mq0=mq0: v.tensor_tensor(
                        sps[:, mq0:mq0 + 128], sps[:, mq0:mq0 + 128], amask[:], ALU.add),
                        deps=[t_s, t_const])
                else:
                    t_s = p.on("dve", lambda v, sps=sps: v.tensor_tensor(
                        sps[:, 0:128], sps[:, 0:128], dmask[:], ALU.add),
                        deps=[t_s, t_const])
                sl = ci % 3
                t_e = p.on("act", lambda s, sl=sl, q0=q0, width=width, sps=sps: s.activation(
                    pbuf[:, sl * 512 + q0: sl * 512 + q0 + width], sps[:, 0:width],
                    AF.Exp, scale=SCALE), deps=[t_s, pbuf_free[sl]])
                psS_free[spar] = t_e
                pads = []
                if q0 > 0:
                    pads.append((sl * 512, q0))
                if q0 + width < 512:
                    pads.append((sl * 512 + q0 + width, 512 - q0 - width))
                for off, ln in pads:
                    t_mz = p.on("dve", lambda v, off=off, ln=ln: v.memset(
                        pbuf[:, off:off + ln], 0.0), deps=[pbuf_free[sl]])
                    t_e = tmax(t_e, t_mz)
                first = (ci == 0)
                last = (ci == nch - 1)
                t_c0 = p.on("pe", lambda t, rsub=rsub, h=h, sl=sl, first=first, last=last: t.matmul(
                    psC[0][0:65, :],
                    bass.AP(vt, rsub * MOP + h * 129, [[8 * MOP, 128], [1, 65]]),
                    pbuf[:, sl * 512:(sl + 1) * 512], start=first, stop=last),
                    deps=[tmax(t_e, t_v, psC_free[0] if first else None)])
                t_c1 = p.on("pe", lambda t, rsub=rsub, h=h, sl=sl, first=first, last=last: t.matmul(
                    psC[1][0:64, :],
                    bass.AP(vt, rsub * MOP + h * 129 + 65, [[8 * MOP, 128], [1, 64]]),
                    pbuf[:, sl * 512:(sl + 1) * 512], start=first, stop=last),
                    deps=[tmax(t_e, t_v, psC_free[1] if first else None)])
                pbuf_free[sl] = tmax(t_c0, t_c1)
                t_c0_last, t_c1_last = t_c0, t_c1
            t_rc = p.on("dve", lambda v: v.reciprocal(lrow[:], psC[0][64:65, 0:512]),
                        deps=[t_c0_last, lrow_free])
            t_b1 = p.on("pe", lambda t: t.matmul(psN[:], ones32r[:], lrow[:],
                                                 start=True, stop=True),
                        deps=[t_rc, psP_free[0]])
            psP_free[0] = t_b1
            lrow_free = t_b1
            t_lc = p.on("dve", lambda v: v.tensor_copy(lsc[:], psN[:]), deps=[t_b1, lsc_free])
            psP_free[0] = t_lc
            t_ev0 = p.on("dve", lambda v, h=h: v.tensor_tensor(
                bass.AP(ctxs, h * 512, [[NH * 512, 64], [1, 512]]),
                psC[0][0:64, :], lsc[0:64, :], ALU.mult),
                deps=[t_lc, t_c0_last, ctxs_free])
            t_ev1 = p.on("dve", lambda v, h=h: v.tensor_tensor(
                bass.AP(ctxs, 64 * NH * 512 + h * 512, [[NH * 512, 64], [1, 512]]),
                psC[1][0:64, :], lsc[64:128, :], ALU.mult),
                deps=[t_lc, t_c1_last, ctxs_free])
            t_ev = tmax(t_ev0, t_ev1)
            lsc_free = t_ev
            psC_free = [t_ev0, t_ev1]
            ev_toks_all.append(t_ev)
        t_ctx_all = tmax(*ev_toks_all)
        qf_free = t_ctx_all

        st_deps = []
        for oc in range(16):
            par = oc % 2
            wosl = wts[:, par * NC * 512: par * NC * 512 + 8 * 128]
            t_lw = p.dma("pool", wosl,
                         bass.AP(io["wo_t"].tensor, oc * 128 * MO, [[MO, 128], [1, MO]]),
                         deps=[wts_free[par]], sem="lw%do" % par)
            t_mm = None
            for cc in range(NH):
                t_mm = p.on("pe", lambda t, oc=oc, cc=cc, par=par, wosl=wosl: t.matmul(
                    psP[1][:], wosl[:, cc * 128:(cc + 1) * 128],
                    ctxs[:, cc * 512:(cc + 1) * 512], start=(cc == 0), stop=(cc == NH - 1)),
                    deps=[tmax(t_ctx_all, t_lw, psP_free[1])] if cc == 0 else (),
                    inc=1 if cc == NH - 1 else 0)
            wts_free[par] = t_mm
            wos = wof[:, par * 512:(par + 1) * 512]
            t_ev = p.on("act", lambda s, par=par, wos=wos: s.copy(wos, psP[1][:]),
                        deps=[t_mm, wof_free])
            psP_free[1] = t_ev
            for b in range(4):
                tpar = (oc * 4 + b) % 2
                t_tp = p.on("pe", lambda t, b=b, tpar=tpar, wos=wos: t.transpose(
                    psT[tpar][:, 0:128], wos[:, b * 128:(b + 1) * 128], ident[:]),
                    deps=[t_ev, psT_free[tpar]])
                t_cp = p.on("dve", lambda v, b=b, oc=oc, tpar=tpar: v.tensor_copy(
                    part[:, b * D + oc * 128: b * D + (oc + 1) * 128], psT[tpar][:, 0:128]),
                    deps=[t_tp, part_free])
                psT_free[tpar] = t_cp
                st_deps.append(t_cp)
            wof_free = st_deps[-1]
        ctxs_free = tmax(*st_deps)
        t_st = p.dma("pool", row_ap(io["out"]),
                     bass.AP(part, 0, [[4 * D, 128], [D, 4], [1, D]]),
                     deps=[tmax(*st_deps)], sem="sto")
        part_free = t_st
        out_toks.append(t_st)
        cos_free = tmax(t_qr, t_kr)

    p.on("sp", lambda e: e.nop(), deps=out_toks, inc=0)
    return out_toks






D = 2048
NE = 16
FM = 1024
NC = 16


def build_moe(p, io, T=2048, ST=1024):
    NSUP = T // ST
    NSB = ST // 128
    NFC = FM // 128

    x3 = p.sbuf("x3", [128, NSB * D], BF16)
    scr = p.sbuf("scr", [128, NSB * D], BF16)
    ye = scr
    ysh = scr
    xf = p.sbuf("xf", [128, NC * ST], BF16)
    wgr = p.sbuf("wgr", [128, NC * NE], BF16)
    ebias = p.sbuf("ebias", [128, 128], F32)
    ebrow = p.sbuf("ebrow", [1, 128], F32)
    scores = p.sbuf("scores", [128, NSB * NE], F32)
    scb = p.sbuf("scb", [128, NSB * NE], F32)
    prs = p.sbuf("prs", [128, 6 * NSB * 4], F32)
    gsum = p.sbuf("gsum", [128, NSB * 4], F32)
    mm1 = p.sbuf("mm1", [128, NSB], F32)
    mm2 = p.sbuf("mm2", [128, NSB], F32)
    mm3 = p.sbuf("mm3", [128, NSB], F32)
    mm4 = p.sbuf("mm4", [128, NSB], F32)
    tau = p.sbuf("tau", [128, NSB], F32)
    gmsk = p.sbuf("gmsk", [128, NSB * 4], F32)
    th8 = p.sbuf("th8", [128, 8], F32)
    thr = p.sbuf("thr", [128, NSB], F32)
    emsk = p.sbuf("emsk", [128, NSB * NE], F32)
    wmun = p.sbuf("wmun", [128, NSB * NE], F32)
    den = p.sbuf("den", [128, NSB], F32)
    wm = p.sbuf("wm", [128, NSB * NE], F32)
    wmb = p.sbuf("wmb", [128, NSB * 128], BF16)
    gwb = p.sbuf("gwb", [128, 2 * D], BF16)
    uwb = p.sbuf("uwb", [128, 2 * D], BF16)
    dwb = p.sbuf("dwb", [128, 2 * FM], BF16)
    sgb = p.sbuf("sgb", [128, 2 * 512], BF16)
    ustmp = p.sbuf("ustmp", [128, 2 * 512], BF16)
    acc = p.sbuf("acc", [128, NC * ST], F32)
    dnf = p.sbuf("dnf", [128, ST], BF16)
    ident = p.sbuf("ident", [128, 128], BF16)
    id32 = p.sbuf("id32", [128, 128], F32)
    ones32r = p.sbuf("ones32r", [1, 128], F32)
    xn = p.sbuf("xn", [128, D], BF16)
    bn6 = p.sbuf("bn6", [128, 24], F32)
    bn2 = p.sbuf("bn2", [128, 2], F32)
    rstd = p.sbuf("rstd", [128, 1], F32)
    epsc = p.sbuf("epsc", [128, 1], F32)

    psA = [p.psum("psA0", [128, 512]), p.psum("psA1", [128, 512])]
    psB = [p.psum("psB0", [128, 512]), p.psum("psB1", [128, 512])]
    psG = [p.psum("psG0", [128, 512]), p.psum("psG1", [128, 512])]
    psT = [p.psum("psT0", [128, 1024], BF16), p.psum("psT1", [128, 1024], BF16)]

    t_id = p.dma("pool", ident[:], io["ident"], sem="ldc")
    p.dma("pool", id32[:], io["ident"], sem="ldc")
    p.dma("pool", wgr[:], bass.AP(io["wgT"].tensor, 0, [[NE, 128], [128 * NE, NC], [1, NE]]), sem="ldc")
    t_eb = p.dma("pool", ebrow[:], bass.AP(io["ebias_rep"].tensor, 0, [[128, 1], [1, 128]]), sem="ldc")
    t_on = p.on("pool", lambda g: g.memset(ones32r[:], 1.0))
    t_eps = p.on("pool", lambda g: g.memset(epsc[:], 1.0e-5))
    t_eb2 = p.on("pe", lambda t: t.matmul(psA[1][:, 0:128], ones32r[:], ebrow[:],
                                          start=True, stop=True), deps=[t_eb, t_on])
    t_eb3 = p.on("dve", lambda v: v.tensor_copy(ebias[:], psA[1][:, 0:128]), deps=[t_eb2])
    t_const = tmax(t_eb, t_eps, t_on, t_eb3)

    out_toks = []
    x3_free = scr_free = xf_free = None
    psA_free = [t_eb3, None]
    psB_free = [None, None]
    psG_free = [None, None]
    psT_free = [None, None]
    gw_free = [None, None]
    uw_free = [None, None]
    dw_free = [None, None]
    sg_free = [None, None]
    us_free = [None, None]
    wmb_free = None
    acc_free = None
    dnf_free = None
    xn_free = None
    route_free = None

    for sup in range(NSUP):
        row0 = sup * ST

        def row_ap(h, r0=row0):
            return bass.AP(h.tensor, h.offset + r0 * D, [[D, 128], [128 * D, NSB], [1, D]])
        x3_ap = bass.AP(x3, 0, [[NSB * D, 128], [D, NSB], [1, D]])
        pr_ap = bass.AP(scr, 0, [[NSB * D, 128], [D, NSB], [1, D]])

        t_x = p.dma("pool", x3_ap, row_ap(io["x"]), deps=[x3_free], sem="lda")
        t_p1 = p.dma("pool", pr_ap, row_ap(io["pa"]), deps=[scr_free], sem="lda")
        t_a1 = p.on("dve", lambda v: v.tensor_tensor(x3[:], x3[:], scr[:], ALU.add), deps=[t_x, t_p1])
        if io.get("pb") is not None:
            t_p2 = p.dma("pool", pr_ap, row_ap(io["pb"]), deps=[t_a1], sem="lda")
            t_a2 = p.on("dve", lambda v: v.tensor_tensor(x3[:], x3[:], scr[:], ALU.add), deps=[t_p2])
        else:
            t_a2 = t_a1
        scr_free = t_a2

        sig_toks = []
        fm_toks = []
        for b in range(NSB):
            xrb = x3[:, b * D:(b + 1) * D]
            t_bs = None
            for s4 in range(4):
                t_bs = p.on("dve", lambda v, xrb=xrb, s4=s4: v.bn_stats(
                    bn6[:, s4 * 6:(s4 + 1) * 6],
                    bass.AP(xrb.tensor, xrb.offset + s4 * 512, [[xrb.ap[0][0], 128], [1, 512]])),
                    deps=[t_a2, xn_free])
            t_ba = p.on("dve", lambda v: v.bn_aggr(bn2[:], bn6[:]), deps=[t_bs])
            t_sq = p.on("act", lambda s: s.activation(rstd[:], bn2[:, 1:2], AF.Sqrt, bias=epsc[:]),
                        deps=[t_ba, t_const])
            t_rs = p.on("dve", lambda v: v.reciprocal(rstd[:], rstd[:]), deps=[t_sq])
            t_xn = p.on("dve", lambda v, xrb=xrb: v.tensor_scalar(
                xn[:], xrb, bn2[:, 0:1], rstd[:], ALU.subtract, ALU.mult),
                deps=[t_rs, xn_free])
            tps = []
            ev1 = []
            for c in range(NC):
                par = c % 2
                t_tp = p.on("pe", lambda t, c=c, par=par: t.transpose(
                    psT[par][:, 0:128], xn[:, c * 128:(c + 1) * 128], ident[:]),
                    deps=[t_xn, t_const, psT_free[par]])
                t_e1 = p.on("dve", lambda v, c=c, b=b, par=par: v.tensor_copy(
                    xf[:, c * ST + b * 128: c * ST + (b + 1) * 128], psT[par][:, 0:128]),
                    deps=[t_tp, xf_free])
                psT_free[par] = t_e1
                tps.append(t_tp)
                fm_toks.append(t_e1)
                ev1.append(t_e1)
            xn_free = tmax(*tps)
            t_rm = None
            for c in range(NC):
                t_rm = p.on("pe", lambda t, c=c, b=b: t.matmul(
                    psA[0][:, 0:NE], xf[:, c * ST + b * 128: c * ST + (b + 1) * 128],
                    wgr[:, c * NE:(c + 1) * NE],
                    start=(c == 0), stop=(c == NC - 1)),
                    deps=[tmax(ev1[c], t_const, psA_free[0])] if c == 0 else [ev1[c]],
                    inc=1 if c == NC - 1 else 0)
            t_sg2 = p.on("act", lambda s, b=b: s.activation(
                scores[:, b * NE:(b + 1) * NE], psA[0][:, 0:NE], AF.Sigmoid),
                deps=[t_rm, route_free])
            psA_free[0] = t_sg2
            sig_toks.append(t_sg2)
        t_fm = tmax(*fm_toks)
        t_sc = tmax(*sig_toks)

        V = 128
        t0 = p.on("dve", lambda v: v.tensor_tensor(scb[:], scores[:], ebias[:], ALU.add),
                  deps=[t_sc, t_const, route_free])
        pairs = [(0, 1), (0, 2), (0, 3), (1, 2), (1, 3), (2, 3)]
        pr_toks = []
        for pi, (i, jj) in enumerate(pairs):
            pr_toks.append(p.on("dve", lambda v, pi=pi, i=i, jj=jj: v.tensor_tensor(
                bass.AP(prs, pi * 32, [[6 * 32, 128], [4, NSB], [1, 4]]),
                bass.AP(scb, i, [[NSB * NE, 128], [NE, NSB], [4, 4]]),
                bass.AP(scb, jj, [[NSB * NE, 128], [NE, NSB], [4, 4]]), ALU.add),
                deps=[t0, route_free]))
        tp = tmax(*pr_toks)
        def prsv(pi):
            return prs[:, pi * 32:(pi + 1) * 32]
        t1 = p.on("dve", lambda v: v.tensor_tensor(prsv(0), prsv(0), prsv(1), ALU.max), deps=[tp])
        t2 = p.on("dve", lambda v: v.tensor_tensor(prsv(2), prsv(2), prsv(3), ALU.max), deps=[tp])
        t3 = p.on("dve", lambda v: v.tensor_tensor(prsv(4), prsv(4), prsv(5), ALU.max), deps=[tp])
        t4 = p.on("dve", lambda v: v.tensor_tensor(prsv(0), prsv(0), prsv(2), ALU.max), deps=[t1, t2])
        t5 = p.on("dve", lambda v: v.tensor_tensor(gsum[:], prsv(0), prsv(4), ALU.max), deps=[t3, t4])
        def gv(g):
            return bass.AP(gsum, g, [[NSB * 4, 128], [4, NSB]])
        t6 = p.on("dve", lambda v: v.tensor_tensor(mm1[:], gv(0), gv(1), ALU.max), deps=[t5])
        t7 = p.on("dve", lambda v: v.tensor_tensor(mm2[:], gv(0), gv(1), ALU.min), deps=[t5])
        t8 = p.on("dve", lambda v: v.tensor_tensor(mm3[:], gv(2), gv(3), ALU.max), deps=[t5])
        t9 = p.on("dve", lambda v: v.tensor_tensor(mm4[:], gv(2), gv(3), ALU.min), deps=[t5])
        ta = p.on("dve", lambda v: v.tensor_tensor(mm1[:], mm1[:], mm3[:], ALU.min), deps=[t6, t8])
        tb = p.on("dve", lambda v: v.tensor_tensor(mm2[:], mm2[:], mm4[:], ALU.max), deps=[t7, t9])
        tc = p.on("dve", lambda v: v.tensor_tensor(tau[:], mm1[:], mm2[:], ALU.max), deps=[ta, tb])
        gm_toks = []
        for g in range(4):
            gm_toks.append(p.on("dve", lambda v, g=g: v.tensor_tensor(
                bass.AP(gmsk, g, [[NSB * 4, 128], [4, NSB]]), gv(g), tau[:], ALU.is_ge),
                deps=[tc]))
        td = p.on("dve", lambda v: v.tensor_scalar(gmsk[:], gmsk[:], 1.0, 1.0e9, ALU.subtract, ALU.mult),
                  deps=gm_toks)
        sm_toks = []
        for i in range(4):
            sm_toks.append(p.on("dve", lambda v, i=i: v.tensor_tensor(
                bass.AP(scb, i, [[NSB * NE, 128], [NE, NSB], [4, 4]]),
                bass.AP(scb, i, [[NSB * NE, 128], [NE, NSB], [4, 4]]),
                bass.AP(gmsk, 0, [[NSB * 4, 128], [4, NSB], [1, 4]]), ALU.add),
                deps=[td]))
        te = tmax(*sm_toks)
        th_toks = []
        t_th = None
        for b in range(NSB):
            t_m8 = p.on("dve", lambda v, b=b: v.max(th8[:], scb[:, b * NE:(b + 1) * NE]),
                        deps=[te, t_th])
            t_th = p.on("dve", lambda v, b=b: v.tensor_copy(thr[:, b:b + 1], th8[:, 3:4]), deps=[t_m8])
            th_toks.append(t_th)
        em_toks = []
        for b in range(NSB):
            em_toks.append(p.on("dve", lambda v, b=b: v.tensor_scalar(
                emsk[:, b * NE:(b + 1) * NE], scb[:, b * NE:(b + 1) * NE],
                thr[:, b:b + 1], None, ALU.is_ge), deps=[th_toks[b]]))
        tf = p.on("dve", lambda v: v.tensor_tensor(wmun[:], scores[:], emsk[:], ALU.mult),
                  deps=tmax(*em_toks, t_sc))
        tg = p.on("dve", lambda v: v.tensor_reduce(
            den[:], bass.AP(wmun, 0, [[NSB * NE, 128], [NE, NSB], [1, NE]]), AX.X, ALU.add),
            deps=[tf])
        th_ = p.on("dve", lambda v: v.tensor_scalar(den[:], den[:], 1.0e-20, None, ALU.add), deps=[tg])
        ti = p.on("dve", lambda v: v.reciprocal(den[:], den[:]), deps=[th_])
        tj = p.on("dve", lambda v: v.tensor_scalar(den[:], den[:], 2.5, None, ALU.mult), deps=[ti])
        wm_toks = []
        for b in range(NSB):
            wm_toks.append(p.on("dve", lambda v, b=b: v.tensor_scalar(
                wm[:, b * NE:(b + 1) * NE], wmun[:, b * NE:(b + 1) * NE],
                den[:, b:b + 1], None, ALU.mult), deps=[tj]))
        t_wm = tmax(*wm_toks)
        route_free = t_wm

        for half in range(2):
            ysh_evs = []
            for oc in range(8):
                par = oc % 2
                gws = gwb[:, par * D:(par + 1) * D]
                uws = uwb[:, par * D:(par + 1) * D]
                t_lg = p.dma("pool", gws, bass.AP(io["shg_t"].tensor, (half * 8 + oc) * 128 * D,
                                                  [[D, 128], [1, D]]), deps=[gw_free[par]], sem="lg%d" % par)
                t_lu = p.dma("pool", uws, bass.AP(io["shu_t"].tensor, (half * 8 + oc) * 128 * D,
                                                  [[D, 128], [1, D]]), deps=[uw_free[par]], sem="lu%d" % par)
                t_g = t_u = None
                for c in range(NC):
                    t_g = p.on("pe", lambda t, c=c, par=par, gws=gws: t.matmul(
                        psG[0][:], gws[:, c * 128:(c + 1) * 128], xf[:, c * ST:c * ST + 512],
                        start=(c == 0), stop=(c == NC - 1)),
                        deps=[tmax(t_fm, t_lg, psG_free[0])] if c == 0 else (), inc=0)
                    t_g = p.on("pe", lambda t, c=c, par=par, gws=gws: t.matmul(
                        psG[1][:], gws[:, c * 128:(c + 1) * 128], xf[:, c * ST + 512:(c + 1) * ST],
                        start=(c == 0), stop=(c == NC - 1)),
                        deps=[psG_free[1]] if c == 0 else (), inc=1 if c == NC - 1 else 0)
                for c in range(NC):
                    t_u = p.on("pe", lambda t, c=c, par=par, uws=uws: t.matmul(
                        psB[0][:], uws[:, c * 128:(c + 1) * 128], xf[:, c * ST:c * ST + 512],
                        start=(c == 0), stop=(c == NC - 1)),
                        deps=[tmax(t_lu, psB_free[0])] if c == 0 else (), inc=0)
                    t_u = p.on("pe", lambda t, c=c, par=par, uws=uws: t.matmul(
                        psB[1][:], uws[:, c * 128:(c + 1) * 128], xf[:, c * ST + 512:(c + 1) * ST],
                        start=(c == 0), stop=(c == NC - 1)),
                        deps=[psB_free[1]] if c == 0 else (), inc=1 if c == NC - 1 else 0)
                gw_free[par] = t_g
                uw_free[par] = t_u
                for th in range(2):
                    t_sg3 = p.on("act", lambda s, th=th: s.activation(
                        sgb[:, th * 512:(th + 1) * 512], psG[th][:], AF.Silu),
                        deps=[t_g, sg_free[th]])
                    psG_free[th] = t_sg3
                    t_ys = p.on("dve", lambda v, oc=oc, th=th: v.tensor_tensor(
                        ysh[:, NFC * ST + oc * ST + th * 512: NFC * ST + oc * ST + (th + 1) * 512],
                        psB[th][:], sgb[:, th * 512:(th + 1) * 512], ALU.mult),
                        deps=[t_u, t_sg3, scr_free])
                    sg_free[th] = t_ys
                    psB_free[th] = t_ys
                    ysh_evs.append(t_ys)
            t_ysh = tmax(*ysh_evs)
            for oc2 in range(NC):
                par = oc2 % 2
                dws = dwb[:, par * FM:(par + 1) * FM]
                t_ld = p.dma("pool", dws, bass.AP(io["shd_t"].tensor, oc2 * 128 * D + half * 1024,
                                                  [[D, 128], [1, 1024]]), deps=[dw_free[par]], sem="ld%d" % par)
                t_d = None
                for fc in range(8):
                    t_d = p.on("pe", lambda t, fc=fc, oc2=oc2, dws=dws: t.matmul(
                        psA[0][:], dws[:, fc * 128:(fc + 1) * 128],
                        scr[:, NFC * ST + fc * ST: NFC * ST + fc * ST + 512],
                        start=(fc == 0), stop=(fc == 7)),
                        deps=[tmax(t_ysh, t_ld, psA_free[0])] if fc == 0 else (), inc=0)
                    t_d = p.on("pe", lambda t, fc=fc, oc2=oc2, dws=dws: t.matmul(
                        psA[1][:], dws[:, fc * 128:(fc + 1) * 128],
                        scr[:, NFC * ST + fc * ST + 512: NFC * ST + (fc + 1) * ST],
                        start=(fc == 0), stop=(fc == 7)),
                        deps=[psA_free[1]] if fc == 0 else (), inc=1 if fc == 7 else 0)
                dw_free[par] = t_d
                for th in range(2):
                    if half == 0:
                        t_ac = p.on("dve", lambda v, oc2=oc2, th=th: v.tensor_copy(
                            acc[:, oc2 * ST + th * 512: oc2 * ST + (th + 1) * 512], psA[th][:]),
                            deps=[t_d, acc_free])
                    else:
                        t_ac = p.on("dve", lambda v, oc2=oc2, th=th: v.tensor_tensor(
                            acc[:, oc2 * ST + th * 512: oc2 * ST + (th + 1) * 512],
                            acc[:, oc2 * ST + th * 512: oc2 * ST + (th + 1) * 512], psA[th][:], ALU.add),
                            deps=[t_d])
                    psA_free[th] = t_ac
            acc_free = None

        for e in range(NE):
            wb_toks = []
            for b in range(NSB):
                t_wb = p.on("pe", lambda t, b=b, e=e: t.matmul(
                    psA[0][:, 0:128],
                    bass.AP(wm, b * NE + e, [[NSB * NE, 128], [0, 128]]), id32[:],
                    start=True, stop=True), deps=[t_wm, psA_free[0], t_const])
                t_wc = p.on("dve", lambda v, b=b: v.tensor_copy(
                    wmb[:, b * 128:(b + 1) * 128], psA[0][:, 0:128]), deps=[t_wb, wmb_free])
                psA_free[0] = t_wc
                wb_toks.append(t_wc)
            t_wmb = tmax(*wb_toks)
            ye_evs = []
            for oc in range(NFC):
                par = oc % 2
                gws = gwb[:, par * D:(par + 1) * D]
                uws = uwb[:, par * D:(par + 1) * D]
                t_lg = p.dma("pool", gws, bass.AP(io["gw_t"].tensor, (e * NFC + oc) * 128 * D,
                                                  [[D, 128], [1, D]]), deps=[gw_free[par]], sem="lg%d" % par)
                t_lu = p.dma("pool", uws, bass.AP(io["uw_t"].tensor, (e * NFC + oc) * 128 * D,
                                                  [[D, 128], [1, D]]), deps=[uw_free[par]], sem="lu%d" % par)
                t_g = t_u = None
                for c in range(NC):
                    t_g = p.on("pe", lambda t, c=c, gws=gws: t.matmul(
                        psG[0][:], gws[:, c * 128:(c + 1) * 128], xf[:, c * ST:c * ST + 512],
                        start=(c == 0), stop=(c == NC - 1)),
                        deps=[tmax(t_fm, t_lg, psG_free[0])] if c == 0 else (), inc=0)
                    t_g = p.on("pe", lambda t, c=c, gws=gws: t.matmul(
                        psG[1][:], gws[:, c * 128:(c + 1) * 128], xf[:, c * ST + 512:(c + 1) * ST],
                        start=(c == 0), stop=(c == NC - 1)),
                        deps=[psG_free[1]] if c == 0 else (), inc=1 if c == NC - 1 else 0)
                for c in range(NC):
                    t_u = p.on("pe", lambda t, c=c, uws=uws: t.matmul(
                        psB[0][:], uws[:, c * 128:(c + 1) * 128], xf[:, c * ST:c * ST + 512],
                        start=(c == 0), stop=(c == NC - 1)),
                        deps=[tmax(t_lu, psB_free[0])] if c == 0 else (), inc=0)
                    t_u = p.on("pe", lambda t, c=c, uws=uws: t.matmul(
                        psB[1][:], uws[:, c * 128:(c + 1) * 128], xf[:, c * ST + 512:(c + 1) * ST],
                        start=(c == 0), stop=(c == NC - 1)),
                        deps=[psB_free[1]] if c == 0 else (), inc=1 if c == NC - 1 else 0)
                gw_free[par] = t_g
                uw_free[par] = t_u
                for th in range(2):
                    t_sg3 = p.on("act", lambda s, th=th: s.activation(
                        sgb[:, th * 512:(th + 1) * 512], psG[th][:], AF.Silu),
                        deps=[t_g, sg_free[th]])
                    psG_free[th] = t_sg3
                    t_us = p.on("dve", lambda v, th=th: v.tensor_tensor(
                        ustmp[:, th * 512:(th + 1) * 512], psB[th][:],
                        bass.AP(wmb, th * 512, [[NSB * 128, 128], [1, 512]]), ALU.mult),
                        deps=[t_u, t_wmb, us_free[th]])
                    psB_free[th] = t_us
                    t_ye = p.on("dve", lambda v, oc=oc, th=th: v.tensor_tensor(
                        ye[:, oc * ST + th * 512: oc * ST + (th + 1) * 512],
                        ustmp[:, th * 512:(th + 1) * 512], sgb[:, th * 512:(th + 1) * 512], ALU.mult),
                        deps=[t_us, t_sg3])
                    sg_free[th] = t_ye
                    us_free[th] = t_ye
                    ye_evs.append(t_ye)
            wmb_free = tmax(*ye_evs)
            t_ye_all = tmax(*ye_evs)
            for oc2 in range(NC):
                par = oc2 % 2
                dws = dwb[:, par * FM:(par + 1) * FM]
                t_ld = p.dma("pool", dws, bass.AP(io["dw_t"].tensor, (e * NC + oc2) * 128 * FM,
                                                  [[FM, 128], [1, FM]]), deps=[dw_free[par]], sem="ld%d" % par)
                t_d = None
                for fc in range(NFC):
                    t_d = p.on("pe", lambda t, fc=fc, dws=dws: t.matmul(
                        psA[0][:], dws[:, fc * 128:(fc + 1) * 128], ye[:, fc * ST: fc * ST + 512],
                        start=(fc == 0), stop=(fc == NFC - 1)),
                        deps=[tmax(t_ye_all, t_ld, psA_free[0])] if fc == 0 else (), inc=0)
                    t_d = p.on("pe", lambda t, fc=fc, dws=dws: t.matmul(
                        psA[1][:], dws[:, fc * 128:(fc + 1) * 128], ye[:, fc * ST + 512:(fc + 1) * ST],
                        start=(fc == 0), stop=(fc == NFC - 1)),
                        deps=[psA_free[1]] if fc == 0 else (), inc=1 if fc == NFC - 1 else 0)
                dw_free[par] = t_d
                for th in range(2):
                    t_ac = p.on("dve", lambda v, oc2=oc2, th=th: v.tensor_tensor(
                        acc[:, oc2 * ST + th * 512: oc2 * ST + (th + 1) * 512],
                        acc[:, oc2 * ST + th * 512: oc2 * ST + (th + 1) * 512], psA[th][:], ALU.add),
                        deps=[t_d])
                    psA_free[th] = t_ac
            scr_free = tmax(*[psA_free[0], psA_free[1]])

        st_deps = []
        t_acc_all = tmax(psA_free[0], psA_free[1])
        for oc2 in range(NC):
            t_cp = p.on("dve", lambda v, oc2=oc2: v.tensor_copy(
                dnf[:], acc[:, oc2 * ST:(oc2 + 1) * ST]), deps=[t_acc_all, dnf_free])
            tp_toks = []
            for b in range(NSB):
                tpar = b // 4
                t_tp = p.on("pe", lambda t, b=b, tpar=tpar: t.transpose(
                    psT[tpar][:, (b % 4) * 128:((b % 4) + 1) * 128],
                    dnf[:, b * 128:(b + 1) * 128], ident[:]),
                    deps=[t_cp, t_const, psT_free[tpar]])
                tp_toks.append(t_tp)
            for tpar in range(2):
                t_ad = p.on("dve", lambda v, oc2=oc2, tpar=tpar: v.tensor_tensor(
                    bass.AP(x3, tpar * 4 * D + oc2 * 128, [[NSB * D, 128], [D, 4], [1, 128]]),
                    bass.AP(x3, tpar * 4 * D + oc2 * 128, [[NSB * D, 128], [D, 4], [1, 128]]),
                    bass.AP(psT[tpar], 0, [[1024, 128], [128, 4], [1, 128]]), ALU.add),
                    deps=tp_toks + [t_a2])
                psT_free[tpar] = t_ad
                st_deps.append(t_ad)
            dnf_free = tmax(*st_deps)
        t_st = p.dma("pool", row_ap(io["out"]), x3_ap, deps=[tmax(*st_deps)], sem="sto")
        x3_free = t_st
        acc_free = t_acc_all
        xf_free = scr_free
        out_toks.append(t_st)

    p.on("sp", lambda e: e.nop(), deps=out_toks, inc=0)
    return out_toks





SEQ = 4096
DM = 2048


def _mk_consts():
    i = np.arange(128, dtype=np.uint64)
    pc = np.array([bin(v).count("1") for v in range(128 * 2)], dtype=np.int64)
    popc = pc[(i[:, None] & i[None, :])]
    h128 = np.where(popc % 2 == 0, 1.0, -1.0).astype(np.float32)
    ident = np.eye(128, dtype=np.float32)
    r = np.arange(128)
    dmask = np.where(r[None, :] <= r[:, None], 0.0, -1e9).astype(np.float32)
    dmaskT = np.ascontiguousarray(dmask.T)
    amask = np.where(r[None, :] > r[:, None], 0.0, -1e9).astype(np.float32)
    amaskT = np.ascontiguousarray(amask.T)
    inv = 1.0 / 10000.0 ** (np.arange(0, 128, 2) / 128.0)
    fr = np.arange(SEQ)[:, None] * inv[None, :]
    cos_rep = np.tile(np.cos(fr), (1, 8)).astype(np.float32)
    sin_rep = np.tile(np.sin(fr), (1, 8)).astype(np.float32)
    return h128, ident, dmask, dmaskT, amask, amaskT, cos_rep, sin_rep


def _perm_gqa():
    perm = []
    for oc in range(8):
        for bhalf in (0, 1):
            g = (oc // 4) * 2 + bhalf
            h = g * 4 + (oc % 4)
            perm.extend(range(h * 64, (h + 1) * 64))
    return np.array(perm)


def _f32(a):
    return np.ascontiguousarray(a, dtype=np.float32)


PAIRS = [[0, 1], [2, 3], [4, 5], [6, 7]]


def kernel(**inp):
    h128, ident, dmask, dmaskT, amask, amaskT, cos_rep, sin_rep = _mk_consts()
    perm = _perm_gqa()
    x = _f32(inp["L0_attn_x"]).reshape(4, SEQ, DM)

    wq_t = [host_tile_T(_f32(inp["L0_attn_Wq"][1024 * m:1024 * (m + 1), :])[perm]) for m in (0, 1)]
    wk_t = [host_tile_T(_f32(inp["L0_attn_Wk"][256 * m:256 * (m + 1), :])) for m in (0, 1)]
    wvT = [_f32(inp["L0_attn_Wv"][256 * m:256 * (m + 1), :].T) for m in (0, 1)]
    wo_t = [host_tile_T(_f32(inp["L0_attn_Wo"][:, 1024 * m:1024 * (m + 1)][:, perm])) for m in (0, 1)]
    wupT = host_tile_T(_f32(inp["L0_ffn_W_up"]))
    wgT = host_tile_T(_f32(inp["L0_ffn_W_gate"]))
    wdT = host_tile_T(_f32(inp["L0_ffn_W_down"]))
    wqT1 = [_f32(inp["L1_attn_Wq"][1024 * m:1024 * (m + 1), :].T) for m in (0, 1)]
    wkT1 = [_f32(inp["L1_attn_Wk"][1024 * m:1024 * (m + 1), :].T) for m in (0, 1)]
    wvT1 = [_f32(inp["L1_attn_Wv"][1024 * m:1024 * (m + 1), :].T) for m in (0, 1)]
    wo_t1 = [host_tile_T(_f32(inp["L1_attn_Wo"][:, 1024 * m:1024 * (m + 1)])) for m in (0, 1)]
    gw_t = np.stack([host_tile_T(_f32(inp["L1_ffn_gate_w"][e])) for e in range(16)])
    uw_t = np.stack([host_tile_T(_f32(inp["L1_ffn_up_w"][e])) for e in range(16)])
    dw_t = np.stack([host_tile_T(_f32(inp["L1_ffn_down_w"][e])) for e in range(16)])
    shg_t = host_tile_T(_f32(inp["L1_ffn_sh_gate_w"]))
    shu_t = host_tile_T(_f32(inp["L1_ffn_sh_up_w"]))
    shd_t = host_tile_T(_f32(inp["L1_ffn_sh_down_w"]))
    wgT1 = _f32(inp["L1_ffn_gate_weight"].T)
    eb = np.tile(_f32(inp["L1_ffn_e_score_correction_bias"]), 8)
    su = _f32(inp["L0_ffn_SU_ug"])
    sd = _f32(inp["L0_ffn_SU_dp"])
    wsu = _f32(inp["L0_ffn_Wscale_ug"])
    wsd = _f32(inp["L0_ffn_Wscale_dp"])

    nc = bass.Bass("TRN2", target_bir_lowering=False)
    ext = {}
    especs = {
        "xA": (SEQ, DM), "xB": (2048, DM),
        "ident": (128, 128), "dmask": (128, 128), "dmaskt": (128, 128), "amask": (128, 128), "amaskt": (128, 128), "h128": (128, 128),
        "cos_rep": cos_rep.shape, "sin_rep": sin_rep.shape,
        "a_wq_t": wq_t[0].shape, "a_wk_t": wk_t[0].shape,
        "a_wvT": wvT[0].shape, "a_wo_t": wo_t[0].shape,
        "b_wupT": wupT.shape, "b_wgT": wgT.shape, "b_wdT": wdT.shape,
        "b_su_ug": su.shape, "b_su_dp": sd.shape,
        "b_wsc_ug": wsu.shape, "b_wsc_dp": wsd.shape,
        "c_wqT": wqT1[0].shape, "c_wkT": wkT1[0].shape,
        "c_wvT": wvT1[0].shape, "c_wo_t": wo_t1[0].shape,
        "d_wgT": wgT1.shape, "d_ebias": eb.shape,
        "d_gw_t": gw_t.shape, "d_uw_t": uw_t.shape, "d_dw_t": dw_t.shape,
        "d_shg_t": shg_t.shape, "d_shu_t": shu_t.shape, "d_shd_t": shd_t.shape,
    }
    BF16W = {"a_wq_t", "a_wk_t", "a_wvT", "a_wo_t",
             "b_wupT", "b_wgT", "b_wdT",
             "c_wqT", "c_wkT", "c_wvT", "c_wo_t",
             "d_gw_t", "d_uw_t", "d_dw_t", "d_shg_t", "d_shu_t", "d_shd_t"}
    for n, s in especs.items():
        dt = BF16 if n in BF16W else F32
        ext[n] = nc.dram_tensor(n, list(s), dt, kind="ExternalInput").ap()
    out_t = nc.dram_tensor("out", [2048, DM], F32, kind="ExternalOutput").ap()

    pA_loc = nc.dram_tensor("pA_loc", [SEQ, DM], F32, kind="Internal")
    pA_red = nc.dram_tensor("pA_red", [2048, DM], F32, kind="Internal")
    x2_half = nc.dram_tensor("x2_half", [2048, DM], F32, kind="Internal")
    x2_full = nc.dram_tensor("x2_full", [SEQ, DM], F32, kind="Internal")
    pC_loc = nc.dram_tensor("pC_loc", [SEQ, DM], F32, kind="Internal")
    pC_red = nc.dram_tensor("pC_red", [2048, DM], F32, kind="Internal")

    p = Prog(nc)

    p.begin_phase("a_")
    ioA = {"x": ext["xA"], "wq_t": ext["a_wq_t"], "wk_t": ext["a_wk_t"],
           "wvT": ext["a_wvT"], "wo_t": ext["a_wo_t"],
           "ident": ext["ident"], "dmaskt": ext["dmaskt"], "out": pA_loc.ap()}
    toksA = build_attn0(p, ioA, S=SEQ)
    p.cc("ReduceScatter", ALU.add, [pA_loc.ap().opt()], [pA_red.ap().opt()],
         PAIRS, deps=[tmax(*toksA)], name="cc1")
    p.fence()

    p.begin_phase("b_")
    ioB = {"x": ext["xB"], "pa": pA_red.ap(), "pb": None, "out": x2_half.ap(),
           "wupT": ext["b_wupT"], "wgT": ext["b_wgT"], "wdT": ext["b_wdT"],
           "su_ug": ext["b_su_ug"], "su_dp": ext["b_su_dp"],
           "wsc_ug": ext["b_wsc_ug"], "wsc_dp": ext["b_wsc_dp"],
           "h128": ext["h128"], "ident": ext["ident"]}
    toksB = build_ffn(p, ioB, T=2048, TT=512)
    p.cc("AllGather", ALU.bypass, [x2_half.ap().opt()], [x2_full.ap().opt()],
         PAIRS, deps=[tmax(*toksB)], name="cc2")
    p.fence()

    p.begin_phase("c_")
    ioC = {"x": x2_full.ap(), "wqT": ext["c_wqT"], "wkT": ext["c_wkT"],
           "wvT": ext["c_wvT"], "wo_t": ext["c_wo_t"],
           "cos_rep": ext["cos_rep"], "sin_rep": ext["sin_rep"],
           "ident": ext["ident"], "dmaskt": ext["dmaskt"], "amaskt": ext["amaskt"],
           "out": pC_loc.ap()}
    toksC = build_attn1(p, ioC, S=SEQ)
    p.cc("ReduceScatter", ALU.add, [pC_loc.ap().opt()], [pC_red.ap().opt()],
         PAIRS, deps=[tmax(*toksC)], name="cc3")
    p.fence()

    p.begin_phase("d_")
    ioD = {"x": x2_half.ap(), "pa": pC_red.ap(), "pb": None, "out": out_t,
           "wgT": ext["d_wgT"], "ebias_rep": ext["d_ebias"],
           "gw_t": ext["d_gw_t"], "uw_t": ext["d_uw_t"], "dw_t": ext["d_dw_t"],
           "shg_t": ext["d_shg_t"], "shu_t": ext["d_shu_t"], "shd_t": ext["d_shd_t"],
           "ident": ext["ident"]}
    build_moe(p, ioD, T=2048, ST=1024)
    p.finalize()

    def _bf(a):
        return np.ascontiguousarray(a.astype(ml_dtypes.bfloat16))
    wq_t = [_bf(a) for a in wq_t]
    wk_t = [_bf(a) for a in wk_t]
    wvT = [_bf(a) for a in wvT]
    wo_t = [_bf(a) for a in wo_t]
    wupT, wgT, wdT = _bf(wupT), _bf(wgT), _bf(wdT)
    wqT1 = [_bf(a) for a in wqT1]
    wkT1 = [_bf(a) for a in wkT1]
    wvT1 = [_bf(a) for a in wvT1]
    wo_t1 = [_bf(a) for a in wo_t1]
    gw_t, uw_t, dw_t = _bf(gw_t), _bf(uw_t), _bf(dw_t)
    shg_t, shu_t, shd_t = _bf(shg_t), _bf(shu_t), _bf(shd_t)

    in_maps = []
    for c in range(8):
        pq, m = c // 2, c % 2
        rows = slice(2048 * m, 2048 * (m + 1))
        in_maps.append({
            "xA": _f32(x[pq]), "xB": _f32(x[pq][rows]),
            "ident": ident, "dmask": dmask, "dmaskt": dmaskT, "amask": amask, "amaskt": amaskT, "h128": h128,
            "cos_rep": cos_rep, "sin_rep": sin_rep,
            "a_wq_t": wq_t[m], "a_wk_t": wk_t[m], "a_wvT": wvT[m], "a_wo_t": wo_t[m],
            "b_wupT": wupT, "b_wgT": wgT, "b_wdT": wdT,
            "b_su_ug": su, "b_su_dp": sd, "b_wsc_ug": wsu, "b_wsc_dp": wsd,
            "c_wqT": wqT1[m], "c_wkT": wkT1[m], "c_wvT": wvT1[m], "c_wo_t": wo_t1[m],
            "d_wgT": wgT1, "d_ebias": eb,
            "d_gw_t": gw_t, "d_uw_t": uw_t, "d_dw_t": dw_t,
            "d_shg_t": shg_t, "d_shu_t": shu_t, "d_shd_t": shd_t,
        })
    res = run_bass_kernel_spmd(nc, in_maps, core_ids=list(range(8)))
    oh = [res.results[c]["out"] for c in range(8)]
    p.close()
    out = np.stack([np.concatenate([oh[2 * pq], oh[2 * pq + 1]], axis=0) for pq in range(4)])
    return out.astype(np.float32)


# version 07
# speedup vs baseline: 1.1394x; 1.1394x over previous
import contextlib
import math
from collections import defaultdict
import numpy as np
import ml_dtypes
import concourse.bass as bass
import concourse.mybir as mybir
from concourse.bass_utils import run_bass_kernel_spmd







F32 = mybir.dt.float32
BF16 = mybir.dt.bfloat16
AF = mybir.ActivationFunctionType
ALU = mybir.AluOpType
AX = mybir.AxisListType

ENGS = ("pe", "dve", "act", "pool", "sp")
ENG_ATTR = {"pe": "tensor", "dve": "vector", "act": "scalar", "pool": "gpsimd", "sp": "sync"}


def tmax(*toks):
    best = {}
    for t in toks:
        if t is None:
            continue
        if isinstance(t, list):
            for u in t:
                if u is None:
                    continue
                s, v = u
                best[s] = max(best.get(s, 0), v)
            continue
        s, v = t
        best[s] = max(best.get(s, 0), v)
    return [(s, v) for s, v in best.items()]


class Prog:
    def __init__(self, nc):
        self.nc = nc
        self.es = contextlib.ExitStack()
        self.ops = {e: [] for e in ENGS}
        self.sems = {}
        self.cnt = defaultdict(int)
        self.waited = {e: defaultdict(int) for e in ENGS}
        self.dma_sems = set()
        self.n_inst = 0
        self.prefix = ""
        self._phase_mode = False
        self._phase_addr = 0
        self._max_addr = 0
        self._arena_base = None
        self._arena_size = 212832
        self._psum_pool = {}
        self._psum_idx = {}

    def sem(self, name):
        if name not in self.sems:
            self.sems[name] = self.es.enter_context(self.nc.semaphore(name))
        return self.sems[name]

    def begin_phase(self, prefix):
        self.prefix = prefix
        self._phase_mode = True
        self._phase_addr = 0
        self._psum_idx = {}
        if self._arena_base is None:
            backing = self.es.enter_context(self.nc.sbuf_tensor(
                "sb_arena", [self.nc.NUM_PARTITIONS, self._arena_size], mybir.dt.uint8))
            self._arena_base = self.nc.lookup_mloc(backing).addr

    def sbuf(self, name, shape, dtype):
        if not self._phase_mode:
            return self.es.enter_context(self.nc.sbuf_tensor("sb_" + name, shape, dtype))
        per_part = 1
        for s in shape[1:]:
            per_part *= s
        per_part *= mybir.dt.size(dtype)
        addr = self._phase_addr
        self._phase_addr = (addr + per_part + 31) & ~31
        self._max_addr = max(self._max_addr, self._phase_addr)
        assert self._phase_addr <= self._arena_size, (
            "phase %s overflows SBUF arena: %d > %d" %
            (self.prefix, self._phase_addr, self._arena_size))
        return self.nc.alloc_sbuf_tensor_at(
            "sb_" + self.prefix + name, shape, dtype,
            offset=self._arena_base + addr)

    def psum(self, name, shape, dtype=F32):
        if not self._phase_mode:
            return self.es.enter_context(self.nc.psum_tensor(name, shape, dtype))
        key = (tuple(shape), dtype)
        idx = self._psum_idx.get(key, 0)
        self._psum_idx[key] = idx + 1
        pool = self._psum_pool.setdefault(key, [])
        if idx < len(pool):
            return pool[idx]
        t = self.es.enter_context(self.nc.psum_tensor(
            "ps_%d_%s_%d" % (shape[1], str(dtype), idx), shape, dtype))
        pool.append(t)
        return t

    def snapshot(self):
        return [(s, v) for s, v in self.cnt.items() if v > 0]

    def fence(self):
        deps = self.snapshot()
        for eng in ENGS:
            self.on(eng, lambda e: e.nop(), deps=deps, inc=0)

    def cc(self, kind, op, ins, outs, groups, deps, name):
        return self.on("pool", lambda g: g.collective_compute(
            kind, op, replica_groups=groups, ins=ins, outs=outs),
            deps=deps, sem=name, inc=1)

    def on(self, eng, fn, deps=(), sem=None, inc=1):
        sname = self.prefix + (sem or eng)
        waits = []
        for (s, v) in tmax(*deps) if deps else []:
            if s in self.dma_sems and v != self.cnt[s]:
                raise RuntimeError(
                    f"unsafe DMA wait on {s}: want {v}, issued {self.cnt[s]} "
                    f"(DMA completions are unordered; use a dedicated sem per slot)")
            if v > self.waited[eng][s]:
                waits.append((s, v))
                self.waited[eng][s] = v
        tok = None
        if inc:
            self.cnt[sname] += inc
            tok = (sname, self.cnt[sname])
        self.ops[eng].append((waits, fn, sname if inc else None, inc))
        self.n_inst += 1 + len(waits)
        return tok

    def dma(self, eng, out, in_, deps=(), sem=None, **kw):
        sname = sem or ("dma_" + eng)
        self.dma_sems.add(self.prefix + sname)
        return self.on(eng, lambda e: e.dma_start(out=out, in_=in_, **kw),
                       deps=deps, sem=sname, inc=16)

    def finalize(self):
        for s in list(self.cnt.keys()):
            self.sem(s)
        block = self.es.enter_context(self.nc.Block())
        for eng in ENGS:
            ops = self.ops[eng]
            if not ops:
                continue
            def make(eng_ops):
                def body(e):
                    for waits, fn, sname, inc in eng_ops:
                        for (s, v) in waits:
                            e.wait_ge(self.sems[s], v)
                        inst = fn(e)
                        if sname is not None:
                            inst.then_inc(self.sems[sname], inc)
                return body
            getattr(block, ENG_ATTR[eng])(make(ops))

    def close(self):
        self.es.close()









def host_tile_T(W, oc_size=128):
    O, Din = W.shape
    t = W.reshape(O // 128, 128, Din // 128, 128)
    t = t.transpose(0, 3, 2, 1)
    return np.ascontiguousarray(t.reshape(O // 128, 128, Din))


def build_ffn(p, io, T=2048, TT=512):
    nc = p.nc
    D = 2048
    INTER = 8192
    NB = TT // 128
    NTT = T // TT
    NC = D // 128
    NF = INTER // 128
    RT2 = math.sqrt(2.0)

    xr_bufs = [p.sbuf("xr", [128, NB * D], BF16),
               p.sbuf("xrB", [128, NB * D], BF16)]
    xn = p.sbuf("xn", [128, D], BF16)
    vA = p.sbuf("vA", [128, NC * TT], BF16)
    vT = p.sbuf("vT", [128, (NC // 2) * TT], BF16)
    y = p.sbuf("y", [128, NF * TT], BF16)
    yT = p.sbuf("yT", [128, (NF // 2) * TT], BF16)
    dn = p.sbuf("dn", [128, NB * D], BF16)
    pr = yT
    wub = p.sbuf("wub", [128, 2 * D], BF16)
    wgb = p.sbuf("wgb", [128, 2 * D], BF16)
    wdb = p.sbuf("wdb", [128, 2 * (INTER // 2)], BF16)
    dnf = p.sbuf("dnf", [128, 2 * TT], BF16)
    h128 = p.sbuf("h128", [128, 128], BF16)
    ident = p.sbuf("ident", [128, 128], BF16)
    suug = p.sbuf("suug", [128, NC], F32)
    sudp = p.sbuf("sudp", [128, NF], F32)
    wscu = p.sbuf("wscu", [128, NF], F32)
    wscg = p.sbuf("wscg", [128, NF], F32)
    upmul = wscu
    wscd = p.sbuf("wscd", [128, NC], F32)
    bn6 = p.sbuf("bn6", [128, 24], F32)
    bn2 = p.sbuf("bn2", [128, 2], F32)
    rstd = p.sbuf("rstd", [128, 1], F32)
    epsc = p.sbuf("epsc", [128, 1], F32)

    psA = [p.psum("psA0", [128, 512]), p.psum("psA1", [128, 512])]
    psU = [p.psum("psU0", [128, 512]), p.psum("psU1", [128, 512])]
    psG = [p.psum("psG0", [128, 512]), p.psum("psG1", [128, 512])]
    psT = [p.psum("psT0", [128, 1024], BF16), p.psum("psT1", [128, 1024], BF16)]

    t_h = p.dma("pool", h128[:], io["h128"], sem="ldc")
    t_id = p.dma("pool", ident[:], io["ident"], sem="ldc")
    p.dma("pool", suug[:], bass.AP(io["su_ug"].tensor, 0, [[1, 128], [128, NC]]),
          allow_slow_non_contiguous=True, sem="ldc")
    p.dma("pool", sudp[:], bass.AP(io["su_dp"].tensor, 0, [[1, 128], [128, NF]]),
          allow_slow_non_contiguous=True, sem="ldc")
    p.dma("pool", wscu[:], bass.AP(io["wsc_ug"].tensor, 0, [[1, 128], [128, NF]]),
          allow_slow_non_contiguous=True, sem="ldc")
    p.dma("pool", wscg[:], bass.AP(io["wsc_ug"].tensor, INTER, [[1, 128], [128, NF]]),
          allow_slow_non_contiguous=True, sem="ldc")
    t_wd = p.dma("pool", wscd[:], bass.AP(io["wsc_dp"].tensor, 0, [[1, 128], [128, NC]]),
                 allow_slow_non_contiguous=True, sem="ldc")
    t_su = t_sd = t_wu = t_wg = t_h = t_id = t_wd
    t_eps = p.on("pool", lambda g: g.memset(epsc[:], 1.0e-5))
    t_su2 = p.on("dve", lambda v: v.tensor_scalar(suug[:], suug[:], 2.0 ** -11, None, ALU.mult), deps=[t_su])
    t_wg2 = p.on("dve", lambda v: v.tensor_scalar(wscg[:], wscg[:], 64.0 / RT2, None, ALU.mult), deps=[t_wg])
    t_um1 = p.on("dve", lambda v: v.tensor_tensor(upmul[:], wscu[:], sudp[:], ALU.mult), deps=[t_wu, t_sd])
    t_um2 = p.on("dve", lambda v: v.tensor_scalar(upmul[:], upmul[:], 2.0 ** -6 / RT2, None, ALU.mult), deps=[t_um1])
    t_wd2 = p.on("dve", lambda v: v.tensor_scalar(wscd[:], wscd[:], 64.0 / RT2, None, ALU.mult), deps=[t_wd])

    out_toks = []
    xr_free = None
    wu_free = [None, None]
    wg_free = [None, None]
    wd_free = [None, None]
    psA_free = [None, None]
    psU_free = [None, None]
    psG_free = [None, None]
    psT_free = [None, None]
    sg_free = [None, None]
    dnf_free = [None, None]
    vA_free = y_free = dn_free = None
    yT_war = None
    xn_free = None
    xr_free2 = [None, None]

    for j in range(NTT):
        row0 = j * TT
        xr = xr_bufs[j % 2]
        xr_free = xr_free2[j % 2]

        def row_ap(h, r0=row0):
            return bass.AP(h.tensor, h.offset + r0 * D, [[D, 128], [128 * D, NB], [1, D]])
        xr_ap = bass.AP(xr, 0, [[NB * D, 128], [D, NB], [1, D]])
        pr_ap = bass.AP(pr, 0, [[(NF // 2) * TT, 128], [D, NB], [1, D]])
        pr_rd = bass.AP(pr, 0, [[(NF // 2) * TT, 128], [1, NB * D]])

        t_x = p.dma("pool", xr_ap, row_ap(io["x"]), deps=[xr_free], sem="lda")
        t_p1 = p.dma("pool", pr_ap, row_ap(io["pa"]), deps=[yT_war], sem="lda")
        t_a1 = p.on("dve", lambda v, xr=xr: v.tensor_tensor(xr[:], xr[:], pr_rd, ALU.add), deps=[t_x, t_p1])
        t_a2 = t_a1

        tp_evac = []
        for b in range(NB):
            xrb = xr[:, b * D:(b + 1) * D]
            t_bs = None
            for s4 in range(4):
                t_bs = p.on("dve", lambda v, xrb=xrb, s4=s4: v.bn_stats(
                    bn6[:, s4 * 6:(s4 + 1) * 6], bass.AP(xrb.tensor, xrb.offset + s4 * 512, [[xrb.ap[0][0], 128], [1, 512]])),
                    deps=[t_a2, xn_free])
            t_ba = p.on("dve", lambda v: v.bn_aggr(bn2[:], bn6[:]), deps=[t_bs])
            t_sq = p.on("act", lambda s: s.activation(rstd[:], bn2[:, 1:2], AF.Sqrt, bias=epsc[:]), deps=[t_ba, t_eps])
            t_rs = p.on("dve", lambda v: v.reciprocal(rstd[:], rstd[:]), deps=[t_sq])
            t_xn = p.on("dve", lambda v, xrb=xrb: v.tensor_scalar(
                xn[:], xrb, bn2[:, 0:1], rstd[:], ALU.subtract, ALU.mult),
                deps=[t_rs, xn_free])
            tps = []
            for c in range(NC):
                par = (b * NC + c) % 2
                t_tp = p.on("pe", lambda t, c=c, par=par: t.transpose(
                    psT[par][:, 0:128], xn[:, c * 128:(c + 1) * 128], ident[:]),
                    deps=[t_xn, t_id, psT_free[par]])
                t_ev = p.on("dve", lambda v, c=c, b=b, par=par: v.tensor_scalar(
                    vA[:, c * TT + b * 128: c * TT + (b + 1) * 128],
                    psT[par][:, 0:128], suug[:, c:c + 1], None, ALU.mult),
                    deps=[t_tp, t_su2, vA_free])
                psT_free[par] = t_ev
                tps.append(t_tp)
                tp_evac.append(t_ev)
            xn_free = tmax(*tps)

        tA = tmax(*tp_evac)
        stA = []
        for c in range(NC):
            par = c % 2
            t_mm = p.on("pe", lambda t, c=c, par=par: t.matmul(
                psA[par][:], h128[:], vA[:, c * TT:(c + 1) * TT], start=True, stop=True),
                deps=[tA, t_h, psA_free[par]])
            t_ev = p.on("act", lambda s, c=c, par=par: s.copy(
                vA[:, c * TT:(c + 1) * TT], psA[par][:]), deps=[t_mm])
            psA_free[par] = t_ev
            stA.append(t_ev)

        tB = tmax(*stA)
        for h in (1, 2, 4, 8):
            nblk = NC // (2 * h)
            HT = TT // 2
            ev = bass.AP(vA, 0, [[NC * TT, 128], [2 * h * TT, nblk], [TT, h], [1, TT]])
            od = bass.AP(vA, h * TT, [[NC * TT, 128], [2 * h * TT, nblk], [TT, h], [1, TT]])
            tp_ = bass.AP(vT, 0, [[(NC // 2) * TT, 128], [h * TT, nblk], [TT, h], [1, TT]])
            ev0 = bass.AP(vA, 0, [[NC * TT, 128], [2 * h * TT, nblk], [TT, h], [1, HT]])
            ev1 = bass.AP(vA, HT, [[NC * TT, 128], [2 * h * TT, nblk], [TT, h], [1, HT]])
            tp0 = bass.AP(vT, 0, [[(NC // 2) * TT, 128], [h * TT, nblk], [TT, h], [1, HT]])
            tp1 = bass.AP(vT, HT, [[(NC // 2) * TT, 128], [h * TT, nblk], [TT, h], [1, HT]])
            t_c0 = p.on("act", lambda s, ev0=ev0, tp0=tp0: s.copy(tp0, ev0), deps=[tB])
            t_c1 = p.on("pool", lambda g, ev1=ev1, tp1=tp1: g.tensor_copy(tp1, ev1), deps=[tB])
            t_e = p.on("dve", lambda v, ev=ev, od=od, tp_=tp_: v.tensor_tensor(ev, tp_, od, ALU.add), deps=[t_c0, t_c1])
            t_o = p.on("dve", lambda v, od=od, tp_=tp_: v.tensor_tensor(od, tp_, od, ALU.subtract), deps=[t_e])
            tB = tmax(t_e, t_o)
        vtok = tB

        y_toks = []
        for oc in range(NF):
            par = oc % 2
            wu_ap = bass.AP(io["wupT"].tensor, oc * 128 * D, [[D, 128], [1, D]])
            wg_ap = bass.AP(io["wgT"].tensor, oc * 128 * D, [[D, 128], [1, D]])
            wus = wub[:, par * D:(par + 1) * D]
            wgs = wgb[:, par * D:(par + 1) * D]
            t_lu = p.dma("pool", wus, wu_ap, deps=[wu_free[par]], sem="lwu%d" % par)
            t_lg = p.dma("pool", wgs, wg_ap, deps=[wg_free[par]], sem="lwg%d" % par)
            t_u = t_g = None
            for c in range(NC):
                t_u = p.on("pe", lambda t, c=c, par=par, wus=wus: t.matmul(
                    psU[par][:], wus[:, c * 128:(c + 1) * 128], vA[:, c * TT:(c + 1) * TT],
                    start=(c == 0), stop=(c == NC - 1)),
                    deps=[tmax(vtok, t_lu, psU_free[par])] if c == 0 else (),
                    inc=1 if c == NC - 1 else 0)
            for c in range(NC):
                t_g = p.on("pe", lambda t, c=c, par=par, wgs=wgs: t.matmul(
                    psG[par][:], wgs[:, c * 128:(c + 1) * 128], vA[:, c * TT:(c + 1) * TT],
                    start=(c == 0), stop=(c == NC - 1)),
                    deps=[tmax(t_lg, psG_free[par])] if c == 0 else (),
                    inc=1 if c == NC - 1 else 0)
            wu_free[par] = t_u
            wg_free[par] = t_g
            sgs = bass.AP(vT, par * TT, [[(NC // 2) * TT, 128], [1, TT]])
            t_sg = p.on("act", lambda s, oc=oc, par=par, sgs=sgs: s.activation(
                sgs, psG[par][:], AF.Silu, scale=wscg[:, oc:oc + 1]),
                deps=[t_g, t_wg2, sg_free[par]])
            psG_free[par] = t_sg
            t_y = p.on("dve", lambda v, oc=oc, par=par, sgs=sgs: v.scalar_tensor_tensor(
                y[:, oc * TT:(oc + 1) * TT], psU[par][:], upmul[:, oc:oc + 1], sgs,
                ALU.mult, ALU.mult), deps=[t_u, t_sg, t_um2, y_free])
            sg_free[par] = t_y
            psU_free[par] = t_y
            y_toks.append(t_y)
        vA_free = tmax(*y_toks)

        tA2 = tmax(*y_toks)
        stA2 = []
        for c in range(NF):
            par = c % 2
            t_mm = p.on("pe", lambda t, c=c, par=par: t.matmul(
                psA[par][:], h128[:], y[:, c * TT:(c + 1) * TT], start=True, stop=True),
                deps=[tA2, psA_free[par]])
            t_ev = p.on("act", lambda s, c=c, par=par: s.copy(
                y[:, c * TT:(c + 1) * TT], psA[par][:]), deps=[t_mm])
            psA_free[par] = t_ev
            stA2.append(t_ev)
        tB = tmax(*stA2)
        for h in (1, 2, 4, 8, 16, 32):
            nblk = NF // (2 * h)
            HT = TT // 2
            ev = bass.AP(y, 0, [[NF * TT, 128], [2 * h * TT, nblk], [TT, h], [1, TT]])
            od = bass.AP(y, h * TT, [[NF * TT, 128], [2 * h * TT, nblk], [TT, h], [1, TT]])
            tp_ = bass.AP(yT, 0, [[(NF // 2) * TT, 128], [h * TT, nblk], [TT, h], [1, TT]])
            ev0 = bass.AP(y, 0, [[NF * TT, 128], [2 * h * TT, nblk], [TT, h], [1, HT]])
            ev1 = bass.AP(y, HT, [[NF * TT, 128], [2 * h * TT, nblk], [TT, h], [1, HT]])
            tp0 = bass.AP(yT, 0, [[(NF // 2) * TT, 128], [h * TT, nblk], [TT, h], [1, HT]])
            tp1 = bass.AP(yT, HT, [[(NF // 2) * TT, 128], [h * TT, nblk], [TT, h], [1, HT]])
            t_c0 = p.on("act", lambda s, ev0=ev0, tp0=tp0: s.copy(tp0, ev0), deps=[tB])
            t_c1 = p.on("pool", lambda g, ev1=ev1, tp1=tp1: g.tensor_copy(tp1, ev1), deps=[tB])
            t_e = p.on("dve", lambda v, ev=ev, od=od, tp_=tp_: v.tensor_tensor(ev, tp_, od, ALU.add), deps=[t_c0, t_c1])
            t_o = p.on("dve", lambda v, od=od, tp_=tp_: v.tensor_tensor(od, tp_, od, ALU.subtract), deps=[t_e])
            tB = tmax(t_e, t_o)
        wtok = tB

        dn_toks = []
        down_mm_toks = []
        for oc in range(NC):
            par = oc % 2
            t_d = None
            for fh in range(2):
                parw = (oc * 2 + fh) % 2
                wds = wdb[:, parw * (INTER // 2):(parw + 1) * (INTER // 2)]
                wd_ap = bass.AP(io["wdT"].tensor, oc * 128 * INTER + fh * (INTER // 2),
                                [[INTER, 128], [1, INTER // 2]])
                t_ld = p.dma("pool", wds, wd_ap, deps=[wd_free[parw]], sem="lwd%d" % parw)
                first_dep = tmax(wtok, t_ld, psU_free[par]) if fh == 0 else tmax(t_ld)
                for fc in range(NF // 2):
                    f = fh * (NF // 2) + fc
                    t_d = p.on("pe", lambda t, fc=fc, f=f, par=par, wds=wds: t.matmul(
                        psU[par][:], wds[:, fc * 128:(fc + 1) * 128], y[:, f * TT:(f + 1) * TT],
                        start=(f == 0), stop=(f == NF - 1)),
                        deps=[first_dep] if fc == 0 else (),
                        inc=1 if fc == NF // 2 - 1 else 0)
                wd_free[parw] = t_d if t_d is not None else wd_free[parw]
            down_mm_toks.append(t_d)
            dns = dnf[:, par * TT:(par + 1) * TT]
            t_ev = p.on("act", lambda s, oc=oc, par=par, dns=dns: s.activation(
                dns, psU[par][:], AF.Copy, scale=wscd[:, oc:oc + 1]),
                deps=[t_d, t_wd2, dnf_free[par]])
            psU_free[par] = t_ev
            for b in range(NB):
                tpar = (oc * NB + b) % 2
                t_tp = p.on("pe", lambda t, b=b, par=par, tpar=tpar, dns=dns: t.transpose(
                    psT[tpar][:, 0:128], dns[:, b * 128:(b + 1) * 128], ident[:]),
                    deps=[t_ev, psT_free[tpar]])
                t_cp = p.on("dve", lambda v, b=b, oc=oc, tpar=tpar: v.tensor_copy(
                    dn[:, b * D + oc * 128: b * D + (oc + 1) * 128], psT[tpar][:, 0:128]),
                    deps=[t_tp, t_a2])
                psT_free[tpar] = t_cp
                dn_toks.append(t_cp)
            dnf_free[par] = dn_toks[-1]
        y_free = tmax(*down_mm_toks)

        t_fin = p.on("dve", lambda v, xr=xr: v.tensor_tensor(xr[:], xr[:], dn[:], ALU.add),
                     deps=[tmax(*dn_toks), t_a2])
        dn_free = t_fin
        yT_war = wtok
        t_st = p.dma("pool", row_ap(io["out"]), xr_ap, deps=[t_fin], sem="sto%d" % (j % 2))
        xr_free2[j % 2] = t_st
        out_toks.append(t_st)

    p.on("sp", lambda e: e.nop(), deps=out_toks, inc=0)
    return out_toks






D = 2048
NQH = 16
NG = 4
HD = 64
QO = 8
KO = 2
VO = 256
VOP = 260
NC = 16


def head_oc_base(h):
    g, j = h // 4, h % 4
    return (g // 2) * 4 + j, (g % 2) * 64


def build_attn0(p, io, S=4096):
    NBLK = S // 512
    xr = p.sbuf("xr", [128, 4 * D], BF16)
    xn = p.sbuf("xn", [128, D], BF16)
    xf = p.sbuf("xf", [128, NC * 512], BF16)
    wq = p.sbuf("wq", [128, QO * D], BF16)
    wk = p.sbuf("wk", [128, KO * D], BF16)
    wv = p.sbuf("wv", [128, NC * VO], BF16)
    wo = p.sbuf("wo", [128, 16 * 1024], BF16)
    qf = p.sbuf("qf", [128, QO * 512], BF16)
    kf = p.sbuf("kf", [128, KO * S], BF16)
    vt = p.sbuf("vt", [128, (S // 128) * VOP], BF16)
    qsq = p.sbuf("qsq", [128, QO * 512], BF16)
    ksq = p.sbuf("ksq", [128, KO * 512], BF16)
    qm = p.sbuf("qm", [1, NQH], F32)
    km = p.sbuf("km", [1, NG], F32)
    kmx = p.sbuf("kmx", [1, NG], F32)
    kmrep = p.sbuf("kmrep", [1, NQH], F32)
    prod = p.sbuf("prod", [1, NQH], F32)
    brow = p.sbuf("brow", [1, NQH], F32)
    bcol = p.sbuf("bcol", [128, NQH], F32)
    ones_r = p.sbuf("ones_r", [1, 128], BF16)
    onecol = p.sbuf("onecol", [128, 1], BF16)
    pbuf = p.sbuf("pbuf", [128, 4 * 512], BF16)
    lrow = p.sbuf("lrow", [1, 2 * 512], F32)
    lsc = p.sbuf("lsc", [128, 512], F32)
    ctxs = p.sbuf("ctxs", [128, 8 * 512], BF16)
    wof = p.sbuf("wof", [128, 2 * 512], BF16)
    part = p.sbuf("part", [128, 4 * D], BF16)
    ident = p.sbuf("ident", [128, 128], BF16)
    ones32r = p.sbuf("ones32r", [1, 128], F32)
    dmask = p.sbuf("dmask", [128, 128], BF16)
    bn6 = p.sbuf("bn6", [128, 24], F32)
    bn2 = p.sbuf("bn2", [128, 2], F32)
    rstd = p.sbuf("rstd", [128, 1], F32)
    epsc = p.sbuf("epsc", [128, 1], F32)

    psP = [p.psum("psP0", [128, 512]), p.psum("psP1", [128, 512])]
    psS = [p.psum("psS0", [128, 512]), p.psum("psS1", [128, 512])]
    psC = [p.psum("psC0", [128, 512]), p.psum("psC1", [128, 512])]

    psT = [p.psum("psT0", [128, 1024], BF16), p.psum("psT1", [128, 1024], BF16)]
    psN = psP[0]

    t_id = p.dma("pool", ident[:], io["ident"], sem="ldc")
    t_dm = p.dma("pool", dmask[:], io["dmaskt"], sem="ldc")
    p.dma("pool", wv[:], bass.AP(io["wvT"].tensor, 0, [[VO, 128], [128 * VO, NC], [1, VO]]), sem="ldc")
    p.dma("pool", wq[:], bass.AP(io["wq_t"].tensor, 0, [[D, 128], [128 * D, QO], [1, D]]), sem="ldc")
    p.dma("pool", wk[:], bass.AP(io["wk_t"].tensor, 0, [[D, 128], [128 * D, KO], [1, D]]), sem="ldc")
    t_wol = p.dma("pool", wo[:], bass.AP(io["wo_t"].tensor, 0, [[1024, 128], [128 * 1024, 16], [1, 1024]]), sem="ldc")
    tc0 = p.on("pool", lambda g: g.memset(ones_r[:], 1.0))
    p.on("pool", lambda g: g.memset(ones32r[:], 1.0))
    tc1 = p.on("pool", lambda g: g.memset(onecol[:], 1.0))
    tc2 = p.on("pool", lambda g: g.memset(km[:], 0.0))
    t_vt1 = p.on("pool", lambda g: g.memset(vt[:], 1.0))
    t_eps = p.on("pool", lambda g: g.memset(epsc[:], 1.0e-5))
    t_const = tmax(t_id, t_dm, t_wol, tc0, tc1, tc2, t_vt1, t_eps)

    out_toks = []
    xr_free = xn_free = xf_free = qf_free = None
    pbuf_free = [None, None, None, None]
    psP_free = [None, None]
    psS_free = [None, None]
    psC_free = [None, None]
    psT_free = [None, None]
    qsq_free = ksq_free = None
    qm_free = kmx_free = bcol_free = None
    lrow_free = lsc_free = None
    ctxs_free = wof_free = part_free = None

    for qb in range(NBLK):
        row0 = qb * 512

        def row_ap(h, r0=row0):
            return bass.AP(h.tensor, h.offset + r0 * D, [[D, 128], [128 * D, 4], [1, D]])
        xr_ap = bass.AP(xr, 0, [[4 * D, 128], [D, 4], [1, D]])

        t_x = p.dma("pool", xr_ap, row_ap(io["x"]), deps=[xr_free], sem="lda")

        tp_evac = []
        for b in range(4):
            xrb = xr[:, b * D:(b + 1) * D]
            t_bs = None
            for s4 in range(4):
                t_bs = p.on("dve", lambda v, xrb=xrb, s4=s4: v.bn_stats(
                    bn6[:, s4 * 6:(s4 + 1) * 6],
                    bass.AP(xrb.tensor, xrb.offset + s4 * 512, [[xrb.ap[0][0], 128], [1, 512]])),
                    deps=[t_x, xn_free])
            t_ba = p.on("dve", lambda v: v.bn_aggr(bn2[:], bn6[:]), deps=[t_bs])
            t_sq = p.on("act", lambda s: s.activation(rstd[:], bn2[:, 1:2], AF.Sqrt, bias=epsc[:]),
                        deps=[t_ba, t_const])
            t_rs = p.on("dve", lambda v: v.reciprocal(rstd[:], rstd[:]), deps=[t_sq])
            t_xn = p.on("dve", lambda v, xrb=xrb: v.tensor_scalar(
                xn[:], xrb, bn2[:, 0:1], rstd[:], ALU.subtract, ALU.mult),
                deps=[t_rs, xn_free])
            tps = []
            for c in range(NC):
                par = (b * NC + c) % 2
                t_tp = p.on("pe", lambda t, c=c, par=par: t.transpose(
                    psT[par][:, 0:128], xn[:, c * 128:(c + 1) * 128], ident[:]),
                    deps=[t_xn, t_const, psT_free[par]])
                t_ev = p.on("dve", lambda v, c=c, b=b, par=par: v.tensor_copy(
                    xf[:, c * 512 + b * 128: c * 512 + (b + 1) * 128], psT[par][:, 0:128]),
                    deps=[t_tp, xf_free])
                psT_free[par] = t_ev
                tps.append(t_tp)
                tp_evac.append(t_ev)
            xn_free = tmax(*tps)
        t_fm = tmax(*tp_evac)

        q_evs = []
        for oc in range(QO):
            par = oc % 2
            t_mm = None
            for c in range(NC):
                t_mm = p.on("pe", lambda t, oc=oc, c=c, par=par: t.matmul(
                    psP[par][:], wq[:, oc * D + c * 128: oc * D + (c + 1) * 128],
                    xf[:, c * 512:(c + 1) * 512], start=(c == 0), stop=(c == NC - 1)),
                    deps=[tmax(t_fm, t_const, psP_free[par])] if c == 0 else (),
                    inc=1 if c == NC - 1 else 0)
            t_ev = p.on("act", lambda s, oc=oc, par=par: s.copy(
                qf[:, oc * 512:(oc + 1) * 512], psP[par][:]), deps=[t_mm, qf_free])
            psP_free[par] = t_ev
            q_evs.append(t_ev)
        k_evs = []
        for oc in range(KO):
            par = oc % 2
            t_mm = None
            for c in range(NC):
                t_mm = p.on("pe", lambda t, oc=oc, c=c, par=par: t.matmul(
                    psP[par][:], wk[:, oc * D + c * 128: oc * D + (c + 1) * 128],
                    xf[:, c * 512:(c + 1) * 512], start=(c == 0), stop=(c == NC - 1)),
                    deps=[tmax(t_fm, psP_free[par])] if c == 0 else (),
                    inc=1 if c == NC - 1 else 0)
            t_ev = p.on("act", lambda s, oc=oc, par=par, qb=qb: s.copy(
                kf[:, oc * S + qb * 512: oc * S + (qb + 1) * 512], psP[par][:]), deps=[t_mm])
            psP_free[par] = t_ev
            k_evs.append(t_ev)
        v_evs = []
        for b in range(4):
            par = b % 2
            t_mm = None
            for c in range(NC):
                t_mm = p.on("pe", lambda t, b=b, c=c, par=par: t.matmul(
                    psP[par][:, 0:VO], xf[:, c * 512 + b * 128: c * 512 + (b + 1) * 128],
                    wv[:, c * VO:(c + 1) * VO], start=(c == 0), stop=(c == NC - 1)),
                    deps=[tmax(t_fm, psP_free[par])] if c == 0 else (),
                    inc=1 if c == NC - 1 else 0)
            sub = qb * 4 + b
            t_ev = p.on("act", lambda s, sub=sub, par=par: s.copy(
                bass.AP(vt, sub * VOP, [[(S // 128) * VOP, 128], [65, 4], [1, 64]]),
                bass.AP(psP[par], 0, [[512, 128], [64, 4], [1, 64]])),
                deps=[t_mm, t_const])
            psP_free[par] = t_ev
            v_evs.append(t_ev)
        t_kv = tmax(*k_evs, *v_evs)
        t_q = tmax(*q_evs)
        xr_free = t_fm

        t_qs2 = p.on("act", lambda s: s.activation(qsq[:], qf[:], AF.Square), deps=[t_q, qsq_free])
        t_ks2 = p.on("act", lambda s, qb=qb: s.activation(
            ksq[:], bass.AP(kf, qb * 512, [[KO * S, 128], [S, KO], [1, 512]]), AF.Square),
            deps=[tmax(*k_evs), ksq_free])
        nm_toks = []
        for h in range(NQH):
            oc, base = head_oc_base(h)
            t_nm = p.on("pe", lambda t, oc=oc, base=base: t.matmul(
                psN[0:1, 0:512], bass.AP(onecol, base, [[1, 64], [0, 1]]),
                bass.AP(qsq, base * QO * 512 + oc * 512, [[QO * 512, 64], [1, 512]]),
                start=True, stop=True),
                deps=[t_qs2, t_const, psP_free[0]])
            t_mx = p.on("dve", lambda v, h=h: v.tensor_reduce(
                qm[:, h:h + 1], psN[0:1, 0:512], AX.X, ALU.max), deps=[t_nm, qm_free])
            psP_free[0] = t_mx
            nm_toks.append(t_mx)
        for g in range(NG):
            oc, base = g // 2, (g % 2) * 64
            t_nm = p.on("pe", lambda t, oc=oc, base=base: t.matmul(
                psN[0:1, 0:512], bass.AP(onecol, base, [[1, 64], [0, 1]]),
                bass.AP(ksq, base * KO * 512 + oc * 512, [[KO * 512, 64], [1, 512]]),
                start=True, stop=True),
                deps=[t_ks2, psP_free[0]])
            t_mx = p.on("dve", lambda v, g=g: v.tensor_reduce(
                kmx[:, g:g + 1], psN[0:1, 0:512], AX.X, ALU.max), deps=[t_nm, kmx_free])
            t_km = p.on("dve", lambda v, g=g: v.tensor_tensor(
                km[:, g:g + 1], km[:, g:g + 1], kmx[:, g:g + 1], ALU.max), deps=[t_mx])
            psP_free[0] = t_mx
            nm_toks.append(t_km)
        t_nm_all = tmax(*nm_toks)
        qsq_free = t_nm_all
        ksq_free = t_nm_all
        kmx_free = t_nm_all
        rep_toks = []
        for g in range(NG):
            rep_toks.append(p.on("dve", lambda v, g=g: v.tensor_scalar(
                kmrep[:, 4 * g:4 * (g + 1)], ones_r[0:1, 0:4], km[:, g:g + 1], None, ALU.mult),
                deps=[t_nm_all]))
        t_pr = p.on("dve", lambda v: v.tensor_tensor(prod[:], qm[:], kmrep[:], ALU.mult),
                    deps=rep_toks)
        qm_free = t_pr
        t_br = p.on("act", lambda s: s.activation(brow[:], prod[:], AF.Sqrt), deps=[t_pr])
        t_bn = p.on("dve", lambda v: v.tensor_scalar(brow[:], brow[:], -0.125, None, ALU.mult), deps=[t_br])
        t_bc0 = p.on("pe", lambda t: t.matmul(psN[:, 0:NQH], ones32r[0:1, :], brow[:],
                                              start=True, stop=True), deps=[t_bn, psP_free[0]])
        t_bc = p.on("dve", lambda v: v.tensor_copy(bcol[:], psN[:, 0:NQH]), deps=[t_bc0, bcol_free])
        psP_free[0] = t_bc

        pair_evac = []
        for pp in range(8):
            if True:
                t_ctx_last = {0: None, 1: None}
                for kc in range(qb + 1):
                    diag = (kc == qb)
                    for hloc in (0, 1):
                        h = (2 * (pp // 4)) * 4 + (pp % 4) + hloc * 4
                        g = h // 4
                        gbase = (g % 2) * 64
                        oc, base = head_oc_base(h)
                        exp_toks = [None] * 4
                        for ks in range(4):
                            q0 = ks * 128 if diag else 0
                            width = 512 - q0
                            spar = ks % 2
                            sps = psS[spar]
                            t_s = p.on("pe", lambda t, oc=oc, base=base, ks=ks, kc=kc, q0=q0, width=width, sps=sps, gbase=gbase, g=g: t.matmul(
                                sps[:, 0:width],
                                bass.AP(kf, gbase * KO * S + (g // 2) * S + kc * 512 + ks * 128, [[KO * S, 64], [1, 128]]),
                                bass.AP(qf, base * QO * 512 + oc * 512 + q0, [[QO * 512, 64], [1, width]]),
                                start=True, stop=True),
                                deps=[tmax(t_q, t_kv, psS_free[spar])])
                            if diag:
                                t_s = p.on("dve", lambda v, sps=sps: v.tensor_tensor(
                                    sps[:, 0:128], sps[:, 0:128], dmask[:], ALU.add),
                                    deps=[t_s, t_const])
                            t_e = p.on("act", lambda s, ks=ks, q0=q0, width=width, sps=sps, h=h: s.activation(
                                pbuf[:, ks * 512 + q0: (ks + 1) * 512], sps[:, 0:width], AF.Exp,
                                bias=bcol[:, h:h + 1], scale=0.125),
                                deps=[t_s, t_bc, pbuf_free[ks]])
                            psS_free[spar] = t_e
                            if diag and ks > 0:
                                t_m = p.on("dve", lambda v, ks=ks, q0=q0: v.memset(
                                    pbuf[:, ks * 512: ks * 512 + q0], 0.0),
                                    deps=[pbuf_free[ks]])
                                t_e = tmax(t_e, t_m)
                            exp_toks[ks] = t_e
                        for ks in range(4):
                            sub = kc * 4 + ks
                            first = (kc == 0 and ks == 0)
                            last = (kc == qb and ks == 3)
                            t_cm = p.on("pe", lambda t, sub=sub, g=g, hloc=hloc, ks=ks, first=first, last=last: t.matmul(
                                psC[hloc][0:65, :],
                                vt[:, sub * VOP + g * 65: sub * VOP + g * 65 + 65],
                                pbuf[:, ks * 512:(ks + 1) * 512], start=first, stop=last),
                                deps=[tmax(exp_toks[ks], t_kv, psC_free[hloc] if first else None)])
                            pbuf_free[ks] = t_cm
                            t_ctx_last[hloc] = t_cm
                cc = pp
                bc_toks = []
                rc_toks = []
                for hloc in (0, 1):
                    t_rc = p.on("dve", lambda v, hloc=hloc: v.reciprocal(
                        lrow[:, hloc * 512:(hloc + 1) * 512], psC[hloc][64:65, 0:512]),
                        deps=[t_ctx_last[hloc], lrow_free])
                    rc_toks.append(t_rc)
                for hloc in (0, 1):
                    t_b1 = p.on("pe", lambda t, hloc=hloc: t.matmul(
                        psN[hloc * 64:(hloc + 1) * 64, :], ones32r[0:1, 0:64],
                        lrow[:, hloc * 512:(hloc + 1) * 512],
                        start=True, stop=True, tile_position=(0, hloc * 64)),
                        deps=[rc_toks[hloc], psP_free[0], t_const])
                    psP_free[0] = t_b1
                    bc_toks.append(t_b1)
                lrow_free = tmax(*bc_toks)
                t_lc = p.on("dve", lambda v: v.tensor_copy(lsc[:], psN[:]),
                            deps=[tmax(*bc_toks), lsc_free])
                psP_free[0] = t_lc
                t_ev0 = p.on("dve", lambda v, cc=cc: v.tensor_tensor(
                    bass.AP(ctxs, cc * 512, [[8 * 512, 64], [1, 512]]),
                    psC[0][0:64, :], lsc[0:64, :], ALU.mult),
                    deps=[t_lc, t_ctx_last[0], ctxs_free])
                t_ev1 = p.on("dve", lambda v, cc=cc: v.tensor_tensor(
                    bass.AP(ctxs, 64 * 8 * 512 + cc * 512, [[8 * 512, 64], [1, 512]]),
                    psC[1][0:64, :], lsc[64:128, :], ALU.mult),
                    deps=[t_lc, t_ctx_last[1], ctxs_free])
                t_ev = tmax(t_ev0, t_ev1)
                lsc_free = t_ev
                psC_free = [t_ev0, t_ev1]
                pair_evac.append(t_ev)
        t_ctx_all = tmax(*pair_evac)
        qf_free = t_ctx_all
        bcol_free = t_ctx_all
        xf_free = tmax(t_q, t_kv)

        st_deps = []
        for oc in range(16):
            par = oc % 2
            t_mm = None
            for cc in range(8):
                t_mm = p.on("pe", lambda t, oc=oc, cc=cc, par=par: t.matmul(
                    psP[par][:], wo[:, oc * 1024 + cc * 128: oc * 1024 + (cc + 1) * 128],
                    ctxs[:, cc * 512:(cc + 1) * 512], start=(cc == 0), stop=(cc == 7)),
                    deps=[tmax(t_ctx_all, psP_free[par])] if cc == 0 else (),
                    inc=1 if cc == 7 else 0)
            wos = wof[:, par * 512:(par + 1) * 512]
            t_ev = p.on("act", lambda s, par=par, wos=wos: s.copy(wos, psP[par][:]),
                        deps=[t_mm, wof_free])
            psP_free[par] = t_ev
            for b in range(4):
                tpar = (oc * 4 + b) % 2
                t_tp = p.on("pe", lambda t, b=b, tpar=tpar, wos=wos: t.transpose(
                    psT[tpar][:, 0:128], wos[:, b * 128:(b + 1) * 128], ident[:]),
                    deps=[t_ev, psT_free[tpar]])
                t_cp = p.on("dve", lambda v, b=b, oc=oc, tpar=tpar: v.tensor_copy(
                    part[:, b * D + oc * 128: b * D + (oc + 1) * 128], psT[tpar][:, 0:128]),
                    deps=[t_tp, part_free])
                psT_free[tpar] = t_cp
                st_deps.append(t_cp)
            wof_free = st_deps[-1]
        ctxs_free = tmax(*[t for t in st_deps])
        t_st = p.dma("pool", row_ap(io["out"]),
                     bass.AP(part, 0, [[4 * D, 128], [D, 4], [1, D]]),
                     deps=[tmax(*st_deps)], sem="sto")
        part_free = t_st
        out_toks.append(t_st)

    p.on("sp", lambda e: e.nop(), deps=out_toks, inc=0)
    return out_toks






D = 2048
NH = 8
HD = 128
NC = 16
MO = NH * HD
MOP = NH * 129
SCALE = float(HD) ** -0.5


def build_attn1(p, io, S=4096):
    NBLK = S // 512

    xr = p.sbuf("xr", [128, 4 * D], BF16)
    xn = p.sbuf("xn", [128, D], BF16)
    xf = p.sbuf("xf", [128, NC * 512], BF16)
    wts = p.sbuf("wts", [128, 2 * NC * 512], BF16)
    qtm = p.sbuf("qtm", [128, 4 * MO], BF16)
    ktm = p.sbuf("ktm", [128, 4 * MO], BF16)
    rtm = p.sbuf("rtm", [128, 4 * 512], BF16)
    qf = p.sbuf("qf", [128, NH * 512], BF16)
    kf = p.sbuf("kf", [128, NH * 1024], BF16)
    vt = p.sbuf("vt", [128, 8 * MOP], BF16)
    cosr = p.sbuf("cosr", [128, 4 * 512], BF16)
    sinr = p.sbuf("sinr", [128, 4 * 512], BF16)
    pbuf = p.sbuf("pbuf", [128, 3 * 512], BF16)
    lrow = p.sbuf("lrow", [1, 512], F32)
    lsc = p.sbuf("lsc", [128, 512], F32)
    ctxs = p.sbuf("ctxs", [128, NH * 512], BF16)
    wof = p.sbuf("wof", [128, 2 * 512], BF16)
    part = p.sbuf("part", [128, 4 * D], BF16)
    ident = p.sbuf("ident", [128, 128], BF16)
    id32 = p.sbuf("id32", [128, 128], F32)
    ones32r = p.sbuf("ones32r", [1, 128], F32)
    dmask = p.sbuf("dmask", [128, 128], BF16)
    amask = p.sbuf("amask", [128, 128], BF16)
    bn6 = p.sbuf("bn6", [128, 24], F32)
    bn2 = p.sbuf("bn2", [128, 2], F32)
    rstd = p.sbuf("rstd", [128, 1], F32)
    epsc = p.sbuf("epsc", [128, 1], F32)

    psP = [p.psum("psP0", [128, 512]), p.psum("psP1", [128, 512])]
    psS = [p.psum("psS0", [128, 512]), p.psum("psS1", [128, 512])]
    psC = [p.psum("psC0", [128, 512]), p.psum("psC1", [128, 512])]
    psN = psP[0]
    psT = [p.psum("psT0", [128, 1024], BF16), p.psum("psT1", [128, 1024], BF16)]

    p.dma("pool", ident[:], io["ident"], sem="ldc")
    p.dma("pool", id32[:], io["ident"], sem="ldc")
    p.dma("pool", dmask[:], io["dmaskt"], sem="ldc")
    t_am = p.dma("pool", amask[:], io["amaskt"], sem="ldc")
    t_on = p.on("pool", lambda g: g.memset(ones32r[:], 1.0))
    t_vt1 = p.on("pool", lambda g: g.memset(vt[:], 1.0))
    t_eps = p.on("pool", lambda g: g.memset(epsc[:], 1.0e-5))
    t_const = tmax(t_am, t_on, t_vt1, t_eps)

    out_toks = []
    xr_free = xn_free = xf_free = None
    wts_free = [None, None]
    psP_free = [None, None]
    psS_free = [None, None]
    psC_free = [None, None]
    psT_free = [None, None]
    qtm_free = ktm_free = rtm_free = None
    qf_free = None
    cos_free = None
    pbuf_free = [None, None, None]
    lrow_free = lsc_free = None
    ctxs_free = wof_free = part_free = None

    for qb in range(NBLK):
        row0 = qb * 512
        slot = qb % 2
        pslot = (qb - 1) % 2

        def row_ap(h, r0=row0):
            return bass.AP(h.tensor, h.offset + r0 * D, [[D, 128], [128 * D, 4], [1, D]])
        xr_ap = bass.AP(xr, 0, [[4 * D, 128], [D, 4], [1, D]])

        t_x = p.dma("pool", xr_ap, row_ap(io["x"]), deps=[xr_free], sem="lda")
        t_cos = p.dma("pool", bass.AP(cosr, 0, [[4 * 512, 128], [512, 4], [1, 512]]),
                      bass.AP(io["cos_rep"].tensor, row0 * 512, [[512, 128], [128 * 512, 4], [1, 512]]),
                      deps=[cos_free], sem="ldr")
        t_sin = p.dma("pool", bass.AP(sinr, 0, [[4 * 512, 128], [512, 4], [1, 512]]),
                      bass.AP(io["sin_rep"].tensor, row0 * 512, [[512, 128], [128 * 512, 4], [1, 512]]),
                      deps=[cos_free], sem="ldr2")

        tp_evac = []
        for b in range(4):
            xrb = xr[:, b * D:(b + 1) * D]
            t_bs = None
            for s4 in range(4):
                t_bs = p.on("dve", lambda v, xrb=xrb, s4=s4: v.bn_stats(
                    bn6[:, s4 * 6:(s4 + 1) * 6],
                    bass.AP(xrb.tensor, xrb.offset + s4 * 512, [[xrb.ap[0][0], 128], [1, 512]])),
                    deps=[t_x, xn_free])
            t_ba = p.on("dve", lambda v: v.bn_aggr(bn2[:], bn6[:]), deps=[t_bs])
            t_sq = p.on("act", lambda s: s.activation(rstd[:], bn2[:, 1:2], AF.Sqrt, bias=epsc[:]),
                        deps=[t_ba, t_const])
            t_rs = p.on("dve", lambda v: v.reciprocal(rstd[:], rstd[:]), deps=[t_sq])
            t_xn = p.on("dve", lambda v, xrb=xrb: v.tensor_scalar(
                xn[:], xrb, bn2[:, 0:1], rstd[:], ALU.subtract, ALU.mult),
                deps=[t_rs, xn_free])
            tps = []
            for c in range(NC):
                par = (b * NC + c) % 2
                t_tp = p.on("pe", lambda t, c=c, par=par: t.transpose(
                    psT[par][:, 0:128], xn[:, c * 128:(c + 1) * 128], ident[:]),
                    deps=[t_xn, t_const, psT_free[par]])
                t_ev = p.on("dve", lambda v, c=c, b=b, par=par: v.tensor_copy(
                    xf[:, c * 512 + b * 128: c * 512 + (b + 1) * 128], psT[par][:, 0:128]),
                    deps=[t_tp, xf_free])
                psT_free[par] = t_ev
                tps.append(t_tp)
                tp_evac.append(t_ev)
            xn_free = tmax(*tps)
        t_fm = tmax(*tp_evac)
        xr_free = t_fm

        def proj_tm(dst, wio, dst_free, tag):
            evs = []
            for osl in range(2):
                par = osl % 2
                ws = wts[:, par * NC * 512:(par + 1) * NC * 512]
                t_lw = p.dma("pool", ws,
                             bass.AP(wio.tensor, osl * 512, [[MO, 128], [128 * MO, NC], [1, 512]]),
                             deps=[wts_free[par]], sem="lw%d%s" % (par, tag))
                for sub in range(4):
                    spar = sub % 2
                    t_mm = None
                    for c in range(NC):
                        t_mm = p.on("pe", lambda t, c=c, sub=sub, spar=spar, ws=ws: t.matmul(
                            psP[spar][:], xf[:, c * 512 + sub * 128: c * 512 + (sub + 1) * 128],
                            ws[:, c * 512:(c + 1) * 512], start=(c == 0), stop=(c == NC - 1)),
                            deps=[tmax(t_fm, t_lw, psP_free[spar])] if c == 0 else (),
                            inc=1 if c == NC - 1 else 0)
                    t_ev = p.on("dve", lambda v, sub=sub, osl=osl, spar=spar, dst=dst: v.tensor_copy(
                        dst[:, sub * MO + osl * 512: sub * MO + (osl + 1) * 512], psP[spar][:]),
                        deps=[t_mm, dst_free])
                    psP_free[spar] = t_ev
                    evs.append(t_ev)
                wts_free[par] = evs[-1]
            return tmax(*evs)

        t_q = proj_tm(qtm, io["wqT"], qtm_free, "q")
        t_k = proj_tm(ktm, io["wkT"], ktm_free, "k")
        v_evs = []
        for osl in range(2):
            par = osl % 2
            ws = wts[:, par * NC * 512:(par + 1) * NC * 512]
            t_lw = p.dma("pool", ws,
                         bass.AP(io["wvT"].tensor, osl * 512, [[MO, 128], [128 * MO, NC], [1, 512]]),
                         deps=[wts_free[par]], sem="lw%dv" % par)
            for sub in range(4):
                spar = sub % 2
                t_mm = None
                for c in range(NC):
                    t_mm = p.on("pe", lambda t, c=c, sub=sub, spar=spar, ws=ws: t.matmul(
                        psP[spar][:], xf[:, c * 512 + sub * 128: c * 512 + (sub + 1) * 128],
                        ws[:, c * 512:(c + 1) * 512], start=(c == 0), stop=(c == NC - 1)),
                        deps=[tmax(t_fm, t_lw, psP_free[spar])] if c == 0 else (),
                        inc=1 if c == NC - 1 else 0)
                rsub = (qb * 4 + sub) % 8
                t_ev = p.on("dve", lambda v, rsub=rsub, osl=osl, spar=spar: v.tensor_copy(
                    bass.AP(vt, rsub * MOP + osl * 4 * 129, [[8 * MOP, 128], [129, 4], [65, 2], [1, 64]]),
                    bass.AP(psP[spar], 0, [[512, 128], [128, 4], [64, 2], [1, 64]])),
                    deps=[t_mm, t_const])
                psP_free[spar] = t_ev
                v_evs.append(t_ev)
            wts_free[par] = v_evs[-1]
        t_v = tmax(*v_evs)

        def rope(dst, t_src, rfree):
            prev = rfree
            toks = []
            for sub in range(4):
                base = sub * MO
                x1 = bass.AP(dst, base, [[4 * MO, 128], [HD, NH], [2, 64]])
                x2 = bass.AP(dst, base + 1, [[4 * MO, 128], [HD, NH], [2, 64]])
                cs = bass.AP(cosr, sub * 512, [[4 * 512, 128], [64, NH], [1, 64]])
                sn = bass.AP(sinr, sub * 512, [[4 * 512, 128], [64, NH], [1, 64]])
                def rt(k):
                    return bass.AP(rtm, k * 512, [[4 * 512, 128], [64, NH], [1, 64]])
                t1 = p.on("dve", lambda v, x1=x1, cs=cs: v.tensor_tensor(rt(0), x1, cs, ALU.mult),
                          deps=[t_src, t_cos, prev])
                t2 = p.on("pool", lambda g, x1=x1, sn=sn: g.tensor_tensor(rt(1), x1, sn, ALU.mult),
                          deps=[t_src, t_sin, prev])
                t3 = p.on("dve", lambda v, x2=x2, sn=sn: v.tensor_tensor(rt(2), x2, sn, ALU.mult),
                          deps=[t_src, t_sin, prev])
                t4 = p.on("pool", lambda g, x2=x2, cs=cs: g.tensor_tensor(rt(3), x2, cs, ALU.mult),
                          deps=[t_src, t_cos, prev])
                t5 = p.on("dve", lambda v, x1=x1: v.tensor_tensor(x1, rt(0), rt(2), ALU.subtract),
                          deps=[t1, t3, t4])
                t6 = p.on("pool", lambda g, x2=x2: g.tensor_tensor(x2, rt(1), rt(3), ALU.add),
                          deps=[t2, t4, t5])
                prev = tmax(t5, t6)
                toks.append(prev)
            return tmax(*toks)

        t_qr = rope(qtm, t_q, rtm_free)
        t_kr = rope(ktm, t_k, t_qr)
        rtm_free = t_kr

        rtm_free = tmax(t_qr, t_kr)
        t_nb = tmax(t_qr, t_kr)

        qf_toks = []
        for sub in range(4):
            for h in range(NH):
                par = (sub * NH + h) % 2
                t_tp = p.on("pe", lambda t, sub=sub, h=h, par=par: t.transpose(
                    psT[par][:, 0:128], qtm[:, sub * MO + h * HD: sub * MO + (h + 1) * HD],
                    ident[:]), deps=[t_qr, t_nb, psT_free[par]])
                t_ev = p.on("dve", lambda v, sub=sub, h=h, par=par: v.tensor_copy(
                    qf[:, h * 512 + sub * 128: h * 512 + (sub + 1) * 128], psT[par][:, 0:128]),
                    deps=[t_tp, qf_free])
                psT_free[par] = t_ev
                qf_toks.append(t_ev)
        t_qf = tmax(*qf_toks)
        qtm_free = t_qf
        kf_toks = []
        for sub in range(4):
            for h in range(NH):
                par = (sub * NH + h) % 2
                t_tp = p.on("pe", lambda t, sub=sub, h=h, par=par: t.transpose(
                    psT[par][:, 0:128], ktm[:, sub * MO + h * HD: sub * MO + (h + 1) * HD],
                    ident[:]), deps=[t_kr, t_nb, psT_free[par]])
                t_ev = p.on("dve", lambda v, sub=sub, h=h, par=par, slot=slot: v.tensor_copy(
                    kf[:, h * 1024 + slot * 512 + sub * 128: h * 1024 + slot * 512 + (sub + 1) * 128],
                    psT[par][:, 0:128]), deps=[t_tp])
                psT_free[par] = t_ev
                kf_toks.append(t_ev)
        t_kf2 = tmax(*kf_toks)
        ktm_free = t_kf2

        ev_toks_all = []
        for h in range(NH):
            chunks = []
            if qb > 0:
                for kt in range(4):
                    chunks.append((True, kt))
            for kt in range(4):
                chunks.append((False, kt))
            nch = len(chunks)
            t_c0_last = t_c1_last = None
            for ci, (isprev, kt) in enumerate(chunks):
                if isprev:
                    q0 = 0
                    width = (kt + 1) * 128
                    koff = pslot * 512 + kt * 128
                    gblk = qb - 1
                else:
                    q0 = kt * 128
                    width = 512 - q0
                    koff = slot * 512 + kt * 128
                    gblk = qb
                rsub = (gblk * 4 + kt) % 8
                spar = ci % 2
                sps = psS[spar]
                t_s = p.on("pe", lambda t, h=h, koff=koff, q0=q0, width=width, sps=sps: t.matmul(
                    sps[:, 0:width],
                    bass.AP(kf, h * 1024 + koff, [[NH * 1024, 128], [1, 128]]),
                    bass.AP(qf, h * 512 + q0, [[NH * 512, 128], [1, width]]),
                    start=True, stop=True),
                    deps=[tmax(t_qf, t_kf2, psS_free[spar])])
                if isprev:
                    mq0 = kt * 128
                    t_s = p.on("dve", lambda v, sps=sps, mq0=mq0: v.tensor_tensor(
                        sps[:, mq0:mq0 + 128], sps[:, mq0:mq0 + 128], amask[:], ALU.add),
                        deps=[t_s, t_const])
                else:
                    t_s = p.on("dve", lambda v, sps=sps: v.tensor_tensor(
                        sps[:, 0:128], sps[:, 0:128], dmask[:], ALU.add),
                        deps=[t_s, t_const])
                sl = ci % 3
                t_e = p.on("act", lambda s, sl=sl, q0=q0, width=width, sps=sps: s.activation(
                    pbuf[:, sl * 512 + q0: sl * 512 + q0 + width], sps[:, 0:width],
                    AF.Exp, scale=SCALE), deps=[t_s, pbuf_free[sl]])
                psS_free[spar] = t_e
                pads = []
                if q0 > 0:
                    pads.append((sl * 512, q0))
                if q0 + width < 512:
                    pads.append((sl * 512 + q0 + width, 512 - q0 - width))
                for off, ln in pads:
                    t_mz = p.on("dve", lambda v, off=off, ln=ln: v.memset(
                        pbuf[:, off:off + ln], 0.0), deps=[pbuf_free[sl]])
                    t_e = tmax(t_e, t_mz)
                first = (ci == 0)
                last = (ci == nch - 1)
                t_c0 = p.on("pe", lambda t, rsub=rsub, h=h, sl=sl, first=first, last=last: t.matmul(
                    psC[0][0:65, :],
                    bass.AP(vt, rsub * MOP + h * 129, [[8 * MOP, 128], [1, 65]]),
                    pbuf[:, sl * 512:(sl + 1) * 512], start=first, stop=last),
                    deps=[tmax(t_e, t_v, psC_free[0] if first else None)])
                t_c1 = p.on("pe", lambda t, rsub=rsub, h=h, sl=sl, first=first, last=last: t.matmul(
                    psC[1][0:64, :],
                    bass.AP(vt, rsub * MOP + h * 129 + 65, [[8 * MOP, 128], [1, 64]]),
                    pbuf[:, sl * 512:(sl + 1) * 512], start=first, stop=last),
                    deps=[tmax(t_e, t_v, psC_free[1] if first else None)])
                pbuf_free[sl] = tmax(t_c0, t_c1)
                t_c0_last, t_c1_last = t_c0, t_c1
            t_rc = p.on("dve", lambda v: v.reciprocal(lrow[:], psC[0][64:65, 0:512]),
                        deps=[t_c0_last, lrow_free])
            t_b1 = p.on("pe", lambda t: t.matmul(psN[:], ones32r[:], lrow[:],
                                                 start=True, stop=True),
                        deps=[t_rc, psP_free[0]])
            psP_free[0] = t_b1
            lrow_free = t_b1
            t_lc = p.on("dve", lambda v: v.tensor_copy(lsc[:], psN[:]), deps=[t_b1, lsc_free])
            psP_free[0] = t_lc
            t_ev0 = p.on("dve", lambda v, h=h: v.tensor_tensor(
                bass.AP(ctxs, h * 512, [[NH * 512, 64], [1, 512]]),
                psC[0][0:64, :], lsc[0:64, :], ALU.mult),
                deps=[t_lc, t_c0_last, ctxs_free])
            t_ev1 = p.on("dve", lambda v, h=h: v.tensor_tensor(
                bass.AP(ctxs, 64 * NH * 512 + h * 512, [[NH * 512, 64], [1, 512]]),
                psC[1][0:64, :], lsc[64:128, :], ALU.mult),
                deps=[t_lc, t_c1_last, ctxs_free])
            t_ev = tmax(t_ev0, t_ev1)
            lsc_free = t_ev
            psC_free = [t_ev0, t_ev1]
            ev_toks_all.append(t_ev)
        t_ctx_all = tmax(*ev_toks_all)
        qf_free = t_ctx_all

        st_deps = []
        for oc in range(16):
            par = oc % 2
            wosl = wts[:, par * NC * 512: par * NC * 512 + 8 * 128]
            t_lw = p.dma("pool", wosl,
                         bass.AP(io["wo_t"].tensor, oc * 128 * MO, [[MO, 128], [1, MO]]),
                         deps=[wts_free[par]], sem="lw%do" % par)
            t_mm = None
            for cc in range(NH):
                t_mm = p.on("pe", lambda t, oc=oc, cc=cc, par=par, wosl=wosl: t.matmul(
                    psP[1][:], wosl[:, cc * 128:(cc + 1) * 128],
                    ctxs[:, cc * 512:(cc + 1) * 512], start=(cc == 0), stop=(cc == NH - 1)),
                    deps=[tmax(t_ctx_all, t_lw, psP_free[1])] if cc == 0 else (),
                    inc=1 if cc == NH - 1 else 0)
            wts_free[par] = t_mm
            wos = wof[:, par * 512:(par + 1) * 512]
            t_ev = p.on("act", lambda s, par=par, wos=wos: s.copy(wos, psP[1][:]),
                        deps=[t_mm, wof_free])
            psP_free[1] = t_ev
            for b in range(4):
                tpar = (oc * 4 + b) % 2
                t_tp = p.on("pe", lambda t, b=b, tpar=tpar, wos=wos: t.transpose(
                    psT[tpar][:, 0:128], wos[:, b * 128:(b + 1) * 128], ident[:]),
                    deps=[t_ev, psT_free[tpar]])
                t_cp = p.on("dve", lambda v, b=b, oc=oc, tpar=tpar: v.tensor_copy(
                    part[:, b * D + oc * 128: b * D + (oc + 1) * 128], psT[tpar][:, 0:128]),
                    deps=[t_tp, part_free])
                psT_free[tpar] = t_cp
                st_deps.append(t_cp)
            wof_free = st_deps[-1]
        ctxs_free = tmax(*st_deps)
        t_st = p.dma("pool", row_ap(io["out"]),
                     bass.AP(part, 0, [[4 * D, 128], [D, 4], [1, D]]),
                     deps=[tmax(*st_deps)], sem="sto")
        part_free = t_st
        out_toks.append(t_st)
        cos_free = tmax(t_qr, t_kr)

    p.on("sp", lambda e: e.nop(), deps=out_toks, inc=0)
    return out_toks






D = 2048
NE = 16
FM = 1024
NC = 16


def build_moe(p, io, T=2048, ST=1024):
    NSUP = T // ST
    NSB = ST // 128
    NFC = FM // 128

    x3 = p.sbuf("x3", [128, NSB * D], BF16)
    scr = p.sbuf("scr", [128, NSB * D], BF16)
    ye = scr
    ysh = scr
    xf = p.sbuf("xf", [128, NC * ST], BF16)
    wgr = p.sbuf("wgr", [128, NC * NE], BF16)
    ebias = p.sbuf("ebias", [128, 128], F32)
    ebrow = p.sbuf("ebrow", [1, 128], F32)
    scores = p.sbuf("scores", [128, NSB * NE], F32)
    scb = p.sbuf("scb", [128, NSB * NE], F32)
    prs = p.sbuf("prs", [128, 6 * NSB * 4], F32)
    gsum = p.sbuf("gsum", [128, NSB * 4], F32)
    mm1 = p.sbuf("mm1", [128, NSB], F32)
    mm2 = p.sbuf("mm2", [128, NSB], F32)
    mm3 = p.sbuf("mm3", [128, NSB], F32)
    mm4 = p.sbuf("mm4", [128, NSB], F32)
    tau = p.sbuf("tau", [128, NSB], F32)
    gmsk = p.sbuf("gmsk", [128, NSB * 4], F32)
    th8 = p.sbuf("th8", [128, 8], F32)
    thr = p.sbuf("thr", [128, NSB], F32)
    emsk = p.sbuf("emsk", [128, NSB * NE], F32)
    wmun = p.sbuf("wmun", [128, NSB * NE], F32)
    den = p.sbuf("den", [128, NSB], F32)
    wm = p.sbuf("wm", [128, NSB * NE], F32)
    wmb = p.sbuf("wmb", [128, NSB * 128], BF16)
    gwb = p.sbuf("gwb", [128, 2 * D], BF16)
    uwb = p.sbuf("uwb", [128, 2 * D], BF16)
    dwb = p.sbuf("dwb", [128, 2 * FM], BF16)
    sgb = p.sbuf("sgb", [128, 2 * 512], BF16)
    ustmp = p.sbuf("ustmp", [128, 2 * 512], BF16)
    acc = p.sbuf("acc", [128, NC * ST], F32)
    dnf = p.sbuf("dnf", [128, ST], BF16)
    ident = p.sbuf("ident", [128, 128], BF16)
    id32 = p.sbuf("id32", [128, 128], F32)
    ones32r = p.sbuf("ones32r", [1, 128], F32)
    xn = p.sbuf("xn", [128, D], BF16)
    bn6 = p.sbuf("bn6", [128, 24], F32)
    bn2 = p.sbuf("bn2", [128, 2], F32)
    rstd = p.sbuf("rstd", [128, 1], F32)
    epsc = p.sbuf("epsc", [128, 1], F32)

    psA = [p.psum("psA0", [128, 512]), p.psum("psA1", [128, 512])]
    psB = [p.psum("psB0", [128, 512]), p.psum("psB1", [128, 512])]
    psG = [p.psum("psG0", [128, 512]), p.psum("psG1", [128, 512])]
    psT = [p.psum("psT0", [128, 1024], BF16), p.psum("psT1", [128, 1024], BF16)]

    t_id = p.dma("pool", ident[:], io["ident"], sem="ldc")
    p.dma("pool", id32[:], io["ident"], sem="ldc")
    p.dma("pool", wgr[:], bass.AP(io["wgT"].tensor, 0, [[NE, 128], [128 * NE, NC], [1, NE]]), sem="ldc")
    t_eb = p.dma("pool", ebrow[:], bass.AP(io["ebias_rep"].tensor, 0, [[128, 1], [1, 128]]), sem="ldc")
    t_on = p.on("pool", lambda g: g.memset(ones32r[:], 1.0))
    t_eps = p.on("pool", lambda g: g.memset(epsc[:], 1.0e-5))
    t_eb2 = p.on("pe", lambda t: t.matmul(psA[1][:, 0:128], ones32r[:], ebrow[:],
                                          start=True, stop=True), deps=[t_eb, t_on])
    t_eb3 = p.on("dve", lambda v: v.tensor_copy(ebias[:], psA[1][:, 0:128]), deps=[t_eb2])
    t_const = tmax(t_eb, t_eps, t_on, t_eb3)

    out_toks = []
    x3_free = scr_free = xf_free = None
    psA_free = [t_eb3, None]
    psB_free = [None, None]
    psG_free = [None, None]
    psT_free = [None, None]
    gw_free = [None, None]
    uw_free = [None, None]
    dw_free = [None, None]
    sg_free = [None, None]
    us_free = [None, None]
    wmb_free = None
    acc_free = None
    dnf_free = None
    xn_free = None
    route_free = None

    for sup in range(NSUP):
        row0 = sup * ST

        def row_ap(h, r0=row0):
            return bass.AP(h.tensor, h.offset + r0 * D, [[D, 128], [128 * D, NSB], [1, D]])
        x3_ap = bass.AP(x3, 0, [[NSB * D, 128], [D, NSB], [1, D]])
        pr_ap = bass.AP(scr, 0, [[NSB * D, 128], [D, NSB], [1, D]])

        t_x = p.dma("pool", x3_ap, row_ap(io["x"]), deps=[x3_free], sem="lda")
        t_p1 = p.dma("pool", pr_ap, row_ap(io["pa"]), deps=[scr_free], sem="lda")
        t_a1 = p.on("dve", lambda v: v.tensor_tensor(x3[:], x3[:], scr[:], ALU.add), deps=[t_x, t_p1])
        if io.get("pb") is not None:
            t_p2 = p.dma("pool", pr_ap, row_ap(io["pb"]), deps=[t_a1], sem="lda")
            t_a2 = p.on("dve", lambda v: v.tensor_tensor(x3[:], x3[:], scr[:], ALU.add), deps=[t_p2])
        else:
            t_a2 = t_a1
        scr_free = t_a2

        sig_toks = []
        fm_toks = []
        for b in range(NSB):
            xrb = x3[:, b * D:(b + 1) * D]
            t_bs = None
            for s4 in range(4):
                t_bs = p.on("dve", lambda v, xrb=xrb, s4=s4: v.bn_stats(
                    bn6[:, s4 * 6:(s4 + 1) * 6],
                    bass.AP(xrb.tensor, xrb.offset + s4 * 512, [[xrb.ap[0][0], 128], [1, 512]])),
                    deps=[t_a2, xn_free])
            t_ba = p.on("dve", lambda v: v.bn_aggr(bn2[:], bn6[:]), deps=[t_bs])
            t_sq = p.on("act", lambda s: s.activation(rstd[:], bn2[:, 1:2], AF.Sqrt, bias=epsc[:]),
                        deps=[t_ba, t_const])
            t_rs = p.on("dve", lambda v: v.reciprocal(rstd[:], rstd[:]), deps=[t_sq])
            t_xn = p.on("dve", lambda v, xrb=xrb: v.tensor_scalar(
                xn[:], xrb, bn2[:, 0:1], rstd[:], ALU.subtract, ALU.mult),
                deps=[t_rs, xn_free])
            tps = []
            ev1 = []
            for c in range(NC):
                par = c % 2
                t_tp = p.on("pe", lambda t, c=c, par=par: t.transpose(
                    psT[par][:, 0:128], xn[:, c * 128:(c + 1) * 128], ident[:]),
                    deps=[t_xn, t_const, psT_free[par]])
                t_e1 = p.on("dve", lambda v, c=c, b=b, par=par: v.tensor_copy(
                    xf[:, c * ST + b * 128: c * ST + (b + 1) * 128], psT[par][:, 0:128]),
                    deps=[t_tp, xf_free])
                psT_free[par] = t_e1
                tps.append(t_tp)
                fm_toks.append(t_e1)
                ev1.append(t_e1)
            xn_free = tmax(*tps)
            t_rm = None
            for c in range(NC):
                t_rm = p.on("pe", lambda t, c=c, b=b: t.matmul(
                    psA[0][:, 0:NE], xf[:, c * ST + b * 128: c * ST + (b + 1) * 128],
                    wgr[:, c * NE:(c + 1) * NE],
                    start=(c == 0), stop=(c == NC - 1)),
                    deps=[tmax(ev1[c], t_const, psA_free[0])] if c == 0 else [ev1[c]],
                    inc=1 if c == NC - 1 else 0)
            t_sg2 = p.on("act", lambda s, b=b: s.activation(
                scores[:, b * NE:(b + 1) * NE], psA[0][:, 0:NE], AF.Sigmoid),
                deps=[t_rm, route_free])
            psA_free[0] = t_sg2
            sig_toks.append(t_sg2)
        t_fm = tmax(*fm_toks)
        t_sc = tmax(*sig_toks)

        V = 128
        t0 = p.on("dve", lambda v: v.tensor_tensor(scb[:], scores[:], ebias[:], ALU.add),
                  deps=[t_sc, t_const, route_free])
        pairs = [(0, 1), (0, 2), (0, 3), (1, 2), (1, 3), (2, 3)]
        pr_toks = []
        for pi, (i, jj) in enumerate(pairs):
            pr_toks.append(p.on("dve", lambda v, pi=pi, i=i, jj=jj: v.tensor_tensor(
                bass.AP(prs, pi * 32, [[6 * 32, 128], [4, NSB], [1, 4]]),
                bass.AP(scb, i, [[NSB * NE, 128], [NE, NSB], [4, 4]]),
                bass.AP(scb, jj, [[NSB * NE, 128], [NE, NSB], [4, 4]]), ALU.add),
                deps=[t0, route_free]))
        tp = tmax(*pr_toks)
        def prsv(pi):
            return prs[:, pi * 32:(pi + 1) * 32]
        t1 = p.on("dve", lambda v: v.tensor_tensor(prsv(0), prsv(0), prsv(1), ALU.max), deps=[tp])
        t2 = p.on("dve", lambda v: v.tensor_tensor(prsv(2), prsv(2), prsv(3), ALU.max), deps=[tp])
        t3 = p.on("dve", lambda v: v.tensor_tensor(prsv(4), prsv(4), prsv(5), ALU.max), deps=[tp])
        t4 = p.on("dve", lambda v: v.tensor_tensor(prsv(0), prsv(0), prsv(2), ALU.max), deps=[t1, t2])
        t5 = p.on("dve", lambda v: v.tensor_tensor(gsum[:], prsv(0), prsv(4), ALU.max), deps=[t3, t4])
        def gv(g):
            return bass.AP(gsum, g, [[NSB * 4, 128], [4, NSB]])
        t6 = p.on("dve", lambda v: v.tensor_tensor(mm1[:], gv(0), gv(1), ALU.max), deps=[t5])
        t7 = p.on("dve", lambda v: v.tensor_tensor(mm2[:], gv(0), gv(1), ALU.min), deps=[t5])
        t8 = p.on("dve", lambda v: v.tensor_tensor(mm3[:], gv(2), gv(3), ALU.max), deps=[t5])
        t9 = p.on("dve", lambda v: v.tensor_tensor(mm4[:], gv(2), gv(3), ALU.min), deps=[t5])
        ta = p.on("dve", lambda v: v.tensor_tensor(mm1[:], mm1[:], mm3[:], ALU.min), deps=[t6, t8])
        tb = p.on("dve", lambda v: v.tensor_tensor(mm2[:], mm2[:], mm4[:], ALU.max), deps=[t7, t9])
        tc = p.on("dve", lambda v: v.tensor_tensor(tau[:], mm1[:], mm2[:], ALU.max), deps=[ta, tb])
        gm_toks = []
        for g in range(4):
            gm_toks.append(p.on("dve", lambda v, g=g: v.tensor_tensor(
                bass.AP(gmsk, g, [[NSB * 4, 128], [4, NSB]]), gv(g), tau[:], ALU.is_ge),
                deps=[tc]))
        td = p.on("dve", lambda v: v.tensor_scalar(gmsk[:], gmsk[:], 1.0, 1.0e9, ALU.subtract, ALU.mult),
                  deps=gm_toks)
        sm_toks = []
        for i in range(4):
            sm_toks.append(p.on("dve", lambda v, i=i: v.tensor_tensor(
                bass.AP(scb, i, [[NSB * NE, 128], [NE, NSB], [4, 4]]),
                bass.AP(scb, i, [[NSB * NE, 128], [NE, NSB], [4, 4]]),
                bass.AP(gmsk, 0, [[NSB * 4, 128], [4, NSB], [1, 4]]), ALU.add),
                deps=[td]))
        te = tmax(*sm_toks)
        th_toks = []
        t_th = None
        for b in range(NSB):
            t_m8 = p.on("dve", lambda v, b=b: v.max(th8[:], scb[:, b * NE:(b + 1) * NE]),
                        deps=[te, t_th])
            t_th = p.on("dve", lambda v, b=b: v.tensor_copy(thr[:, b:b + 1], th8[:, 3:4]), deps=[t_m8])
            th_toks.append(t_th)
        em_toks = []
        for b in range(NSB):
            em_toks.append(p.on("dve", lambda v, b=b: v.tensor_scalar(
                emsk[:, b * NE:(b + 1) * NE], scb[:, b * NE:(b + 1) * NE],
                thr[:, b:b + 1], None, ALU.is_ge), deps=[th_toks[b]]))
        tf = p.on("dve", lambda v: v.tensor_tensor(wmun[:], scores[:], emsk[:], ALU.mult),
                  deps=tmax(*em_toks, t_sc))
        tg = p.on("dve", lambda v: v.tensor_reduce(
            den[:], bass.AP(wmun, 0, [[NSB * NE, 128], [NE, NSB], [1, NE]]), AX.X, ALU.add),
            deps=[tf])
        th_ = p.on("dve", lambda v: v.tensor_scalar(den[:], den[:], 1.0e-20, None, ALU.add), deps=[tg])
        ti = p.on("dve", lambda v: v.reciprocal(den[:], den[:]), deps=[th_])
        tj = p.on("dve", lambda v: v.tensor_scalar(den[:], den[:], 2.5, None, ALU.mult), deps=[ti])
        wm_toks = []
        for b in range(NSB):
            wm_toks.append(p.on("dve", lambda v, b=b: v.tensor_scalar(
                wm[:, b * NE:(b + 1) * NE], wmun[:, b * NE:(b + 1) * NE],
                den[:, b:b + 1], None, ALU.mult), deps=[tj]))
        t_wm = tmax(*wm_toks)
        route_free = t_wm

        for half in range(2):
            ysh_evs = []
            for oc in range(8):
                par = oc % 2
                gws = gwb[:, par * D:(par + 1) * D]
                uws = uwb[:, par * D:(par + 1) * D]
                t_lg = p.dma("pool", gws, bass.AP(io["shg_t"].tensor, (half * 8 + oc) * 128 * D,
                                                  [[D, 128], [1, D]]), deps=[gw_free[par]], sem="lg%d" % par)
                t_lu = p.dma("pool", uws, bass.AP(io["shu_t"].tensor, (half * 8 + oc) * 128 * D,
                                                  [[D, 128], [1, D]]), deps=[uw_free[par]], sem="lu%d" % par)
                t_g = t_u = None
                for c in range(NC):
                    t_g = p.on("pe", lambda t, c=c, par=par, gws=gws: t.matmul(
                        psG[0][:], gws[:, c * 128:(c + 1) * 128], xf[:, c * ST:c * ST + 512],
                        start=(c == 0), stop=(c == NC - 1)),
                        deps=[tmax(t_fm, t_lg, psG_free[0])] if c == 0 else (), inc=0)
                    t_g = p.on("pe", lambda t, c=c, par=par, gws=gws: t.matmul(
                        psG[1][:], gws[:, c * 128:(c + 1) * 128], xf[:, c * ST + 512:(c + 1) * ST],
                        start=(c == 0), stop=(c == NC - 1)),
                        deps=[psG_free[1]] if c == 0 else (), inc=1 if c == NC - 1 else 0)
                for c in range(NC):
                    t_u = p.on("pe", lambda t, c=c, par=par, uws=uws: t.matmul(
                        psB[0][:], uws[:, c * 128:(c + 1) * 128], xf[:, c * ST:c * ST + 512],
                        start=(c == 0), stop=(c == NC - 1)),
                        deps=[tmax(t_lu, psB_free[0])] if c == 0 else (), inc=0)
                    t_u = p.on("pe", lambda t, c=c, par=par, uws=uws: t.matmul(
                        psB[1][:], uws[:, c * 128:(c + 1) * 128], xf[:, c * ST + 512:(c + 1) * ST],
                        start=(c == 0), stop=(c == NC - 1)),
                        deps=[psB_free[1]] if c == 0 else (), inc=1 if c == NC - 1 else 0)
                gw_free[par] = t_g
                uw_free[par] = t_u
                for th in range(2):
                    t_sg3 = p.on("act", lambda s, th=th: s.activation(
                        sgb[:, th * 512:(th + 1) * 512], psG[th][:], AF.Silu),
                        deps=[t_g, sg_free[th]])
                    psG_free[th] = t_sg3
                    t_ys = p.on("dve", lambda v, oc=oc, th=th: v.tensor_tensor(
                        ysh[:, NFC * ST + oc * ST + th * 512: NFC * ST + oc * ST + (th + 1) * 512],
                        psB[th][:], sgb[:, th * 512:(th + 1) * 512], ALU.mult),
                        deps=[t_u, t_sg3, scr_free])
                    sg_free[th] = t_ys
                    psB_free[th] = t_ys
                    ysh_evs.append(t_ys)
            t_ysh = tmax(*ysh_evs)
            for oc2 in range(NC):
                par = oc2 % 2
                dws = dwb[:, par * FM:(par + 1) * FM]
                t_ld = p.dma("pool", dws, bass.AP(io["shd_t"].tensor, oc2 * 128 * D + half * 1024,
                                                  [[D, 128], [1, 1024]]), deps=[dw_free[par]], sem="ld%d" % par)
                t_d = None
                for fc in range(8):
                    t_d = p.on("pe", lambda t, fc=fc, oc2=oc2, dws=dws: t.matmul(
                        psA[0][:], dws[:, fc * 128:(fc + 1) * 128],
                        scr[:, NFC * ST + fc * ST: NFC * ST + fc * ST + 512],
                        start=(fc == 0), stop=(fc == 7)),
                        deps=[tmax(t_ysh, t_ld, psA_free[0])] if fc == 0 else (), inc=0)
                    t_d = p.on("pe", lambda t, fc=fc, oc2=oc2, dws=dws: t.matmul(
                        psA[1][:], dws[:, fc * 128:(fc + 1) * 128],
                        scr[:, NFC * ST + fc * ST + 512: NFC * ST + (fc + 1) * ST],
                        start=(fc == 0), stop=(fc == 7)),
                        deps=[psA_free[1]] if fc == 0 else (), inc=1 if fc == 7 else 0)
                dw_free[par] = t_d
                for th in range(2):
                    if half == 0:
                        t_ac = p.on("dve", lambda v, oc2=oc2, th=th: v.tensor_copy(
                            acc[:, oc2 * ST + th * 512: oc2 * ST + (th + 1) * 512], psA[th][:]),
                            deps=[t_d, acc_free])
                    else:
                        t_ac = p.on("dve", lambda v, oc2=oc2, th=th: v.tensor_tensor(
                            acc[:, oc2 * ST + th * 512: oc2 * ST + (th + 1) * 512],
                            acc[:, oc2 * ST + th * 512: oc2 * ST + (th + 1) * 512], psA[th][:], ALU.add),
                            deps=[t_d])
                    psA_free[th] = t_ac
            acc_free = None

        for e in range(NE):
            wb_toks = []
            for b in range(NSB):
                t_wb = p.on("pe", lambda t, b=b, e=e: t.matmul(
                    psA[0][:, 0:128],
                    bass.AP(wm, b * NE + e, [[NSB * NE, 128], [0, 128]]), id32[:],
                    start=True, stop=True), deps=[t_wm, psA_free[0], t_const])
                t_wc = p.on("dve", lambda v, b=b: v.tensor_copy(
                    wmb[:, b * 128:(b + 1) * 128], psA[0][:, 0:128]), deps=[t_wb, wmb_free])
                psA_free[0] = t_wc
                wb_toks.append(t_wc)
            t_wmb = tmax(*wb_toks)
            ye_evs = []
            for oc in range(NFC):
                par = oc % 2
                gws = gwb[:, par * D:(par + 1) * D]
                uws = uwb[:, par * D:(par + 1) * D]
                t_lg = p.dma("pool", gws, bass.AP(io["gw_t"].tensor, (e * NFC + oc) * 128 * D,
                                                  [[D, 128], [1, D]]), deps=[gw_free[par]], sem="lg%d" % par)
                t_lu = p.dma("pool", uws, bass.AP(io["uw_t"].tensor, (e * NFC + oc) * 128 * D,
                                                  [[D, 128], [1, D]]), deps=[uw_free[par]], sem="lu%d" % par)
                t_g = t_u = None
                for c in range(NC):
                    t_g = p.on("pe", lambda t, c=c, gws=gws: t.matmul(
                        psG[0][:], gws[:, c * 128:(c + 1) * 128], xf[:, c * ST:c * ST + 512],
                        start=(c == 0), stop=(c == NC - 1)),
                        deps=[tmax(t_fm, t_lg, psG_free[0])] if c == 0 else (), inc=0)
                    t_g = p.on("pe", lambda t, c=c, gws=gws: t.matmul(
                        psG[1][:], gws[:, c * 128:(c + 1) * 128], xf[:, c * ST + 512:(c + 1) * ST],
                        start=(c == 0), stop=(c == NC - 1)),
                        deps=[psG_free[1]] if c == 0 else (), inc=1 if c == NC - 1 else 0)
                for c in range(NC):
                    t_u = p.on("pe", lambda t, c=c, uws=uws: t.matmul(
                        psB[0][:], uws[:, c * 128:(c + 1) * 128], xf[:, c * ST:c * ST + 512],
                        start=(c == 0), stop=(c == NC - 1)),
                        deps=[tmax(t_lu, psB_free[0])] if c == 0 else (), inc=0)
                    t_u = p.on("pe", lambda t, c=c, uws=uws: t.matmul(
                        psB[1][:], uws[:, c * 128:(c + 1) * 128], xf[:, c * ST + 512:(c + 1) * ST],
                        start=(c == 0), stop=(c == NC - 1)),
                        deps=[psB_free[1]] if c == 0 else (), inc=1 if c == NC - 1 else 0)
                gw_free[par] = t_g
                uw_free[par] = t_u
                for th in range(2):
                    t_sg3 = p.on("act", lambda s, th=th: s.activation(
                        sgb[:, th * 512:(th + 1) * 512], psG[th][:], AF.Silu),
                        deps=[t_g, sg_free[th]])
                    psG_free[th] = t_sg3
                    t_us = p.on("dve", lambda v, th=th: v.tensor_tensor(
                        ustmp[:, th * 512:(th + 1) * 512], psB[th][:],
                        bass.AP(wmb, th * 512, [[NSB * 128, 128], [1, 512]]), ALU.mult),
                        deps=[t_u, t_wmb, us_free[th]])
                    psB_free[th] = t_us
                    t_ye = p.on("dve", lambda v, oc=oc, th=th: v.tensor_tensor(
                        ye[:, oc * ST + th * 512: oc * ST + (th + 1) * 512],
                        ustmp[:, th * 512:(th + 1) * 512], sgb[:, th * 512:(th + 1) * 512], ALU.mult),
                        deps=[t_us, t_sg3])
                    sg_free[th] = t_ye
                    us_free[th] = t_ye
                    ye_evs.append(t_ye)
            wmb_free = tmax(*ye_evs)
            t_ye_all = tmax(*ye_evs)
            for oc2 in range(NC):
                par = oc2 % 2
                dws = dwb[:, par * FM:(par + 1) * FM]
                t_ld = p.dma("pool", dws, bass.AP(io["dw_t"].tensor, (e * NC + oc2) * 128 * FM,
                                                  [[FM, 128], [1, FM]]), deps=[dw_free[par]], sem="ld%d" % par)
                t_d = None
                for fc in range(NFC):
                    t_d = p.on("pe", lambda t, fc=fc, dws=dws: t.matmul(
                        psA[0][:], dws[:, fc * 128:(fc + 1) * 128], ye[:, fc * ST: fc * ST + 512],
                        start=(fc == 0), stop=(fc == NFC - 1)),
                        deps=[tmax(t_ye_all, t_ld, psA_free[0])] if fc == 0 else (), inc=0)
                    t_d = p.on("pe", lambda t, fc=fc, dws=dws: t.matmul(
                        psA[1][:], dws[:, fc * 128:(fc + 1) * 128], ye[:, fc * ST + 512:(fc + 1) * ST],
                        start=(fc == 0), stop=(fc == NFC - 1)),
                        deps=[psA_free[1]] if fc == 0 else (), inc=1 if fc == NFC - 1 else 0)
                dw_free[par] = t_d
                for th in range(2):
                    t_ac = p.on("dve", lambda v, oc2=oc2, th=th: v.tensor_tensor(
                        acc[:, oc2 * ST + th * 512: oc2 * ST + (th + 1) * 512],
                        acc[:, oc2 * ST + th * 512: oc2 * ST + (th + 1) * 512], psA[th][:], ALU.add),
                        deps=[t_d])
                    psA_free[th] = t_ac
            scr_free = tmax(*[psA_free[0], psA_free[1]])

        st_deps = []
        t_acc_all = tmax(psA_free[0], psA_free[1])
        for oc2 in range(NC):
            t_cp = p.on("dve", lambda v, oc2=oc2: v.tensor_copy(
                dnf[:], acc[:, oc2 * ST:(oc2 + 1) * ST]), deps=[t_acc_all, dnf_free])
            tp_toks = []
            for b in range(NSB):
                tpar = b // 4
                t_tp = p.on("pe", lambda t, b=b, tpar=tpar: t.transpose(
                    psT[tpar][:, (b % 4) * 128:((b % 4) + 1) * 128],
                    dnf[:, b * 128:(b + 1) * 128], ident[:]),
                    deps=[t_cp, t_const, psT_free[tpar]])
                tp_toks.append(t_tp)
            for tpar in range(2):
                t_ad = p.on("dve", lambda v, oc2=oc2, tpar=tpar: v.tensor_tensor(
                    bass.AP(x3, tpar * 4 * D + oc2 * 128, [[NSB * D, 128], [D, 4], [1, 128]]),
                    bass.AP(x3, tpar * 4 * D + oc2 * 128, [[NSB * D, 128], [D, 4], [1, 128]]),
                    bass.AP(psT[tpar], 0, [[1024, 128], [128, 4], [1, 128]]), ALU.add),
                    deps=tp_toks + [t_a2])
                psT_free[tpar] = t_ad
                st_deps.append(t_ad)
            dnf_free = tmax(*st_deps)
        t_st = p.dma("pool", row_ap(io["out"]), x3_ap, deps=[tmax(*st_deps)], sem="sto")
        x3_free = t_st
        acc_free = t_acc_all
        xf_free = scr_free
        out_toks.append(t_st)

    p.on("sp", lambda e: e.nop(), deps=out_toks, inc=0)
    return out_toks





SEQ = 4096
DM = 2048


def _mk_consts():
    i = np.arange(128, dtype=np.uint64)
    pc = np.array([bin(v).count("1") for v in range(128 * 2)], dtype=np.int64)
    popc = pc[(i[:, None] & i[None, :])]
    h128 = np.where(popc % 2 == 0, 1.0, -1.0).astype(np.float32)
    ident = np.eye(128, dtype=np.float32)
    r = np.arange(128)
    dmask = np.where(r[None, :] <= r[:, None], 0.0, -1e9).astype(np.float32)
    dmaskT = np.ascontiguousarray(dmask.T)
    amask = np.where(r[None, :] > r[:, None], 0.0, -1e9).astype(np.float32)
    amaskT = np.ascontiguousarray(amask.T)
    inv = 1.0 / 10000.0 ** (np.arange(0, 128, 2) / 128.0)
    fr = np.arange(SEQ)[:, None] * inv[None, :]
    cos_rep = np.tile(np.cos(fr), (1, 8)).astype(np.float32)
    sin_rep = np.tile(np.sin(fr), (1, 8)).astype(np.float32)
    return h128, ident, dmask, dmaskT, amask, amaskT, cos_rep, sin_rep


def _perm_gqa():
    perm = []
    for oc in range(8):
        for bhalf in (0, 1):
            g = (oc // 4) * 2 + bhalf
            h = g * 4 + (oc % 4)
            perm.extend(range(h * 64, (h + 1) * 64))
    return np.array(perm)


def _f32(a):
    return np.ascontiguousarray(a, dtype=np.float32)


PAIRS = [[0, 1], [2, 3], [4, 5], [6, 7]]


def kernel(**inp):
    h128, ident, dmask, dmaskT, amask, amaskT, cos_rep, sin_rep = _mk_consts()
    perm = _perm_gqa()
    x = _f32(inp["L0_attn_x"]).reshape(4, SEQ, DM)

    wq_t = [host_tile_T(_f32(inp["L0_attn_Wq"][1024 * m:1024 * (m + 1), :])[perm]) for m in (0, 1)]
    wk_t = [host_tile_T(_f32(inp["L0_attn_Wk"][256 * m:256 * (m + 1), :])) for m in (0, 1)]
    wvT = [_f32(inp["L0_attn_Wv"][256 * m:256 * (m + 1), :].T) for m in (0, 1)]
    wo_t = [host_tile_T(_f32(inp["L0_attn_Wo"][:, 1024 * m:1024 * (m + 1)][:, perm])) for m in (0, 1)]
    wupT = host_tile_T(_f32(inp["L0_ffn_W_up"]))
    wgT = host_tile_T(_f32(inp["L0_ffn_W_gate"]))
    wdT = host_tile_T(_f32(inp["L0_ffn_W_down"]))
    wqT1 = [_f32(inp["L1_attn_Wq"][1024 * m:1024 * (m + 1), :].T) for m in (0, 1)]
    wkT1 = [_f32(inp["L1_attn_Wk"][1024 * m:1024 * (m + 1), :].T) for m in (0, 1)]
    wvT1 = [_f32(inp["L1_attn_Wv"][1024 * m:1024 * (m + 1), :].T) for m in (0, 1)]
    wo_t1 = [host_tile_T(_f32(inp["L1_attn_Wo"][:, 1024 * m:1024 * (m + 1)])) for m in (0, 1)]
    gw_t = np.stack([host_tile_T(_f32(inp["L1_ffn_gate_w"][e])) for e in range(16)])
    uw_t = np.stack([host_tile_T(_f32(inp["L1_ffn_up_w"][e])) for e in range(16)])
    dw_t = np.stack([host_tile_T(_f32(inp["L1_ffn_down_w"][e])) for e in range(16)])
    shg_t = host_tile_T(_f32(inp["L1_ffn_sh_gate_w"]))
    shu_t = host_tile_T(_f32(inp["L1_ffn_sh_up_w"]))
    shd_t = host_tile_T(_f32(inp["L1_ffn_sh_down_w"]))
    wgT1 = _f32(inp["L1_ffn_gate_weight"].T)
    eb = np.tile(_f32(inp["L1_ffn_e_score_correction_bias"]), 8)
    su = _f32(inp["L0_ffn_SU_ug"])
    sd = _f32(inp["L0_ffn_SU_dp"])
    wsu = _f32(inp["L0_ffn_Wscale_ug"])
    wsd = _f32(inp["L0_ffn_Wscale_dp"])

    nc = bass.Bass("TRN2", target_bir_lowering=False)
    ext = {}
    especs = {
        "xA": (SEQ, DM), "xB": (2048, DM),
        "ident": (128, 128), "dmask": (128, 128), "dmaskt": (128, 128), "amask": (128, 128), "amaskt": (128, 128), "h128": (128, 128),
        "cos_rep": cos_rep.shape, "sin_rep": sin_rep.shape,
        "a_wq_t": wq_t[0].shape, "a_wk_t": wk_t[0].shape,
        "a_wvT": wvT[0].shape, "a_wo_t": wo_t[0].shape,
        "b_wupT": wupT.shape, "b_wgT": wgT.shape, "b_wdT": wdT.shape,
        "b_su_ug": su.shape, "b_su_dp": sd.shape,
        "b_wsc_ug": wsu.shape, "b_wsc_dp": wsd.shape,
        "c_wqT": wqT1[0].shape, "c_wkT": wkT1[0].shape,
        "c_wvT": wvT1[0].shape, "c_wo_t": wo_t1[0].shape,
        "d_wgT": wgT1.shape, "d_ebias": eb.shape,
        "d_gw_t": gw_t.shape, "d_uw_t": uw_t.shape, "d_dw_t": dw_t.shape,
        "d_shg_t": shg_t.shape, "d_shu_t": shu_t.shape, "d_shd_t": shd_t.shape,
    }
    BF16W = {"a_wq_t", "a_wk_t", "a_wvT", "a_wo_t",
             "b_wupT", "b_wgT", "b_wdT",
             "c_wqT", "c_wkT", "c_wvT", "c_wo_t",
             "d_gw_t", "d_uw_t", "d_dw_t", "d_shg_t", "d_shu_t", "d_shd_t"}
    for n, s in especs.items():
        dt = BF16 if n in BF16W else F32
        ext[n] = nc.dram_tensor(n, list(s), dt, kind="ExternalInput").ap()
    out_t = nc.dram_tensor("out", [2048, DM], F32, kind="ExternalOutput").ap()

    pA_loc = nc.dram_tensor("pA_loc", [SEQ, DM], BF16, kind="Internal")
    pA_red = nc.dram_tensor("pA_red", [2048, DM], BF16, kind="Internal")
    x2_half = nc.dram_tensor("x2_half", [2048, DM], BF16, kind="Internal")
    x2_full = nc.dram_tensor("x2_full", [SEQ, DM], BF16, kind="Internal")
    pC_loc = nc.dram_tensor("pC_loc", [SEQ, DM], BF16, kind="Internal")
    pC_red = nc.dram_tensor("pC_red", [2048, DM], BF16, kind="Internal")

    p = Prog(nc)

    p.begin_phase("a_")
    ioA = {"x": ext["xA"], "wq_t": ext["a_wq_t"], "wk_t": ext["a_wk_t"],
           "wvT": ext["a_wvT"], "wo_t": ext["a_wo_t"],
           "ident": ext["ident"], "dmaskt": ext["dmaskt"], "out": pA_loc.ap()}
    toksA = build_attn0(p, ioA, S=SEQ)
    p.cc("ReduceScatter", ALU.add, [pA_loc.ap().opt()], [pA_red.ap().opt()],
         PAIRS, deps=[tmax(*toksA)], name="cc1")
    p.fence()

    p.begin_phase("b_")
    ioB = {"x": ext["xB"], "pa": pA_red.ap(), "pb": None, "out": x2_half.ap(),
           "wupT": ext["b_wupT"], "wgT": ext["b_wgT"], "wdT": ext["b_wdT"],
           "su_ug": ext["b_su_ug"], "su_dp": ext["b_su_dp"],
           "wsc_ug": ext["b_wsc_ug"], "wsc_dp": ext["b_wsc_dp"],
           "h128": ext["h128"], "ident": ext["ident"]}
    toksB = build_ffn(p, ioB, T=2048, TT=512)
    p.cc("AllGather", ALU.bypass, [x2_half.ap().opt()], [x2_full.ap().opt()],
         PAIRS, deps=[tmax(*toksB)], name="cc2")
    p.fence()

    p.begin_phase("c_")
    ioC = {"x": x2_full.ap(), "wqT": ext["c_wqT"], "wkT": ext["c_wkT"],
           "wvT": ext["c_wvT"], "wo_t": ext["c_wo_t"],
           "cos_rep": ext["cos_rep"], "sin_rep": ext["sin_rep"],
           "ident": ext["ident"], "dmaskt": ext["dmaskt"], "amaskt": ext["amaskt"],
           "out": pC_loc.ap()}
    toksC = build_attn1(p, ioC, S=SEQ)
    p.cc("ReduceScatter", ALU.add, [pC_loc.ap().opt()], [pC_red.ap().opt()],
         PAIRS, deps=[tmax(*toksC)], name="cc3")
    p.fence()

    p.begin_phase("d_")
    ioD = {"x": x2_half.ap(), "pa": pC_red.ap(), "pb": None, "out": out_t,
           "wgT": ext["d_wgT"], "ebias_rep": ext["d_ebias"],
           "gw_t": ext["d_gw_t"], "uw_t": ext["d_uw_t"], "dw_t": ext["d_dw_t"],
           "shg_t": ext["d_shg_t"], "shu_t": ext["d_shu_t"], "shd_t": ext["d_shd_t"],
           "ident": ext["ident"]}
    build_moe(p, ioD, T=2048, ST=1024)
    p.finalize()

    def _bf(a):
        return np.ascontiguousarray(a.astype(ml_dtypes.bfloat16))
    wq_t = [_bf(a) for a in wq_t]
    wk_t = [_bf(a) for a in wk_t]
    wvT = [_bf(a) for a in wvT]
    wo_t = [_bf(a) for a in wo_t]
    wupT, wgT, wdT = _bf(wupT), _bf(wgT), _bf(wdT)
    wqT1 = [_bf(a) for a in wqT1]
    wkT1 = [_bf(a) for a in wkT1]
    wvT1 = [_bf(a) for a in wvT1]
    wo_t1 = [_bf(a) for a in wo_t1]
    gw_t, uw_t, dw_t = _bf(gw_t), _bf(uw_t), _bf(dw_t)
    shg_t, shu_t, shd_t = _bf(shg_t), _bf(shu_t), _bf(shd_t)

    in_maps = []
    for c in range(8):
        pq, m = c // 2, c % 2
        rows = slice(2048 * m, 2048 * (m + 1))
        in_maps.append({
            "xA": _f32(x[pq]), "xB": _f32(x[pq][rows]),
            "ident": ident, "dmask": dmask, "dmaskt": dmaskT, "amask": amask, "amaskt": amaskT, "h128": h128,
            "cos_rep": cos_rep, "sin_rep": sin_rep,
            "a_wq_t": wq_t[m], "a_wk_t": wk_t[m], "a_wvT": wvT[m], "a_wo_t": wo_t[m],
            "b_wupT": wupT, "b_wgT": wgT, "b_wdT": wdT,
            "b_su_ug": su, "b_su_dp": sd, "b_wsc_ug": wsu, "b_wsc_dp": wsd,
            "c_wqT": wqT1[m], "c_wkT": wkT1[m], "c_wvT": wvT1[m], "c_wo_t": wo_t1[m],
            "d_wgT": wgT1, "d_ebias": eb,
            "d_gw_t": gw_t, "d_uw_t": uw_t, "d_dw_t": dw_t,
            "d_shg_t": shg_t, "d_shu_t": shu_t, "d_shd_t": shd_t,
        })
    res = run_bass_kernel_spmd(nc, in_maps, core_ids=list(range(8)))
    oh = [res.results[c]["out"] for c in range(8)]
    p.close()
    out = np.stack([np.concatenate([oh[2 * pq], oh[2 * pq + 1]], axis=0) for pq in range(4)])
    return out.astype(np.float32)


# version 08 (final)
# speedup vs baseline: 1.1441x; 1.0041x over previous
import contextlib
import math
from collections import defaultdict
import numpy as np
import ml_dtypes
import concourse.bass as bass
import concourse.mybir as mybir
from concourse.bass_utils import run_bass_kernel_spmd







F32 = mybir.dt.float32
BF16 = mybir.dt.bfloat16
AF = mybir.ActivationFunctionType
ALU = mybir.AluOpType
AX = mybir.AxisListType

ENGS = ("pe", "dve", "act", "pool", "sp")
ENG_ATTR = {"pe": "tensor", "dve": "vector", "act": "scalar", "pool": "gpsimd", "sp": "sync"}


def tmax(*toks):
    best = {}
    for t in toks:
        if t is None:
            continue
        if isinstance(t, list):
            for u in t:
                if u is None:
                    continue
                s, v = u
                best[s] = max(best.get(s, 0), v)
            continue
        s, v = t
        best[s] = max(best.get(s, 0), v)
    return [(s, v) for s, v in best.items()]


class Prog:
    def __init__(self, nc):
        self.nc = nc
        self.es = contextlib.ExitStack()
        self.ops = {e: [] for e in ENGS}
        self.sems = {}
        self.cnt = defaultdict(int)
        self.waited = {e: defaultdict(int) for e in ENGS}
        self.dma_sems = set()
        self.n_inst = 0
        self.prefix = ""
        self._phase_mode = False
        self._phase_addr = 0
        self._max_addr = 0
        self._arena_base = None
        self._arena_size = 212832
        self._psum_pool = {}
        self._psum_idx = {}

    def sem(self, name):
        if name not in self.sems:
            self.sems[name] = self.es.enter_context(self.nc.semaphore(name))
        return self.sems[name]

    def begin_phase(self, prefix):
        self.prefix = prefix
        self._phase_mode = True
        self._phase_addr = 0
        self._psum_idx = {}
        if self._arena_base is None:
            backing = self.es.enter_context(self.nc.sbuf_tensor(
                "sb_arena", [self.nc.NUM_PARTITIONS, self._arena_size], mybir.dt.uint8))
            self._arena_base = self.nc.lookup_mloc(backing).addr

    def sbuf(self, name, shape, dtype):
        if not self._phase_mode:
            return self.es.enter_context(self.nc.sbuf_tensor("sb_" + name, shape, dtype))
        per_part = 1
        for s in shape[1:]:
            per_part *= s
        per_part *= mybir.dt.size(dtype)
        addr = self._phase_addr
        self._phase_addr = (addr + per_part + 31) & ~31
        self._max_addr = max(self._max_addr, self._phase_addr)
        assert self._phase_addr <= self._arena_size, (
            "phase %s overflows SBUF arena: %d > %d" %
            (self.prefix, self._phase_addr, self._arena_size))
        return self.nc.alloc_sbuf_tensor_at(
            "sb_" + self.prefix + name, shape, dtype,
            offset=self._arena_base + addr)

    def psum(self, name, shape, dtype=F32):
        if not self._phase_mode:
            return self.es.enter_context(self.nc.psum_tensor(name, shape, dtype))
        key = (tuple(shape), dtype)
        idx = self._psum_idx.get(key, 0)
        self._psum_idx[key] = idx + 1
        pool = self._psum_pool.setdefault(key, [])
        if idx < len(pool):
            return pool[idx]
        t = self.es.enter_context(self.nc.psum_tensor(
            "ps_%d_%s_%d" % (shape[1], str(dtype), idx), shape, dtype))
        pool.append(t)
        return t

    def snapshot(self):
        return [(s, v) for s, v in self.cnt.items() if v > 0]

    def fence(self):
        deps = self.snapshot()
        for eng in ENGS:
            self.on(eng, lambda e: e.nop(), deps=deps, inc=0)

    def cc(self, kind, op, ins, outs, groups, deps, name):
        return self.on("pool", lambda g: g.collective_compute(
            kind, op, replica_groups=groups, ins=ins, outs=outs),
            deps=deps, sem=name, inc=1)

    def on(self, eng, fn, deps=(), sem=None, inc=1):
        sname = self.prefix + (sem or eng)
        waits = []
        for (s, v) in tmax(*deps) if deps else []:
            if s in self.dma_sems and v != self.cnt[s]:
                raise RuntimeError(
                    f"unsafe DMA wait on {s}: want {v}, issued {self.cnt[s]} "
                    f"(DMA completions are unordered; use a dedicated sem per slot)")
            if v > self.waited[eng][s]:
                waits.append((s, v))
                self.waited[eng][s] = v
        tok = None
        if inc:
            self.cnt[sname] += inc
            tok = (sname, self.cnt[sname])
        self.ops[eng].append((waits, fn, sname if inc else None, inc))
        self.n_inst += 1 + len(waits)
        return tok

    def dma(self, eng, out, in_, deps=(), sem=None, **kw):
        sname = sem or ("dma_" + eng)
        self.dma_sems.add(self.prefix + sname)
        return self.on(eng, lambda e: e.dma_start(out=out, in_=in_, **kw),
                       deps=deps, sem=sname, inc=16)

    def finalize(self):
        for s in list(self.cnt.keys()):
            self.sem(s)
        block = self.es.enter_context(self.nc.Block())
        for eng in ENGS:
            ops = self.ops[eng]
            if not ops:
                continue
            def make(eng_ops):
                def body(e):
                    for waits, fn, sname, inc in eng_ops:
                        for (s, v) in waits:
                            e.wait_ge(self.sems[s], v)
                        inst = fn(e)
                        if sname is not None:
                            inst.then_inc(self.sems[sname], inc)
                return body
            getattr(block, ENG_ATTR[eng])(make(ops))

    def close(self):
        self.es.close()









def host_tile_T(W, oc_size=128):
    O, Din = W.shape
    t = W.reshape(O // 128, 128, Din // 128, 128)
    t = t.transpose(0, 3, 2, 1)
    return np.ascontiguousarray(t.reshape(O // 128, 128, Din))


def build_ffn(p, io, T=2048, TT=512):
    nc = p.nc
    D = 2048
    INTER = 8192
    NB = TT // 128
    NTT = T // TT
    NC = D // 128
    NF = INTER // 128
    RT2 = math.sqrt(2.0)

    xr_bufs = [p.sbuf("xr", [128, NB * D], BF16),
               p.sbuf("xrB", [128, NB * D], BF16)]
    xn = p.sbuf("xn", [128, D], BF16)
    vA = p.sbuf("vA", [128, NC * TT], BF16)
    vT = p.sbuf("vT", [128, (NC // 2) * TT], BF16)
    y = p.sbuf("y", [128, NF * TT], BF16)
    yT = p.sbuf("yT", [128, (NF // 2) * TT], BF16)
    dn = p.sbuf("dn", [128, NB * D], BF16)
    wub = p.sbuf("wub", [128, 2 * D], BF16)
    wgb = p.sbuf("wgb", [128, 2 * D], BF16)
    wdb = p.sbuf("wdb", [128, 2 * (INTER // 2)], BF16)
    dnf = p.sbuf("dnf", [128, 2 * TT], BF16)
    h128 = p.sbuf("h128", [128, 128], BF16)
    ident = p.sbuf("ident", [128, 128], BF16)
    suug = p.sbuf("suug", [128, NC], F32)
    sudp = p.sbuf("sudp", [128, NF], F32)
    wscu = p.sbuf("wscu", [128, NF], F32)
    wscg = p.sbuf("wscg", [128, NF], F32)
    upmul = wscu
    wscd = p.sbuf("wscd", [128, NC], F32)
    bn6 = p.sbuf("bn6", [128, 24], F32)
    bn2 = p.sbuf("bn2", [128, 2], F32)
    rstd = p.sbuf("rstd", [128, 1], F32)
    epsc = p.sbuf("epsc", [128, 1], F32)

    psA = [p.psum("psA0", [128, 512]), p.psum("psA1", [128, 512])]
    psU = [p.psum("psU0", [128, 512]), p.psum("psU1", [128, 512])]
    psG = [p.psum("psG0", [128, 512]), p.psum("psG1", [128, 512])]
    psT = [p.psum("psT0", [128, 1024], BF16), p.psum("psT1", [128, 1024], BF16)]

    t_h = p.dma("pool", h128[:], io["h128"], sem="ldc")
    t_id = p.dma("pool", ident[:], io["ident"], sem="ldc")
    p.dma("pool", suug[:], bass.AP(io["su_ug"].tensor, 0, [[1, 128], [128, NC]]),
          allow_slow_non_contiguous=True, sem="ldc")
    p.dma("pool", sudp[:], bass.AP(io["su_dp"].tensor, 0, [[1, 128], [128, NF]]),
          allow_slow_non_contiguous=True, sem="ldc")
    p.dma("pool", wscu[:], bass.AP(io["wsc_ug"].tensor, 0, [[1, 128], [128, NF]]),
          allow_slow_non_contiguous=True, sem="ldc")
    p.dma("pool", wscg[:], bass.AP(io["wsc_ug"].tensor, INTER, [[1, 128], [128, NF]]),
          allow_slow_non_contiguous=True, sem="ldc")
    t_wd = p.dma("pool", wscd[:], bass.AP(io["wsc_dp"].tensor, 0, [[1, 128], [128, NC]]),
                 allow_slow_non_contiguous=True, sem="ldc")
    t_su = t_sd = t_wu = t_wg = t_h = t_id = t_wd
    t_eps = p.on("pool", lambda g: g.memset(epsc[:], 1.0e-5))
    t_su2 = p.on("dve", lambda v: v.tensor_scalar(suug[:], suug[:], 2.0 ** -11, None, ALU.mult), deps=[t_su])
    t_wg2 = p.on("dve", lambda v: v.tensor_scalar(wscg[:], wscg[:], 64.0 / RT2, None, ALU.mult), deps=[t_wg])
    t_um1 = p.on("dve", lambda v: v.tensor_tensor(upmul[:], wscu[:], sudp[:], ALU.mult), deps=[t_wu, t_sd])
    t_um2 = p.on("dve", lambda v: v.tensor_scalar(upmul[:], upmul[:], 2.0 ** -6 / RT2, None, ALU.mult), deps=[t_um1])
    t_wd2 = p.on("dve", lambda v: v.tensor_scalar(wscd[:], wscd[:], 64.0 / RT2, None, ALU.mult), deps=[t_wd])

    out_toks = []
    xr_free = None
    wu_free = [None, None]
    wg_free = [None, None]
    wd_free = [None, None]
    psA_free = [None, None]
    psU_free = [None, None]
    psG_free = [None, None]
    psT_free = [None, None]
    sg_free = [None, None]
    dnf_free = [None, None]
    vA_free = y_free = dn_free = None
    xn_free = None
    xr_free2 = [None, None]

    def emit_front(j):
        nonlocal xn_free
        row0 = j * TT
        xr = xr_bufs[j % 2]
        xr_free = xr_free2[j % 2]

        def row_ap(h, r0=row0):
            return bass.AP(h.tensor, h.offset + r0 * D, [[D, 128], [128 * D, NB], [1, D]])
        xr_ap = bass.AP(xr, 0, [[NB * D, 128], [D, NB], [1, D]])
        pr_ap = bass.AP(vA, 0, [[NC * TT, 128], [D, NB], [1, D]])
        pr_rd = bass.AP(vA, 0, [[NC * TT, 128], [1, NB * D]])

        t_x = p.dma("pool", xr_ap, row_ap(io["x"]), deps=[xr_free], sem="lda")
        t_p1 = p.dma("pool", pr_ap, row_ap(io["pa"]), deps=[vA_free], sem="lda")
        t_a1 = p.on("dve", lambda v, xr=xr: v.tensor_tensor(xr[:], xr[:], pr_rd, ALU.add), deps=[t_x, t_p1])
        t_a2 = t_a1

        tp_evac = []
        for b in range(NB):
            xrb = xr[:, b * D:(b + 1) * D]
            t_bs = None
            for s4 in range(4):
                t_bs = p.on("dve", lambda v, xrb=xrb, s4=s4: v.bn_stats(
                    bn6[:, s4 * 6:(s4 + 1) * 6], bass.AP(xrb.tensor, xrb.offset + s4 * 512, [[xrb.ap[0][0], 128], [1, 512]])),
                    deps=[t_a2, xn_free])
            t_ba = p.on("dve", lambda v: v.bn_aggr(bn2[:], bn6[:]), deps=[t_bs])
            t_sq = p.on("act", lambda s: s.activation(rstd[:], bn2[:, 1:2], AF.Sqrt, bias=epsc[:]), deps=[t_ba, t_eps])
            t_rs = p.on("dve", lambda v: v.reciprocal(rstd[:], rstd[:]), deps=[t_sq])
            t_xn = p.on("dve", lambda v, xrb=xrb: v.tensor_scalar(
                xn[:], xrb, bn2[:, 0:1], rstd[:], ALU.subtract, ALU.mult),
                deps=[t_rs, xn_free])
            tps = []
            for c in range(NC):
                par = (b * NC + c) % 2
                t_tp = p.on("pe", lambda t, c=c, par=par: t.transpose(
                    psT[par][:, 0:128], xn[:, c * 128:(c + 1) * 128], ident[:]),
                    deps=[t_xn, t_id, psT_free[par]])
                t_ev = p.on("dve", lambda v, c=c, b=b, par=par: v.tensor_scalar(
                    vA[:, c * TT + b * 128: c * TT + (b + 1) * 128],
                    psT[par][:, 0:128], suug[:, c:c + 1], None, ALU.mult),
                    deps=[t_tp, t_su2, t_a2])
                psT_free[par] = t_ev
                tps.append(t_tp)
                tp_evac.append(t_ev)
            xn_free = tmax(*tps)

        tA = tmax(*tp_evac)
        stA = []
        for c in range(NC):
            par = c % 2
            t_mm = p.on("pe", lambda t, c=c, par=par: t.matmul(
                psA[par][:], h128[:], vA[:, c * TT:(c + 1) * TT], start=True, stop=True),
                deps=[tA, t_h, psA_free[par]])
            t_ev = p.on("act", lambda s, c=c, par=par: s.copy(
                vA[:, c * TT:(c + 1) * TT], psA[par][:]), deps=[t_mm])
            psA_free[par] = t_ev
            stA.append(t_ev)

        tB = tmax(*stA)
        for h in (1, 2, 4, 8):
            nblk = NC // (2 * h)
            HT = TT // 2
            ev = bass.AP(vA, 0, [[NC * TT, 128], [2 * h * TT, nblk], [TT, h], [1, TT]])
            od = bass.AP(vA, h * TT, [[NC * TT, 128], [2 * h * TT, nblk], [TT, h], [1, TT]])
            tp_ = bass.AP(vT, 0, [[(NC // 2) * TT, 128], [h * TT, nblk], [TT, h], [1, TT]])
            ev0 = bass.AP(vA, 0, [[NC * TT, 128], [2 * h * TT, nblk], [TT, h], [1, HT]])
            ev1 = bass.AP(vA, HT, [[NC * TT, 128], [2 * h * TT, nblk], [TT, h], [1, HT]])
            tp0 = bass.AP(vT, 0, [[(NC // 2) * TT, 128], [h * TT, nblk], [TT, h], [1, HT]])
            tp1 = bass.AP(vT, HT, [[(NC // 2) * TT, 128], [h * TT, nblk], [TT, h], [1, HT]])
            t_c0 = p.on("act", lambda s, ev0=ev0, tp0=tp0: s.copy(tp0, ev0), deps=[tB])
            t_c1 = p.on("pool", lambda g, ev1=ev1, tp1=tp1: g.tensor_copy(tp1, ev1), deps=[tB])
            t_e = p.on("dve", lambda v, ev=ev, od=od, tp_=tp_: v.tensor_tensor(ev, tp_, od, ALU.add), deps=[t_c0, t_c1])
            t_o = p.on("dve", lambda v, od=od, tp_=tp_: v.tensor_tensor(od, tp_, od, ALU.subtract), deps=[t_e])
            tB = tmax(t_e, t_o)
        return dict(vtok=tB, t_a2=t_a2, xr=xr, xr_ap=xr_ap, row_ap=row_ap, j=j)

    front = emit_front(0)
    for j in range(NTT):
        vtok = front["vtok"]
        t_a2 = front["t_a2"]
        xr = front["xr"]
        xr_ap = front["xr_ap"]
        row_ap = front["row_ap"]

        y_toks = []
        for oc in range(NF):
            par = oc % 2
            wu_ap = bass.AP(io["wupT"].tensor, oc * 128 * D, [[D, 128], [1, D]])
            wg_ap = bass.AP(io["wgT"].tensor, oc * 128 * D, [[D, 128], [1, D]])
            wus = wub[:, par * D:(par + 1) * D]
            wgs = wgb[:, par * D:(par + 1) * D]
            t_lu = p.dma("pool", wus, wu_ap, deps=[wu_free[par]], sem="lwu%d" % par)
            t_lg = p.dma("pool", wgs, wg_ap, deps=[wg_free[par]], sem="lwg%d" % par)
            t_u = t_g = None
            for c in range(NC):
                t_u = p.on("pe", lambda t, c=c, par=par, wus=wus: t.matmul(
                    psU[par][:], wus[:, c * 128:(c + 1) * 128], vA[:, c * TT:(c + 1) * TT],
                    start=(c == 0), stop=(c == NC - 1)),
                    deps=[tmax(vtok, t_lu, psU_free[par])] if c == 0 else (),
                    inc=1 if c == NC - 1 else 0)
            for c in range(NC):
                t_g = p.on("pe", lambda t, c=c, par=par, wgs=wgs: t.matmul(
                    psG[par][:], wgs[:, c * 128:(c + 1) * 128], vA[:, c * TT:(c + 1) * TT],
                    start=(c == 0), stop=(c == NC - 1)),
                    deps=[tmax(t_lg, psG_free[par])] if c == 0 else (),
                    inc=1 if c == NC - 1 else 0)
            wu_free[par] = t_u
            wg_free[par] = t_g
            sgs = bass.AP(vT, par * TT, [[(NC // 2) * TT, 128], [1, TT]])
            t_sg = p.on("act", lambda s, oc=oc, par=par, sgs=sgs: s.activation(
                sgs, psG[par][:], AF.Silu, scale=wscg[:, oc:oc + 1]),
                deps=[t_g, t_wg2, sg_free[par]])
            psG_free[par] = t_sg
            t_y = p.on("dve", lambda v, oc=oc, par=par, sgs=sgs: v.scalar_tensor_tensor(
                y[:, oc * TT:(oc + 1) * TT], psU[par][:], upmul[:, oc:oc + 1], sgs,
                ALU.mult, ALU.mult), deps=[t_u, t_sg, t_um2, y_free])
            sg_free[par] = t_y
            psU_free[par] = t_y
            y_toks.append(t_y)
        vA_free = tmax(*y_toks)

        tA2 = tmax(*y_toks)
        stA2 = []
        for c in range(NF):
            par = c % 2
            t_mm = p.on("pe", lambda t, c=c, par=par: t.matmul(
                psA[par][:], h128[:], y[:, c * TT:(c + 1) * TT], start=True, stop=True),
                deps=[tA2, psA_free[par]])
            t_ev = p.on("act", lambda s, c=c, par=par: s.copy(
                y[:, c * TT:(c + 1) * TT], psA[par][:]), deps=[t_mm])
            psA_free[par] = t_ev
            stA2.append(t_ev)
        tB = tmax(*stA2)
        for h in (1, 2, 4, 8, 16, 32):
            nblk = NF // (2 * h)
            HT = TT // 2
            ev = bass.AP(y, 0, [[NF * TT, 128], [2 * h * TT, nblk], [TT, h], [1, TT]])
            od = bass.AP(y, h * TT, [[NF * TT, 128], [2 * h * TT, nblk], [TT, h], [1, TT]])
            tp_ = bass.AP(yT, 0, [[(NF // 2) * TT, 128], [h * TT, nblk], [TT, h], [1, TT]])
            ev0 = bass.AP(y, 0, [[NF * TT, 128], [2 * h * TT, nblk], [TT, h], [1, HT]])
            ev1 = bass.AP(y, HT, [[NF * TT, 128], [2 * h * TT, nblk], [TT, h], [1, HT]])
            tp0 = bass.AP(yT, 0, [[(NF // 2) * TT, 128], [h * TT, nblk], [TT, h], [1, HT]])
            tp1 = bass.AP(yT, HT, [[(NF // 2) * TT, 128], [h * TT, nblk], [TT, h], [1, HT]])
            t_c0 = p.on("act", lambda s, ev0=ev0, tp0=tp0: s.copy(tp0, ev0), deps=[tB])
            t_c1 = p.on("pool", lambda g, ev1=ev1, tp1=tp1: g.tensor_copy(tp1, ev1), deps=[tB])
            t_e = p.on("dve", lambda v, ev=ev, od=od, tp_=tp_: v.tensor_tensor(ev, tp_, od, ALU.add), deps=[t_c0, t_c1])
            t_o = p.on("dve", lambda v, od=od, tp_=tp_: v.tensor_tensor(od, tp_, od, ALU.subtract), deps=[t_e])
            tB = tmax(t_e, t_o)
        wtok = tB

        if j + 1 < NTT:
            front = emit_front(j + 1)

        dn_toks = []
        down_mm_toks = []
        for oc in range(NC):
            par = oc % 2
            t_d = None
            for fh in range(2):
                parw = (oc * 2 + fh) % 2
                wds = wdb[:, parw * (INTER // 2):(parw + 1) * (INTER // 2)]
                wd_ap = bass.AP(io["wdT"].tensor, oc * 128 * INTER + fh * (INTER // 2),
                                [[INTER, 128], [1, INTER // 2]])
                t_ld = p.dma("pool", wds, wd_ap, deps=[wd_free[parw]], sem="lwd%d" % parw)
                first_dep = tmax(wtok, t_ld, psU_free[par]) if fh == 0 else tmax(t_ld)
                for fc in range(NF // 2):
                    f = fh * (NF // 2) + fc
                    t_d = p.on("pe", lambda t, fc=fc, f=f, par=par, wds=wds: t.matmul(
                        psU[par][:], wds[:, fc * 128:(fc + 1) * 128], y[:, f * TT:(f + 1) * TT],
                        start=(f == 0), stop=(f == NF - 1)),
                        deps=[first_dep] if fc == 0 else (),
                        inc=1 if fc == NF // 2 - 1 else 0)
                wd_free[parw] = t_d if t_d is not None else wd_free[parw]
            down_mm_toks.append(t_d)
            dns = dnf[:, par * TT:(par + 1) * TT]
            t_ev = p.on("act", lambda s, oc=oc, par=par, dns=dns: s.activation(
                dns, psU[par][:], AF.Copy, scale=wscd[:, oc:oc + 1]),
                deps=[t_d, t_wd2, dnf_free[par]])
            psU_free[par] = t_ev
            for b in range(NB):
                tpar = (oc * NB + b) % 2
                t_tp = p.on("pe", lambda t, b=b, par=par, tpar=tpar, dns=dns: t.transpose(
                    psT[tpar][:, 0:128], dns[:, b * 128:(b + 1) * 128], ident[:]),
                    deps=[t_ev, psT_free[tpar]])
                t_cp = p.on("dve", lambda v, b=b, oc=oc, tpar=tpar: v.tensor_copy(
                    dn[:, b * D + oc * 128: b * D + (oc + 1) * 128], psT[tpar][:, 0:128]),
                    deps=[t_tp, t_a2])
                psT_free[tpar] = t_cp
                dn_toks.append(t_cp)
            dnf_free[par] = dn_toks[-1]
        y_free = tmax(*down_mm_toks)

        t_fin = p.on("dve", lambda v, xr=xr: v.tensor_tensor(xr[:], xr[:], dn[:], ALU.add),
                     deps=[tmax(*dn_toks), t_a2])
        dn_free = t_fin
        t_st = p.dma("pool", row_ap(io["out"]), xr_ap, deps=[t_fin], sem="sto%d" % (j % 2))
        xr_free2[j % 2] = t_st
        out_toks.append(t_st)

    p.on("sp", lambda e: e.nop(), deps=out_toks, inc=0)
    return out_toks






D = 2048
NQH = 16
NG = 4
HD = 64
QO = 8
KO = 2
VO = 256
VOP = 260
NC = 16


def head_oc_base(h):
    g, j = h // 4, h % 4
    return (g // 2) * 4 + j, (g % 2) * 64


def build_attn0(p, io, S=4096):
    NBLK = S // 512
    xr = p.sbuf("xr", [128, 4 * D], BF16)
    xn = p.sbuf("xn", [128, D], BF16)
    xf = p.sbuf("xf", [128, NC * 512], BF16)
    wq = p.sbuf("wq", [128, QO * D], BF16)
    wk = p.sbuf("wk", [128, KO * D], BF16)
    wv = p.sbuf("wv", [128, NC * VO], BF16)
    wo = p.sbuf("wo", [128, 16 * 1024], BF16)
    qf = p.sbuf("qf", [128, QO * 512], BF16)
    kf = p.sbuf("kf", [128, KO * S], BF16)
    vt = p.sbuf("vt", [128, (S // 128) * VOP], BF16)
    qsq = p.sbuf("qsq", [128, QO * 512], BF16)
    ksq = p.sbuf("ksq", [128, KO * 512], BF16)
    qm = p.sbuf("qm", [1, NQH], F32)
    km = p.sbuf("km", [1, NG], F32)
    kmx = p.sbuf("kmx", [1, NG], F32)
    kmrep = p.sbuf("kmrep", [1, NQH], F32)
    prod = p.sbuf("prod", [1, NQH], F32)
    brow = p.sbuf("brow", [1, NQH], F32)
    bcol = p.sbuf("bcol", [128, NQH], F32)
    ones_r = p.sbuf("ones_r", [1, 128], BF16)
    onecol = p.sbuf("onecol", [128, 1], BF16)
    pbuf = p.sbuf("pbuf", [128, 4 * 512], BF16)
    lrow = p.sbuf("lrow", [1, 2 * 512], F32)
    lsc = p.sbuf("lsc", [128, 512], F32)
    ctxs = p.sbuf("ctxs", [128, 8 * 512], BF16)
    wof = p.sbuf("wof", [128, 2 * 512], BF16)
    part = p.sbuf("part", [128, 4 * D], BF16)
    ident = p.sbuf("ident", [128, 128], BF16)
    ones32r = p.sbuf("ones32r", [1, 128], F32)
    dmask = p.sbuf("dmask", [128, 128], BF16)
    bn6 = p.sbuf("bn6", [128, 24], F32)
    bn2 = p.sbuf("bn2", [128, 2], F32)
    rstd = p.sbuf("rstd", [128, 1], F32)
    epsc = p.sbuf("epsc", [128, 1], F32)

    psP = [p.psum("psP0", [128, 512]), p.psum("psP1", [128, 512])]
    psS = [p.psum("psS0", [128, 512]), p.psum("psS1", [128, 512])]
    psC = [p.psum("psC0", [128, 512]), p.psum("psC1", [128, 512])]

    psT = [p.psum("psT0", [128, 1024], BF16), p.psum("psT1", [128, 1024], BF16)]
    psN = psP[0]

    t_id = p.dma("pool", ident[:], io["ident"], sem="ldc")
    t_dm = p.dma("pool", dmask[:], io["dmaskt"], sem="ldc")
    p.dma("pool", wv[:], bass.AP(io["wvT"].tensor, 0, [[VO, 128], [128 * VO, NC], [1, VO]]), sem="ldc")
    p.dma("pool", wq[:], bass.AP(io["wq_t"].tensor, 0, [[D, 128], [128 * D, QO], [1, D]]), sem="ldc")
    p.dma("pool", wk[:], bass.AP(io["wk_t"].tensor, 0, [[D, 128], [128 * D, KO], [1, D]]), sem="ldc")
    t_wol = p.dma("pool", wo[:], bass.AP(io["wo_t"].tensor, 0, [[1024, 128], [128 * 1024, 16], [1, 1024]]), sem="ldc")
    tc0 = p.on("pool", lambda g: g.memset(ones_r[:], 1.0))
    p.on("pool", lambda g: g.memset(ones32r[:], 1.0))
    tc1 = p.on("pool", lambda g: g.memset(onecol[:], 1.0))
    tc2 = p.on("pool", lambda g: g.memset(km[:], 0.0))
    t_vt1 = p.on("pool", lambda g: g.memset(vt[:], 1.0))
    t_eps = p.on("pool", lambda g: g.memset(epsc[:], 1.0e-5))
    t_const = tmax(t_id, t_dm, t_wol, tc0, tc1, tc2, t_vt1, t_eps)

    out_toks = []
    xr_free = xn_free = xf_free = qf_free = None
    pbuf_free = [None, None, None, None]
    psP_free = [None, None]
    psS_free = [None, None]
    psC_free = [None, None]
    psT_free = [None, None]
    qsq_free = ksq_free = None
    qm_free = kmx_free = bcol_free = None
    lrow_free = lsc_free = None
    ctxs_free = wof_free = part_free = None

    for qb in range(NBLK):
        row0 = qb * 512

        def row_ap(h, r0=row0):
            return bass.AP(h.tensor, h.offset + r0 * D, [[D, 128], [128 * D, 4], [1, D]])
        xr_ap = bass.AP(xr, 0, [[4 * D, 128], [D, 4], [1, D]])

        t_x = p.dma("pool", xr_ap, row_ap(io["x"]), deps=[xr_free], sem="lda")

        tp_evac = []
        for b in range(4):
            xrb = xr[:, b * D:(b + 1) * D]
            t_bs = None
            for s4 in range(4):
                t_bs = p.on("dve", lambda v, xrb=xrb, s4=s4: v.bn_stats(
                    bn6[:, s4 * 6:(s4 + 1) * 6],
                    bass.AP(xrb.tensor, xrb.offset + s4 * 512, [[xrb.ap[0][0], 128], [1, 512]])),
                    deps=[t_x, xn_free])
            t_ba = p.on("dve", lambda v: v.bn_aggr(bn2[:], bn6[:]), deps=[t_bs])
            t_sq = p.on("act", lambda s: s.activation(rstd[:], bn2[:, 1:2], AF.Sqrt, bias=epsc[:]),
                        deps=[t_ba, t_const])
            t_rs = p.on("dve", lambda v: v.reciprocal(rstd[:], rstd[:]), deps=[t_sq])
            t_xn = p.on("dve", lambda v, xrb=xrb: v.tensor_scalar(
                xn[:], xrb, bn2[:, 0:1], rstd[:], ALU.subtract, ALU.mult),
                deps=[t_rs, xn_free])
            tps = []
            for c in range(NC):
                par = (b * NC + c) % 2
                t_tp = p.on("pe", lambda t, c=c, par=par: t.transpose(
                    psT[par][:, 0:128], xn[:, c * 128:(c + 1) * 128], ident[:]),
                    deps=[t_xn, t_const, psT_free[par]])
                t_ev = p.on("dve", lambda v, c=c, b=b, par=par: v.tensor_copy(
                    xf[:, c * 512 + b * 128: c * 512 + (b + 1) * 128], psT[par][:, 0:128]),
                    deps=[t_tp, xf_free])
                psT_free[par] = t_ev
                tps.append(t_tp)
                tp_evac.append(t_ev)
            xn_free = tmax(*tps)
        t_fm = tmax(*tp_evac)

        q_evs = []
        for oc in range(QO):
            par = oc % 2
            t_mm = None
            for c in range(NC):
                t_mm = p.on("pe", lambda t, oc=oc, c=c, par=par: t.matmul(
                    psP[par][:], wq[:, oc * D + c * 128: oc * D + (c + 1) * 128],
                    xf[:, c * 512:(c + 1) * 512], start=(c == 0), stop=(c == NC - 1)),
                    deps=[tmax(t_fm, t_const, psP_free[par])] if c == 0 else (),
                    inc=1 if c == NC - 1 else 0)
            t_ev = p.on("act", lambda s, oc=oc, par=par: s.copy(
                qf[:, oc * 512:(oc + 1) * 512], psP[par][:]), deps=[t_mm, qf_free])
            psP_free[par] = t_ev
            q_evs.append(t_ev)
        k_evs = []
        for oc in range(KO):
            par = oc % 2
            t_mm = None
            for c in range(NC):
                t_mm = p.on("pe", lambda t, oc=oc, c=c, par=par: t.matmul(
                    psP[par][:], wk[:, oc * D + c * 128: oc * D + (c + 1) * 128],
                    xf[:, c * 512:(c + 1) * 512], start=(c == 0), stop=(c == NC - 1)),
                    deps=[tmax(t_fm, psP_free[par])] if c == 0 else (),
                    inc=1 if c == NC - 1 else 0)
            t_ev = p.on("act", lambda s, oc=oc, par=par, qb=qb: s.copy(
                kf[:, oc * S + qb * 512: oc * S + (qb + 1) * 512], psP[par][:]), deps=[t_mm])
            psP_free[par] = t_ev
            k_evs.append(t_ev)
        v_evs = []
        for b in range(4):
            par = b % 2
            t_mm = None
            for c in range(NC):
                t_mm = p.on("pe", lambda t, b=b, c=c, par=par: t.matmul(
                    psP[par][:, 0:VO], xf[:, c * 512 + b * 128: c * 512 + (b + 1) * 128],
                    wv[:, c * VO:(c + 1) * VO], start=(c == 0), stop=(c == NC - 1)),
                    deps=[tmax(t_fm, psP_free[par])] if c == 0 else (),
                    inc=1 if c == NC - 1 else 0)
            sub = qb * 4 + b
            t_ev = p.on("act", lambda s, sub=sub, par=par: s.copy(
                bass.AP(vt, sub * VOP, [[(S // 128) * VOP, 128], [65, 4], [1, 64]]),
                bass.AP(psP[par], 0, [[512, 128], [64, 4], [1, 64]])),
                deps=[t_mm, t_const])
            psP_free[par] = t_ev
            v_evs.append(t_ev)
        t_kv = tmax(*k_evs, *v_evs)
        t_q = tmax(*q_evs)
        xr_free = t_fm

        t_qs2 = p.on("act", lambda s: s.activation(qsq[:], qf[:], AF.Square), deps=[t_q, qsq_free])
        t_ks2 = p.on("act", lambda s, qb=qb: s.activation(
            ksq[:], bass.AP(kf, qb * 512, [[KO * S, 128], [S, KO], [1, 512]]), AF.Square),
            deps=[tmax(*k_evs), ksq_free])
        nm_toks = []
        for h in range(NQH):
            oc, base = head_oc_base(h)
            t_nm = p.on("pe", lambda t, oc=oc, base=base: t.matmul(
                psN[0:1, 0:512], bass.AP(onecol, base, [[1, 64], [0, 1]]),
                bass.AP(qsq, base * QO * 512 + oc * 512, [[QO * 512, 64], [1, 512]]),
                start=True, stop=True),
                deps=[t_qs2, t_const, psP_free[0]])
            t_mx = p.on("dve", lambda v, h=h: v.tensor_reduce(
                qm[:, h:h + 1], psN[0:1, 0:512], AX.X, ALU.max), deps=[t_nm, qm_free])
            psP_free[0] = t_mx
            nm_toks.append(t_mx)
        for g in range(NG):
            oc, base = g // 2, (g % 2) * 64
            t_nm = p.on("pe", lambda t, oc=oc, base=base: t.matmul(
                psN[0:1, 0:512], bass.AP(onecol, base, [[1, 64], [0, 1]]),
                bass.AP(ksq, base * KO * 512 + oc * 512, [[KO * 512, 64], [1, 512]]),
                start=True, stop=True),
                deps=[t_ks2, psP_free[0]])
            t_mx = p.on("dve", lambda v, g=g: v.tensor_reduce(
                kmx[:, g:g + 1], psN[0:1, 0:512], AX.X, ALU.max), deps=[t_nm, kmx_free])
            t_km = p.on("dve", lambda v, g=g: v.tensor_tensor(
                km[:, g:g + 1], km[:, g:g + 1], kmx[:, g:g + 1], ALU.max), deps=[t_mx])
            psP_free[0] = t_mx
            nm_toks.append(t_km)
        t_nm_all = tmax(*nm_toks)
        qsq_free = t_nm_all
        ksq_free = t_nm_all
        kmx_free = t_nm_all
        rep_toks = []
        for g in range(NG):
            rep_toks.append(p.on("dve", lambda v, g=g: v.tensor_scalar(
                kmrep[:, 4 * g:4 * (g + 1)], ones_r[0:1, 0:4], km[:, g:g + 1], None, ALU.mult),
                deps=[t_nm_all]))
        t_pr = p.on("dve", lambda v: v.tensor_tensor(prod[:], qm[:], kmrep[:], ALU.mult),
                    deps=rep_toks)
        qm_free = t_pr
        t_br = p.on("act", lambda s: s.activation(brow[:], prod[:], AF.Sqrt), deps=[t_pr])
        t_bn = p.on("dve", lambda v: v.tensor_scalar(brow[:], brow[:], -0.125, None, ALU.mult), deps=[t_br])
        t_bc0 = p.on("pe", lambda t: t.matmul(psN[:, 0:NQH], ones32r[0:1, :], brow[:],
                                              start=True, stop=True), deps=[t_bn, psP_free[0]])
        t_bc = p.on("dve", lambda v: v.tensor_copy(bcol[:], psN[:, 0:NQH]), deps=[t_bc0, bcol_free])
        psP_free[0] = t_bc

        pair_evac = []
        for pp in range(8):
            if True:
                t_ctx_last = {0: None, 1: None}
                for kc in range(qb + 1):
                    diag = (kc == qb)
                    for hloc in (0, 1):
                        h = (2 * (pp // 4)) * 4 + (pp % 4) + hloc * 4
                        g = h // 4
                        gbase = (g % 2) * 64
                        oc, base = head_oc_base(h)
                        exp_toks = [None] * 4
                        for ks in range(4):
                            q0 = ks * 128 if diag else 0
                            width = 512 - q0
                            spar = ks % 2
                            sps = psS[spar]
                            t_s = p.on("pe", lambda t, oc=oc, base=base, ks=ks, kc=kc, q0=q0, width=width, sps=sps, gbase=gbase, g=g: t.matmul(
                                sps[:, 0:width],
                                bass.AP(kf, gbase * KO * S + (g // 2) * S + kc * 512 + ks * 128, [[KO * S, 64], [1, 128]]),
                                bass.AP(qf, base * QO * 512 + oc * 512 + q0, [[QO * 512, 64], [1, width]]),
                                start=True, stop=True),
                                deps=[tmax(t_q, t_kv, psS_free[spar])])
                            if diag:
                                t_s = p.on("dve", lambda v, sps=sps: v.tensor_tensor(
                                    sps[:, 0:128], sps[:, 0:128], dmask[:], ALU.add),
                                    deps=[t_s, t_const])
                            t_e = p.on("act", lambda s, ks=ks, q0=q0, width=width, sps=sps, h=h: s.activation(
                                pbuf[:, ks * 512 + q0: (ks + 1) * 512], sps[:, 0:width], AF.Exp,
                                bias=bcol[:, h:h + 1], scale=0.125),
                                deps=[t_s, t_bc, pbuf_free[ks]])
                            psS_free[spar] = t_e
                            if diag and ks > 0:
                                t_m = p.on("dve", lambda v, ks=ks, q0=q0: v.memset(
                                    pbuf[:, ks * 512: ks * 512 + q0], 0.0),
                                    deps=[pbuf_free[ks]])
                                t_e = tmax(t_e, t_m)
                            exp_toks[ks] = t_e
                        for ks in range(4):
                            sub = kc * 4 + ks
                            first = (kc == 0 and ks == 0)
                            last = (kc == qb and ks == 3)
                            t_cm = p.on("pe", lambda t, sub=sub, g=g, hloc=hloc, ks=ks, first=first, last=last: t.matmul(
                                psC[hloc][0:65, :],
                                vt[:, sub * VOP + g * 65: sub * VOP + g * 65 + 65],
                                pbuf[:, ks * 512:(ks + 1) * 512], start=first, stop=last),
                                deps=[tmax(exp_toks[ks], t_kv, psC_free[hloc] if first else None)])
                            pbuf_free[ks] = t_cm
                            t_ctx_last[hloc] = t_cm
                cc = pp
                bc_toks = []
                rc_toks = []
                for hloc in (0, 1):
                    t_rc = p.on("dve", lambda v, hloc=hloc: v.reciprocal(
                        lrow[:, hloc * 512:(hloc + 1) * 512], psC[hloc][64:65, 0:512]),
                        deps=[t_ctx_last[hloc], lrow_free])
                    rc_toks.append(t_rc)
                for hloc in (0, 1):
                    t_b1 = p.on("pe", lambda t, hloc=hloc: t.matmul(
                        psN[hloc * 64:(hloc + 1) * 64, :], ones32r[0:1, 0:64],
                        lrow[:, hloc * 512:(hloc + 1) * 512],
                        start=True, stop=True, tile_position=(0, hloc * 64)),
                        deps=[rc_toks[hloc], psP_free[0], t_const])
                    psP_free[0] = t_b1
                    bc_toks.append(t_b1)
                lrow_free = tmax(*bc_toks)
                t_lc = p.on("dve", lambda v: v.tensor_copy(lsc[:], psN[:]),
                            deps=[tmax(*bc_toks), lsc_free])
                psP_free[0] = t_lc
                t_ev0 = p.on("dve", lambda v, cc=cc: v.tensor_tensor(
                    bass.AP(ctxs, cc * 512, [[8 * 512, 64], [1, 512]]),
                    psC[0][0:64, :], lsc[0:64, :], ALU.mult),
                    deps=[t_lc, t_ctx_last[0], ctxs_free])
                t_ev1 = p.on("dve", lambda v, cc=cc: v.tensor_tensor(
                    bass.AP(ctxs, 64 * 8 * 512 + cc * 512, [[8 * 512, 64], [1, 512]]),
                    psC[1][0:64, :], lsc[64:128, :], ALU.mult),
                    deps=[t_lc, t_ctx_last[1], ctxs_free])
                t_ev = tmax(t_ev0, t_ev1)
                lsc_free = t_ev
                psC_free = [t_ev0, t_ev1]
                pair_evac.append(t_ev)
        t_ctx_all = tmax(*pair_evac)
        qf_free = t_ctx_all
        bcol_free = t_ctx_all
        xf_free = tmax(t_q, t_kv)

        st_deps = []
        for oc in range(16):
            par = oc % 2
            t_mm = None
            for cc in range(8):
                t_mm = p.on("pe", lambda t, oc=oc, cc=cc, par=par: t.matmul(
                    psP[par][:], wo[:, oc * 1024 + cc * 128: oc * 1024 + (cc + 1) * 128],
                    ctxs[:, cc * 512:(cc + 1) * 512], start=(cc == 0), stop=(cc == 7)),
                    deps=[tmax(t_ctx_all, psP_free[par])] if cc == 0 else (),
                    inc=1 if cc == 7 else 0)
            wos = wof[:, par * 512:(par + 1) * 512]
            t_ev = p.on("act", lambda s, par=par, wos=wos: s.copy(wos, psP[par][:]),
                        deps=[t_mm, wof_free])
            psP_free[par] = t_ev
            for b in range(4):
                tpar = (oc * 4 + b) % 2
                t_tp = p.on("pe", lambda t, b=b, tpar=tpar, wos=wos: t.transpose(
                    psT[tpar][:, 0:128], wos[:, b * 128:(b + 1) * 128], ident[:]),
                    deps=[t_ev, psT_free[tpar]])
                t_cp = p.on("dve", lambda v, b=b, oc=oc, tpar=tpar: v.tensor_copy(
                    part[:, b * D + oc * 128: b * D + (oc + 1) * 128], psT[tpar][:, 0:128]),
                    deps=[t_tp, part_free])
                psT_free[tpar] = t_cp
                st_deps.append(t_cp)
            wof_free = st_deps[-1]
        ctxs_free = tmax(*[t for t in st_deps])
        t_st = p.dma("pool", row_ap(io["out"]),
                     bass.AP(part, 0, [[4 * D, 128], [D, 4], [1, D]]),
                     deps=[tmax(*st_deps)], sem="sto")
        part_free = t_st
        out_toks.append(t_st)

    p.on("sp", lambda e: e.nop(), deps=out_toks, inc=0)
    return out_toks






D = 2048
NH = 8
HD = 128
NC = 16
MO = NH * HD
MOP = NH * 129
SCALE = float(HD) ** -0.5


def build_attn1(p, io, S=4096):
    NBLK = S // 512

    xr = p.sbuf("xr", [128, 4 * D], BF16)
    xn = p.sbuf("xn", [128, D], BF16)
    xf = p.sbuf("xf", [128, NC * 512], BF16)
    wts = p.sbuf("wts", [128, 2 * NC * 512], BF16)
    qtm = p.sbuf("qtm", [128, 4 * MO], BF16)
    ktm = p.sbuf("ktm", [128, 4 * MO], BF16)
    rtm = p.sbuf("rtm", [128, 4 * 512], BF16)
    qf = p.sbuf("qf", [128, NH * 512], BF16)
    kf = p.sbuf("kf", [128, NH * 1024], BF16)
    vt = p.sbuf("vt", [128, 8 * MOP], BF16)
    cosr = p.sbuf("cosr", [128, 4 * 512], BF16)
    sinr = p.sbuf("sinr", [128, 4 * 512], BF16)
    pbuf = p.sbuf("pbuf", [128, 3 * 512], BF16)
    lrow = p.sbuf("lrow", [1, 512], F32)
    lsc = p.sbuf("lsc", [128, 512], F32)
    ctxs = p.sbuf("ctxs", [128, NH * 512], BF16)
    wof = p.sbuf("wof", [128, 2 * 512], BF16)
    part = p.sbuf("part", [128, 4 * D], BF16)
    ident = p.sbuf("ident", [128, 128], BF16)
    id32 = p.sbuf("id32", [128, 128], F32)
    ones32r = p.sbuf("ones32r", [1, 128], F32)
    dmask = p.sbuf("dmask", [128, 128], BF16)
    amask = p.sbuf("amask", [128, 128], BF16)
    bn6 = p.sbuf("bn6", [128, 24], F32)
    bn2 = p.sbuf("bn2", [128, 2], F32)
    rstd = p.sbuf("rstd", [128, 1], F32)
    epsc = p.sbuf("epsc", [128, 1], F32)

    psP = [p.psum("psP0", [128, 512]), p.psum("psP1", [128, 512])]
    psS = [p.psum("psS0", [128, 512]), p.psum("psS1", [128, 512])]
    psC = [p.psum("psC0", [128, 512]), p.psum("psC1", [128, 512])]
    psN = psP[0]
    psT = [p.psum("psT0", [128, 1024], BF16), p.psum("psT1", [128, 1024], BF16)]

    p.dma("pool", ident[:], io["ident"], sem="ldc")
    p.dma("pool", id32[:], io["ident"], sem="ldc")
    p.dma("pool", dmask[:], io["dmaskt"], sem="ldc")
    t_am = p.dma("pool", amask[:], io["amaskt"], sem="ldc")
    t_on = p.on("pool", lambda g: g.memset(ones32r[:], 1.0))
    t_vt1 = p.on("pool", lambda g: g.memset(vt[:], 1.0))
    t_eps = p.on("pool", lambda g: g.memset(epsc[:], 1.0e-5))
    t_const = tmax(t_am, t_on, t_vt1, t_eps)

    out_toks = []
    xr_free = xn_free = xf_free = None
    wts_free = [None, None]
    psP_free = [None, None]
    psS_free = [None, None]
    psC_free = [None, None]
    psT_free = [None, None]
    qtm_free = ktm_free = rtm_free = None
    qf_free = None
    cos_free = None
    pbuf_free = [None, None, None]
    lrow_free = lsc_free = None
    ctxs_free = wof_free = part_free = None

    for qb in range(NBLK):
        row0 = qb * 512
        slot = qb % 2
        pslot = (qb - 1) % 2

        def row_ap(h, r0=row0):
            return bass.AP(h.tensor, h.offset + r0 * D, [[D, 128], [128 * D, 4], [1, D]])
        xr_ap = bass.AP(xr, 0, [[4 * D, 128], [D, 4], [1, D]])

        t_x = p.dma("pool", xr_ap, row_ap(io["x"]), deps=[xr_free], sem="lda")
        t_cos = p.dma("pool", bass.AP(cosr, 0, [[4 * 512, 128], [512, 4], [1, 512]]),
                      bass.AP(io["cos_rep"].tensor, row0 * 512, [[512, 128], [128 * 512, 4], [1, 512]]),
                      deps=[cos_free], sem="ldr")
        t_sin = p.dma("pool", bass.AP(sinr, 0, [[4 * 512, 128], [512, 4], [1, 512]]),
                      bass.AP(io["sin_rep"].tensor, row0 * 512, [[512, 128], [128 * 512, 4], [1, 512]]),
                      deps=[cos_free], sem="ldr2")

        tp_evac = []
        for b in range(4):
            xrb = xr[:, b * D:(b + 1) * D]
            t_bs = None
            for s4 in range(4):
                t_bs = p.on("dve", lambda v, xrb=xrb, s4=s4: v.bn_stats(
                    bn6[:, s4 * 6:(s4 + 1) * 6],
                    bass.AP(xrb.tensor, xrb.offset + s4 * 512, [[xrb.ap[0][0], 128], [1, 512]])),
                    deps=[t_x, xn_free])
            t_ba = p.on("dve", lambda v: v.bn_aggr(bn2[:], bn6[:]), deps=[t_bs])
            t_sq = p.on("act", lambda s: s.activation(rstd[:], bn2[:, 1:2], AF.Sqrt, bias=epsc[:]),
                        deps=[t_ba, t_const])
            t_rs = p.on("dve", lambda v: v.reciprocal(rstd[:], rstd[:]), deps=[t_sq])
            t_xn = p.on("dve", lambda v, xrb=xrb: v.tensor_scalar(
                xn[:], xrb, bn2[:, 0:1], rstd[:], ALU.subtract, ALU.mult),
                deps=[t_rs, xn_free])
            tps = []
            for c in range(NC):
                par = (b * NC + c) % 2
                t_tp = p.on("pe", lambda t, c=c, par=par: t.transpose(
                    psT[par][:, 0:128], xn[:, c * 128:(c + 1) * 128], ident[:]),
                    deps=[t_xn, t_const, psT_free[par]])
                t_ev = p.on("dve", lambda v, c=c, b=b, par=par: v.tensor_copy(
                    xf[:, c * 512 + b * 128: c * 512 + (b + 1) * 128], psT[par][:, 0:128]),
                    deps=[t_tp, xf_free])
                psT_free[par] = t_ev
                tps.append(t_tp)
                tp_evac.append(t_ev)
            xn_free = tmax(*tps)
        t_fm = tmax(*tp_evac)
        xr_free = t_fm

        def proj_tm(dst, wio, dst_free, tag):
            evs = []
            for osl in range(2):
                par = osl % 2
                ws = wts[:, par * NC * 512:(par + 1) * NC * 512]
                t_lw = p.dma("pool", ws,
                             bass.AP(wio.tensor, osl * 512, [[MO, 128], [128 * MO, NC], [1, 512]]),
                             deps=[wts_free[par]], sem="lw%d%s" % (par, tag))
                for sub in range(4):
                    spar = sub % 2
                    t_mm = None
                    for c in range(NC):
                        t_mm = p.on("pe", lambda t, c=c, sub=sub, spar=spar, ws=ws: t.matmul(
                            psP[spar][:], xf[:, c * 512 + sub * 128: c * 512 + (sub + 1) * 128],
                            ws[:, c * 512:(c + 1) * 512], start=(c == 0), stop=(c == NC - 1)),
                            deps=[tmax(t_fm, t_lw, psP_free[spar])] if c == 0 else (),
                            inc=1 if c == NC - 1 else 0)
                    t_ev = p.on("dve", lambda v, sub=sub, osl=osl, spar=spar, dst=dst: v.tensor_copy(
                        dst[:, sub * MO + osl * 512: sub * MO + (osl + 1) * 512], psP[spar][:]),
                        deps=[t_mm, dst_free])
                    psP_free[spar] = t_ev
                    evs.append(t_ev)
                wts_free[par] = evs[-1]
            return tmax(*evs)

        t_q = proj_tm(qtm, io["wqT"], qtm_free, "q")
        t_k = proj_tm(ktm, io["wkT"], ktm_free, "k")
        v_evs = []
        for osl in range(2):
            par = osl % 2
            ws = wts[:, par * NC * 512:(par + 1) * NC * 512]
            t_lw = p.dma("pool", ws,
                         bass.AP(io["wvT"].tensor, osl * 512, [[MO, 128], [128 * MO, NC], [1, 512]]),
                         deps=[wts_free[par]], sem="lw%dv" % par)
            for sub in range(4):
                spar = sub % 2
                t_mm = None
                for c in range(NC):
                    t_mm = p.on("pe", lambda t, c=c, sub=sub, spar=spar, ws=ws: t.matmul(
                        psP[spar][:], xf[:, c * 512 + sub * 128: c * 512 + (sub + 1) * 128],
                        ws[:, c * 512:(c + 1) * 512], start=(c == 0), stop=(c == NC - 1)),
                        deps=[tmax(t_fm, t_lw, psP_free[spar])] if c == 0 else (),
                        inc=1 if c == NC - 1 else 0)
                rsub = (qb * 4 + sub) % 8
                t_ev = p.on("dve", lambda v, rsub=rsub, osl=osl, spar=spar: v.tensor_copy(
                    bass.AP(vt, rsub * MOP + osl * 4 * 129, [[8 * MOP, 128], [129, 4], [65, 2], [1, 64]]),
                    bass.AP(psP[spar], 0, [[512, 128], [128, 4], [64, 2], [1, 64]])),
                    deps=[t_mm, t_const])
                psP_free[spar] = t_ev
                v_evs.append(t_ev)
            wts_free[par] = v_evs[-1]
        t_v = tmax(*v_evs)

        def rope(dst, t_src, rfree):
            prev = rfree
            toks = []
            for sub in range(4):
                base = sub * MO
                x1 = bass.AP(dst, base, [[4 * MO, 128], [HD, NH], [2, 64]])
                x2 = bass.AP(dst, base + 1, [[4 * MO, 128], [HD, NH], [2, 64]])
                cs = bass.AP(cosr, sub * 512, [[4 * 512, 128], [64, NH], [1, 64]])
                sn = bass.AP(sinr, sub * 512, [[4 * 512, 128], [64, NH], [1, 64]])
                def rt(k):
                    return bass.AP(rtm, k * 512, [[4 * 512, 128], [64, NH], [1, 64]])
                t1 = p.on("dve", lambda v, x1=x1, cs=cs: v.tensor_tensor(rt(0), x1, cs, ALU.mult),
                          deps=[t_src, t_cos, prev])
                t2 = p.on("pool", lambda g, x1=x1, sn=sn: g.tensor_tensor(rt(1), x1, sn, ALU.mult),
                          deps=[t_src, t_sin, prev])
                t3 = p.on("dve", lambda v, x2=x2, sn=sn: v.tensor_tensor(rt(2), x2, sn, ALU.mult),
                          deps=[t_src, t_sin, prev])
                t4 = p.on("pool", lambda g, x2=x2, cs=cs: g.tensor_tensor(rt(3), x2, cs, ALU.mult),
                          deps=[t_src, t_cos, prev])
                t5 = p.on("dve", lambda v, x1=x1: v.tensor_tensor(x1, rt(0), rt(2), ALU.subtract),
                          deps=[t1, t3, t4])
                t6 = p.on("pool", lambda g, x2=x2: g.tensor_tensor(x2, rt(1), rt(3), ALU.add),
                          deps=[t2, t4, t5])
                prev = tmax(t5, t6)
                toks.append(prev)
            return tmax(*toks)

        t_qr = rope(qtm, t_q, rtm_free)
        t_kr = rope(ktm, t_k, t_qr)
        rtm_free = t_kr

        rtm_free = tmax(t_qr, t_kr)
        t_nb = tmax(t_qr, t_kr)

        qf_toks = []
        for sub in range(4):
            for h in range(NH):
                par = (sub * NH + h) % 2
                t_tp = p.on("pe", lambda t, sub=sub, h=h, par=par: t.transpose(
                    psT[par][:, 0:128], qtm[:, sub * MO + h * HD: sub * MO + (h + 1) * HD],
                    ident[:]), deps=[t_qr, t_nb, psT_free[par]])
                t_ev = p.on("dve", lambda v, sub=sub, h=h, par=par: v.tensor_copy(
                    qf[:, h * 512 + sub * 128: h * 512 + (sub + 1) * 128], psT[par][:, 0:128]),
                    deps=[t_tp, qf_free])
                psT_free[par] = t_ev
                qf_toks.append(t_ev)
        t_qf = tmax(*qf_toks)
        qtm_free = t_qf
        kf_toks = []
        for sub in range(4):
            for h in range(NH):
                par = (sub * NH + h) % 2
                t_tp = p.on("pe", lambda t, sub=sub, h=h, par=par: t.transpose(
                    psT[par][:, 0:128], ktm[:, sub * MO + h * HD: sub * MO + (h + 1) * HD],
                    ident[:]), deps=[t_kr, t_nb, psT_free[par]])
                t_ev = p.on("dve", lambda v, sub=sub, h=h, par=par, slot=slot: v.tensor_copy(
                    kf[:, h * 1024 + slot * 512 + sub * 128: h * 1024 + slot * 512 + (sub + 1) * 128],
                    psT[par][:, 0:128]), deps=[t_tp])
                psT_free[par] = t_ev
                kf_toks.append(t_ev)
        t_kf2 = tmax(*kf_toks)
        ktm_free = t_kf2

        ev_toks_all = []
        for h in range(NH):
            chunks = []
            if qb > 0:
                for kt in range(4):
                    chunks.append((True, kt))
            for kt in range(4):
                chunks.append((False, kt))
            nch = len(chunks)
            t_c0_last = t_c1_last = None
            for ci, (isprev, kt) in enumerate(chunks):
                if isprev:
                    q0 = 0
                    width = (kt + 1) * 128
                    koff = pslot * 512 + kt * 128
                    gblk = qb - 1
                else:
                    q0 = kt * 128
                    width = 512 - q0
                    koff = slot * 512 + kt * 128
                    gblk = qb
                rsub = (gblk * 4 + kt) % 8
                spar = ci % 2
                sps = psS[spar]
                t_s = p.on("pe", lambda t, h=h, koff=koff, q0=q0, width=width, sps=sps: t.matmul(
                    sps[:, 0:width],
                    bass.AP(kf, h * 1024 + koff, [[NH * 1024, 128], [1, 128]]),
                    bass.AP(qf, h * 512 + q0, [[NH * 512, 128], [1, width]]),
                    start=True, stop=True),
                    deps=[tmax(t_qf, t_kf2, psS_free[spar])])
                if isprev:
                    mq0 = kt * 128
                    t_s = p.on("dve", lambda v, sps=sps, mq0=mq0: v.tensor_tensor(
                        sps[:, mq0:mq0 + 128], sps[:, mq0:mq0 + 128], amask[:], ALU.add),
                        deps=[t_s, t_const])
                else:
                    t_s = p.on("dve", lambda v, sps=sps: v.tensor_tensor(
                        sps[:, 0:128], sps[:, 0:128], dmask[:], ALU.add),
                        deps=[t_s, t_const])
                sl = ci % 3
                t_e = p.on("act", lambda s, sl=sl, q0=q0, width=width, sps=sps: s.activation(
                    pbuf[:, sl * 512 + q0: sl * 512 + q0 + width], sps[:, 0:width],
                    AF.Exp, scale=SCALE), deps=[t_s, pbuf_free[sl]])
                psS_free[spar] = t_e
                pads = []
                if q0 > 0:
                    pads.append((sl * 512, q0))
                if q0 + width < 512:
                    pads.append((sl * 512 + q0 + width, 512 - q0 - width))
                for off, ln in pads:
                    t_mz = p.on("dve", lambda v, off=off, ln=ln: v.memset(
                        pbuf[:, off:off + ln], 0.0), deps=[pbuf_free[sl]])
                    t_e = tmax(t_e, t_mz)
                first = (ci == 0)
                last = (ci == nch - 1)
                t_c0 = p.on("pe", lambda t, rsub=rsub, h=h, sl=sl, first=first, last=last: t.matmul(
                    psC[0][0:65, :],
                    bass.AP(vt, rsub * MOP + h * 129, [[8 * MOP, 128], [1, 65]]),
                    pbuf[:, sl * 512:(sl + 1) * 512], start=first, stop=last),
                    deps=[tmax(t_e, t_v, psC_free[0] if first else None)])
                t_c1 = p.on("pe", lambda t, rsub=rsub, h=h, sl=sl, first=first, last=last: t.matmul(
                    psC[1][0:64, :],
                    bass.AP(vt, rsub * MOP + h * 129 + 65, [[8 * MOP, 128], [1, 64]]),
                    pbuf[:, sl * 512:(sl + 1) * 512], start=first, stop=last),
                    deps=[tmax(t_e, t_v, psC_free[1] if first else None)])
                pbuf_free[sl] = tmax(t_c0, t_c1)
                t_c0_last, t_c1_last = t_c0, t_c1
            t_rc = p.on("dve", lambda v: v.reciprocal(lrow[:], psC[0][64:65, 0:512]),
                        deps=[t_c0_last, lrow_free])
            t_b1 = p.on("pe", lambda t: t.matmul(psN[:], ones32r[:], lrow[:],
                                                 start=True, stop=True),
                        deps=[t_rc, psP_free[0]])
            psP_free[0] = t_b1
            lrow_free = t_b1
            t_lc = p.on("dve", lambda v: v.tensor_copy(lsc[:], psN[:]), deps=[t_b1, lsc_free])
            psP_free[0] = t_lc
            t_ev0 = p.on("dve", lambda v, h=h: v.tensor_tensor(
                bass.AP(ctxs, h * 512, [[NH * 512, 64], [1, 512]]),
                psC[0][0:64, :], lsc[0:64, :], ALU.mult),
                deps=[t_lc, t_c0_last, ctxs_free])
            t_ev1 = p.on("dve", lambda v, h=h: v.tensor_tensor(
                bass.AP(ctxs, 64 * NH * 512 + h * 512, [[NH * 512, 64], [1, 512]]),
                psC[1][0:64, :], lsc[64:128, :], ALU.mult),
                deps=[t_lc, t_c1_last, ctxs_free])
            t_ev = tmax(t_ev0, t_ev1)
            lsc_free = t_ev
            psC_free = [t_ev0, t_ev1]
            ev_toks_all.append(t_ev)
        t_ctx_all = tmax(*ev_toks_all)
        qf_free = t_ctx_all

        st_deps = []
        for oc in range(16):
            par = oc % 2
            wosl = wts[:, par * NC * 512: par * NC * 512 + 8 * 128]
            t_lw = p.dma("pool", wosl,
                         bass.AP(io["wo_t"].tensor, oc * 128 * MO, [[MO, 128], [1, MO]]),
                         deps=[wts_free[par]], sem="lw%do" % par)
            t_mm = None
            for cc in range(NH):
                t_mm = p.on("pe", lambda t, oc=oc, cc=cc, par=par, wosl=wosl: t.matmul(
                    psP[1][:], wosl[:, cc * 128:(cc + 1) * 128],
                    ctxs[:, cc * 512:(cc + 1) * 512], start=(cc == 0), stop=(cc == NH - 1)),
                    deps=[tmax(t_ctx_all, t_lw, psP_free[1])] if cc == 0 else (),
                    inc=1 if cc == NH - 1 else 0)
            wts_free[par] = t_mm
            wos = wof[:, par * 512:(par + 1) * 512]
            t_ev = p.on("act", lambda s, par=par, wos=wos: s.copy(wos, psP[1][:]),
                        deps=[t_mm, wof_free])
            psP_free[1] = t_ev
            for b in range(4):
                tpar = (oc * 4 + b) % 2
                t_tp = p.on("pe", lambda t, b=b, tpar=tpar, wos=wos: t.transpose(
                    psT[tpar][:, 0:128], wos[:, b * 128:(b + 1) * 128], ident[:]),
                    deps=[t_ev, psT_free[tpar]])
                t_cp = p.on("dve", lambda v, b=b, oc=oc, tpar=tpar: v.tensor_copy(
                    part[:, b * D + oc * 128: b * D + (oc + 1) * 128], psT[tpar][:, 0:128]),
                    deps=[t_tp, part_free])
                psT_free[tpar] = t_cp
                st_deps.append(t_cp)
            wof_free = st_deps[-1]
        ctxs_free = tmax(*st_deps)
        t_st = p.dma("pool", row_ap(io["out"]),
                     bass.AP(part, 0, [[4 * D, 128], [D, 4], [1, D]]),
                     deps=[tmax(*st_deps)], sem="sto")
        part_free = t_st
        out_toks.append(t_st)
        cos_free = tmax(t_qr, t_kr)

    p.on("sp", lambda e: e.nop(), deps=out_toks, inc=0)
    return out_toks






D = 2048
NE = 16
FM = 1024
NC = 16


def build_moe(p, io, T=2048, ST=1024):
    NSUP = T // ST
    NSB = ST // 128
    NFC = FM // 128

    x3 = p.sbuf("x3", [128, NSB * D], BF16)
    scr = p.sbuf("scr", [128, NSB * D], BF16)
    ye = scr
    ysh = scr
    xf = p.sbuf("xf", [128, NC * ST], BF16)
    wgr = p.sbuf("wgr", [128, NC * NE], BF16)
    ebias = p.sbuf("ebias", [128, 128], F32)
    ebrow = p.sbuf("ebrow", [1, 128], F32)
    scores = p.sbuf("scores", [128, NSB * NE], F32)
    scb = p.sbuf("scb", [128, NSB * NE], F32)
    prs = p.sbuf("prs", [128, 6 * NSB * 4], F32)
    gsum = p.sbuf("gsum", [128, NSB * 4], F32)
    mm1 = p.sbuf("mm1", [128, NSB], F32)
    mm2 = p.sbuf("mm2", [128, NSB], F32)
    mm3 = p.sbuf("mm3", [128, NSB], F32)
    mm4 = p.sbuf("mm4", [128, NSB], F32)
    tau = p.sbuf("tau", [128, NSB], F32)
    gmsk = p.sbuf("gmsk", [128, NSB * 4], F32)
    th8 = p.sbuf("th8", [128, 8], F32)
    thr = p.sbuf("thr", [128, NSB], F32)
    emsk = p.sbuf("emsk", [128, NSB * NE], F32)
    wmun = p.sbuf("wmun", [128, NSB * NE], F32)
    den = p.sbuf("den", [128, NSB], F32)
    wm = p.sbuf("wm", [128, NSB * NE], F32)
    wmb = p.sbuf("wmb", [128, NSB * 128], BF16)
    gwb = p.sbuf("gwb", [128, 2 * D], BF16)
    uwb = p.sbuf("uwb", [128, 2 * D], BF16)
    dwb = p.sbuf("dwb", [128, 2 * FM], BF16)
    sgb = p.sbuf("sgb", [128, 2 * 512], BF16)
    ustmp = p.sbuf("ustmp", [128, 2 * 512], BF16)
    acc = p.sbuf("acc", [128, NC * ST], F32)
    dnf = p.sbuf("dnf", [128, ST], BF16)
    ident = p.sbuf("ident", [128, 128], BF16)
    id32 = p.sbuf("id32", [128, 128], F32)
    ones32r = p.sbuf("ones32r", [1, 128], F32)
    xn = p.sbuf("xn", [128, D], BF16)
    bn6 = p.sbuf("bn6", [128, 24], F32)
    bn2 = p.sbuf("bn2", [128, 2], F32)
    rstd = p.sbuf("rstd", [128, 1], F32)
    epsc = p.sbuf("epsc", [128, 1], F32)

    psA = [p.psum("psA0", [128, 512]), p.psum("psA1", [128, 512])]
    psB = [p.psum("psB0", [128, 512]), p.psum("psB1", [128, 512])]
    psG = [p.psum("psG0", [128, 512]), p.psum("psG1", [128, 512])]
    psT = [p.psum("psT0", [128, 1024], BF16), p.psum("psT1", [128, 1024], BF16)]

    t_id = p.dma("pool", ident[:], io["ident"], sem="ldc")
    p.dma("pool", id32[:], io["ident"], sem="ldc")
    p.dma("pool", wgr[:], bass.AP(io["wgT"].tensor, 0, [[NE, 128], [128 * NE, NC], [1, NE]]), sem="ldc")
    t_eb = p.dma("pool", ebrow[:], bass.AP(io["ebias_rep"].tensor, 0, [[128, 1], [1, 128]]), sem="ldc")
    t_on = p.on("pool", lambda g: g.memset(ones32r[:], 1.0))
    t_eps = p.on("pool", lambda g: g.memset(epsc[:], 1.0e-5))
    t_eb2 = p.on("pe", lambda t: t.matmul(psA[1][:, 0:128], ones32r[:], ebrow[:],
                                          start=True, stop=True), deps=[t_eb, t_on])
    t_eb3 = p.on("dve", lambda v: v.tensor_copy(ebias[:], psA[1][:, 0:128]), deps=[t_eb2])
    t_const = tmax(t_eb, t_eps, t_on, t_eb3)

    out_toks = []
    x3_free = scr_free = xf_free = None
    psA_free = [t_eb3, None]
    psB_free = [None, None]
    psG_free = [None, None]
    psT_free = [None, None]
    gw_free = [None, None]
    uw_free = [None, None]
    dw_free = [None, None]
    sg_free = [None, None]
    us_free = [None, None]
    wmb_free = None
    acc_free = None
    dnf_free = None
    xn_free = None
    route_free = None

    for sup in range(NSUP):
        row0 = sup * ST

        def row_ap(h, r0=row0):
            return bass.AP(h.tensor, h.offset + r0 * D, [[D, 128], [128 * D, NSB], [1, D]])
        x3_ap = bass.AP(x3, 0, [[NSB * D, 128], [D, NSB], [1, D]])
        pr_ap = bass.AP(scr, 0, [[NSB * D, 128], [D, NSB], [1, D]])

        t_x = p.dma("pool", x3_ap, row_ap(io["x"]), deps=[x3_free], sem="lda")
        t_p1 = p.dma("pool", pr_ap, row_ap(io["pa"]), deps=[scr_free], sem="lda")
        t_a1 = p.on("dve", lambda v: v.tensor_tensor(x3[:], x3[:], scr[:], ALU.add), deps=[t_x, t_p1])
        if io.get("pb") is not None:
            t_p2 = p.dma("pool", pr_ap, row_ap(io["pb"]), deps=[t_a1], sem="lda")
            t_a2 = p.on("dve", lambda v: v.tensor_tensor(x3[:], x3[:], scr[:], ALU.add), deps=[t_p2])
        else:
            t_a2 = t_a1
        scr_free = t_a2

        sig_toks = []
        fm_toks = []
        for b in range(NSB):
            xrb = x3[:, b * D:(b + 1) * D]
            t_bs = None
            for s4 in range(4):
                t_bs = p.on("dve", lambda v, xrb=xrb, s4=s4: v.bn_stats(
                    bn6[:, s4 * 6:(s4 + 1) * 6],
                    bass.AP(xrb.tensor, xrb.offset + s4 * 512, [[xrb.ap[0][0], 128], [1, 512]])),
                    deps=[t_a2, xn_free])
            t_ba = p.on("dve", lambda v: v.bn_aggr(bn2[:], bn6[:]), deps=[t_bs])
            t_sq = p.on("act", lambda s: s.activation(rstd[:], bn2[:, 1:2], AF.Sqrt, bias=epsc[:]),
                        deps=[t_ba, t_const])
            t_rs = p.on("dve", lambda v: v.reciprocal(rstd[:], rstd[:]), deps=[t_sq])
            t_xn = p.on("dve", lambda v, xrb=xrb: v.tensor_scalar(
                xn[:], xrb, bn2[:, 0:1], rstd[:], ALU.subtract, ALU.mult),
                deps=[t_rs, xn_free])
            tps = []
            ev1 = []
            for c in range(NC):
                par = c % 2
                t_tp = p.on("pe", lambda t, c=c, par=par: t.transpose(
                    psT[par][:, 0:128], xn[:, c * 128:(c + 1) * 128], ident[:]),
                    deps=[t_xn, t_const, psT_free[par]])
                t_e1 = p.on("dve", lambda v, c=c, b=b, par=par: v.tensor_copy(
                    xf[:, c * ST + b * 128: c * ST + (b + 1) * 128], psT[par][:, 0:128]),
                    deps=[t_tp, xf_free])
                psT_free[par] = t_e1
                tps.append(t_tp)
                fm_toks.append(t_e1)
                ev1.append(t_e1)
            xn_free = tmax(*tps)
            t_rm = None
            for c in range(NC):
                t_rm = p.on("pe", lambda t, c=c, b=b: t.matmul(
                    psA[0][:, 0:NE], xf[:, c * ST + b * 128: c * ST + (b + 1) * 128],
                    wgr[:, c * NE:(c + 1) * NE],
                    start=(c == 0), stop=(c == NC - 1)),
                    deps=[tmax(ev1[c], t_const, psA_free[0])] if c == 0 else [ev1[c]],
                    inc=1 if c == NC - 1 else 0)
            t_sg2 = p.on("act", lambda s, b=b: s.activation(
                scores[:, b * NE:(b + 1) * NE], psA[0][:, 0:NE], AF.Sigmoid),
                deps=[t_rm, route_free])
            psA_free[0] = t_sg2
            sig_toks.append(t_sg2)
        t_fm = tmax(*fm_toks)
        t_sc = tmax(*sig_toks)

        V = 128
        t0 = p.on("dve", lambda v: v.tensor_tensor(scb[:], scores[:], ebias[:], ALU.add),
                  deps=[t_sc, t_const, route_free])
        pairs = [(0, 1), (0, 2), (0, 3), (1, 2), (1, 3), (2, 3)]
        pr_toks = []
        for pi, (i, jj) in enumerate(pairs):
            pr_toks.append(p.on("dve", lambda v, pi=pi, i=i, jj=jj: v.tensor_tensor(
                bass.AP(prs, pi * 32, [[6 * 32, 128], [4, NSB], [1, 4]]),
                bass.AP(scb, i, [[NSB * NE, 128], [NE, NSB], [4, 4]]),
                bass.AP(scb, jj, [[NSB * NE, 128], [NE, NSB], [4, 4]]), ALU.add),
                deps=[t0, route_free]))
        tp = tmax(*pr_toks)
        def prsv(pi):
            return prs[:, pi * 32:(pi + 1) * 32]
        t1 = p.on("dve", lambda v: v.tensor_tensor(prsv(0), prsv(0), prsv(1), ALU.max), deps=[tp])
        t2 = p.on("dve", lambda v: v.tensor_tensor(prsv(2), prsv(2), prsv(3), ALU.max), deps=[tp])
        t3 = p.on("dve", lambda v: v.tensor_tensor(prsv(4), prsv(4), prsv(5), ALU.max), deps=[tp])
        t4 = p.on("dve", lambda v: v.tensor_tensor(prsv(0), prsv(0), prsv(2), ALU.max), deps=[t1, t2])
        t5 = p.on("dve", lambda v: v.tensor_tensor(gsum[:], prsv(0), prsv(4), ALU.max), deps=[t3, t4])
        def gv(g):
            return bass.AP(gsum, g, [[NSB * 4, 128], [4, NSB]])
        t6 = p.on("dve", lambda v: v.tensor_tensor(mm1[:], gv(0), gv(1), ALU.max), deps=[t5])
        t7 = p.on("dve", lambda v: v.tensor_tensor(mm2[:], gv(0), gv(1), ALU.min), deps=[t5])
        t8 = p.on("dve", lambda v: v.tensor_tensor(mm3[:], gv(2), gv(3), ALU.max), deps=[t5])
        t9 = p.on("dve", lambda v: v.tensor_tensor(mm4[:], gv(2), gv(3), ALU.min), deps=[t5])
        ta = p.on("dve", lambda v: v.tensor_tensor(mm1[:], mm1[:], mm3[:], ALU.min), deps=[t6, t8])
        tb = p.on("dve", lambda v: v.tensor_tensor(mm2[:], mm2[:], mm4[:], ALU.max), deps=[t7, t9])
        tc = p.on("dve", lambda v: v.tensor_tensor(tau[:], mm1[:], mm2[:], ALU.max), deps=[ta, tb])
        gm_toks = []
        for g in range(4):
            gm_toks.append(p.on("dve", lambda v, g=g: v.tensor_tensor(
                bass.AP(gmsk, g, [[NSB * 4, 128], [4, NSB]]), gv(g), tau[:], ALU.is_ge),
                deps=[tc]))
        td = p.on("dve", lambda v: v.tensor_scalar(gmsk[:], gmsk[:], 1.0, 1.0e9, ALU.subtract, ALU.mult),
                  deps=gm_toks)
        sm_toks = []
        for i in range(4):
            sm_toks.append(p.on("dve", lambda v, i=i: v.tensor_tensor(
                bass.AP(scb, i, [[NSB * NE, 128], [NE, NSB], [4, 4]]),
                bass.AP(scb, i, [[NSB * NE, 128], [NE, NSB], [4, 4]]),
                bass.AP(gmsk, 0, [[NSB * 4, 128], [4, NSB], [1, 4]]), ALU.add),
                deps=[td]))
        te = tmax(*sm_toks)
        th_toks = []
        t_th = None
        for b in range(NSB):
            t_m8 = p.on("dve", lambda v, b=b: v.max(th8[:], scb[:, b * NE:(b + 1) * NE]),
                        deps=[te, t_th])
            t_th = p.on("dve", lambda v, b=b: v.tensor_copy(thr[:, b:b + 1], th8[:, 3:4]), deps=[t_m8])
            th_toks.append(t_th)
        em_toks = []
        for b in range(NSB):
            em_toks.append(p.on("dve", lambda v, b=b: v.tensor_scalar(
                emsk[:, b * NE:(b + 1) * NE], scb[:, b * NE:(b + 1) * NE],
                thr[:, b:b + 1], None, ALU.is_ge), deps=[th_toks[b]]))
        tf = p.on("dve", lambda v: v.tensor_tensor(wmun[:], scores[:], emsk[:], ALU.mult),
                  deps=tmax(*em_toks, t_sc))
        tg = p.on("dve", lambda v: v.tensor_reduce(
            den[:], bass.AP(wmun, 0, [[NSB * NE, 128], [NE, NSB], [1, NE]]), AX.X, ALU.add),
            deps=[tf])
        th_ = p.on("dve", lambda v: v.tensor_scalar(den[:], den[:], 1.0e-20, None, ALU.add), deps=[tg])
        ti = p.on("dve", lambda v: v.reciprocal(den[:], den[:]), deps=[th_])
        tj = p.on("dve", lambda v: v.tensor_scalar(den[:], den[:], 2.5, None, ALU.mult), deps=[ti])
        wm_toks = []
        for b in range(NSB):
            wm_toks.append(p.on("dve", lambda v, b=b: v.tensor_scalar(
                wm[:, b * NE:(b + 1) * NE], wmun[:, b * NE:(b + 1) * NE],
                den[:, b:b + 1], None, ALU.mult), deps=[tj]))
        t_wm = tmax(*wm_toks)
        route_free = t_wm

        for half in range(2):
            ysh_evs = []
            for oc in range(8):
                par = oc % 2
                gws = gwb[:, par * D:(par + 1) * D]
                uws = uwb[:, par * D:(par + 1) * D]
                t_lg = p.dma("pool", gws, bass.AP(io["shg_t"].tensor, (half * 8 + oc) * 128 * D,
                                                  [[D, 128], [1, D]]), deps=[gw_free[par]], sem="lg%d" % par)
                t_lu = p.dma("pool", uws, bass.AP(io["shu_t"].tensor, (half * 8 + oc) * 128 * D,
                                                  [[D, 128], [1, D]]), deps=[uw_free[par]], sem="lu%d" % par)
                t_g = t_u = None
                for c in range(NC):
                    t_g = p.on("pe", lambda t, c=c, par=par, gws=gws: t.matmul(
                        psG[0][:], gws[:, c * 128:(c + 1) * 128], xf[:, c * ST:c * ST + 512],
                        start=(c == 0), stop=(c == NC - 1)),
                        deps=[tmax(t_fm, t_lg, psG_free[0])] if c == 0 else (), inc=0)
                    t_g = p.on("pe", lambda t, c=c, par=par, gws=gws: t.matmul(
                        psG[1][:], gws[:, c * 128:(c + 1) * 128], xf[:, c * ST + 512:(c + 1) * ST],
                        start=(c == 0), stop=(c == NC - 1)),
                        deps=[psG_free[1]] if c == 0 else (), inc=1 if c == NC - 1 else 0)
                for c in range(NC):
                    t_u = p.on("pe", lambda t, c=c, par=par, uws=uws: t.matmul(
                        psB[0][:], uws[:, c * 128:(c + 1) * 128], xf[:, c * ST:c * ST + 512],
                        start=(c == 0), stop=(c == NC - 1)),
                        deps=[tmax(t_lu, psB_free[0])] if c == 0 else (), inc=0)
                    t_u = p.on("pe", lambda t, c=c, par=par, uws=uws: t.matmul(
                        psB[1][:], uws[:, c * 128:(c + 1) * 128], xf[:, c * ST + 512:(c + 1) * ST],
                        start=(c == 0), stop=(c == NC - 1)),
                        deps=[psB_free[1]] if c == 0 else (), inc=1 if c == NC - 1 else 0)
                gw_free[par] = t_g
                uw_free[par] = t_u
                for th in range(2):
                    t_sg3 = p.on("act", lambda s, th=th: s.activation(
                        sgb[:, th * 512:(th + 1) * 512], psG[th][:], AF.Silu),
                        deps=[t_g, sg_free[th]])
                    psG_free[th] = t_sg3
                    t_ys = p.on("dve", lambda v, oc=oc, th=th: v.tensor_tensor(
                        ysh[:, NFC * ST + oc * ST + th * 512: NFC * ST + oc * ST + (th + 1) * 512],
                        psB[th][:], sgb[:, th * 512:(th + 1) * 512], ALU.mult),
                        deps=[t_u, t_sg3, scr_free])
                    sg_free[th] = t_ys
                    psB_free[th] = t_ys
                    ysh_evs.append(t_ys)
            t_ysh = tmax(*ysh_evs)
            for oc2 in range(NC):
                par = oc2 % 2
                dws = dwb[:, par * FM:(par + 1) * FM]
                t_ld = p.dma("pool", dws, bass.AP(io["shd_t"].tensor, oc2 * 128 * D + half * 1024,
                                                  [[D, 128], [1, 1024]]), deps=[dw_free[par]], sem="ld%d" % par)
                t_d = None
                for fc in range(8):
                    t_d = p.on("pe", lambda t, fc=fc, oc2=oc2, dws=dws: t.matmul(
                        psA[0][:], dws[:, fc * 128:(fc + 1) * 128],
                        scr[:, NFC * ST + fc * ST: NFC * ST + fc * ST + 512],
                        start=(fc == 0), stop=(fc == 7)),
                        deps=[tmax(t_ysh, t_ld, psA_free[0])] if fc == 0 else (), inc=0)
                    t_d = p.on("pe", lambda t, fc=fc, oc2=oc2, dws=dws: t.matmul(
                        psA[1][:], dws[:, fc * 128:(fc + 1) * 128],
                        scr[:, NFC * ST + fc * ST + 512: NFC * ST + (fc + 1) * ST],
                        start=(fc == 0), stop=(fc == 7)),
                        deps=[psA_free[1]] if fc == 0 else (), inc=1 if fc == 7 else 0)
                dw_free[par] = t_d
                for th in range(2):
                    if half == 0:
                        t_ac = p.on("dve", lambda v, oc2=oc2, th=th: v.tensor_copy(
                            acc[:, oc2 * ST + th * 512: oc2 * ST + (th + 1) * 512], psA[th][:]),
                            deps=[t_d, acc_free])
                    else:
                        t_ac = p.on("dve", lambda v, oc2=oc2, th=th: v.tensor_tensor(
                            acc[:, oc2 * ST + th * 512: oc2 * ST + (th + 1) * 512],
                            acc[:, oc2 * ST + th * 512: oc2 * ST + (th + 1) * 512], psA[th][:], ALU.add),
                            deps=[t_d])
                    psA_free[th] = t_ac
            acc_free = None

        for e in range(NE):
            wb_toks = []
            for b in range(NSB):
                t_wb = p.on("pe", lambda t, b=b, e=e: t.matmul(
                    psA[0][:, 0:128],
                    bass.AP(wm, b * NE + e, [[NSB * NE, 128], [0, 128]]), id32[:],
                    start=True, stop=True), deps=[t_wm, psA_free[0], t_const])
                t_wc = p.on("dve", lambda v, b=b: v.tensor_copy(
                    wmb[:, b * 128:(b + 1) * 128], psA[0][:, 0:128]), deps=[t_wb, wmb_free])
                psA_free[0] = t_wc
                wb_toks.append(t_wc)
            t_wmb = tmax(*wb_toks)
            ye_evs = []
            for oc in range(NFC):
                par = oc % 2
                gws = gwb[:, par * D:(par + 1) * D]
                uws = uwb[:, par * D:(par + 1) * D]
                t_lg = p.dma("pool", gws, bass.AP(io["gw_t"].tensor, (e * NFC + oc) * 128 * D,
                                                  [[D, 128], [1, D]]), deps=[gw_free[par]], sem="lg%d" % par)
                t_lu = p.dma("pool", uws, bass.AP(io["uw_t"].tensor, (e * NFC + oc) * 128 * D,
                                                  [[D, 128], [1, D]]), deps=[uw_free[par]], sem="lu%d" % par)
                t_g = t_u = None
                for c in range(NC):
                    t_g = p.on("pe", lambda t, c=c, gws=gws: t.matmul(
                        psG[0][:], gws[:, c * 128:(c + 1) * 128], xf[:, c * ST:c * ST + 512],
                        start=(c == 0), stop=(c == NC - 1)),
                        deps=[tmax(t_fm, t_lg, psG_free[0])] if c == 0 else (), inc=0)
                    t_g = p.on("pe", lambda t, c=c, gws=gws: t.matmul(
                        psG[1][:], gws[:, c * 128:(c + 1) * 128], xf[:, c * ST + 512:(c + 1) * ST],
                        start=(c == 0), stop=(c == NC - 1)),
                        deps=[psG_free[1]] if c == 0 else (), inc=1 if c == NC - 1 else 0)
                for c in range(NC):
                    t_u = p.on("pe", lambda t, c=c, uws=uws: t.matmul(
                        psB[0][:], uws[:, c * 128:(c + 1) * 128], xf[:, c * ST:c * ST + 512],
                        start=(c == 0), stop=(c == NC - 1)),
                        deps=[tmax(t_lu, psB_free[0])] if c == 0 else (), inc=0)
                    t_u = p.on("pe", lambda t, c=c, uws=uws: t.matmul(
                        psB[1][:], uws[:, c * 128:(c + 1) * 128], xf[:, c * ST + 512:(c + 1) * ST],
                        start=(c == 0), stop=(c == NC - 1)),
                        deps=[psB_free[1]] if c == 0 else (), inc=1 if c == NC - 1 else 0)
                gw_free[par] = t_g
                uw_free[par] = t_u
                for th in range(2):
                    t_sg3 = p.on("act", lambda s, th=th: s.activation(
                        sgb[:, th * 512:(th + 1) * 512], psG[th][:], AF.Silu),
                        deps=[t_g, sg_free[th]])
                    psG_free[th] = t_sg3
                    t_us = p.on("dve", lambda v, th=th: v.tensor_tensor(
                        ustmp[:, th * 512:(th + 1) * 512], psB[th][:],
                        bass.AP(wmb, th * 512, [[NSB * 128, 128], [1, 512]]), ALU.mult),
                        deps=[t_u, t_wmb, us_free[th]])
                    psB_free[th] = t_us
                    t_ye = p.on("dve", lambda v, oc=oc, th=th: v.tensor_tensor(
                        ye[:, oc * ST + th * 512: oc * ST + (th + 1) * 512],
                        ustmp[:, th * 512:(th + 1) * 512], sgb[:, th * 512:(th + 1) * 512], ALU.mult),
                        deps=[t_us, t_sg3])
                    sg_free[th] = t_ye
                    us_free[th] = t_ye
                    ye_evs.append(t_ye)
            wmb_free = tmax(*ye_evs)
            t_ye_all = tmax(*ye_evs)
            for oc2 in range(NC):
                par = oc2 % 2
                dws = dwb[:, par * FM:(par + 1) * FM]
                t_ld = p.dma("pool", dws, bass.AP(io["dw_t"].tensor, (e * NC + oc2) * 128 * FM,
                                                  [[FM, 128], [1, FM]]), deps=[dw_free[par]], sem="ld%d" % par)
                t_d = None
                for fc in range(NFC):
                    t_d = p.on("pe", lambda t, fc=fc, dws=dws: t.matmul(
                        psA[0][:], dws[:, fc * 128:(fc + 1) * 128], ye[:, fc * ST: fc * ST + 512],
                        start=(fc == 0), stop=(fc == NFC - 1)),
                        deps=[tmax(t_ye_all, t_ld, psA_free[0])] if fc == 0 else (), inc=0)
                    t_d = p.on("pe", lambda t, fc=fc, dws=dws: t.matmul(
                        psA[1][:], dws[:, fc * 128:(fc + 1) * 128], ye[:, fc * ST + 512:(fc + 1) * ST],
                        start=(fc == 0), stop=(fc == NFC - 1)),
                        deps=[psA_free[1]] if fc == 0 else (), inc=1 if fc == NFC - 1 else 0)
                dw_free[par] = t_d
                for th in range(2):
                    t_ac = p.on("dve", lambda v, oc2=oc2, th=th: v.tensor_tensor(
                        acc[:, oc2 * ST + th * 512: oc2 * ST + (th + 1) * 512],
                        acc[:, oc2 * ST + th * 512: oc2 * ST + (th + 1) * 512], psA[th][:], ALU.add),
                        deps=[t_d])
                    psA_free[th] = t_ac
            scr_free = tmax(*[psA_free[0], psA_free[1]])

        st_deps = []
        t_acc_all = tmax(psA_free[0], psA_free[1])
        for oc2 in range(NC):
            t_cp = p.on("dve", lambda v, oc2=oc2: v.tensor_copy(
                dnf[:], acc[:, oc2 * ST:(oc2 + 1) * ST]), deps=[t_acc_all, dnf_free])
            tp_toks = []
            for b in range(NSB):
                tpar = b // 4
                t_tp = p.on("pe", lambda t, b=b, tpar=tpar: t.transpose(
                    psT[tpar][:, (b % 4) * 128:((b % 4) + 1) * 128],
                    dnf[:, b * 128:(b + 1) * 128], ident[:]),
                    deps=[t_cp, t_const, psT_free[tpar]])
                tp_toks.append(t_tp)
            for tpar in range(2):
                t_ad = p.on("dve", lambda v, oc2=oc2, tpar=tpar: v.tensor_tensor(
                    bass.AP(x3, tpar * 4 * D + oc2 * 128, [[NSB * D, 128], [D, 4], [1, 128]]),
                    bass.AP(x3, tpar * 4 * D + oc2 * 128, [[NSB * D, 128], [D, 4], [1, 128]]),
                    bass.AP(psT[tpar], 0, [[1024, 128], [128, 4], [1, 128]]), ALU.add),
                    deps=tp_toks + [t_a2])
                psT_free[tpar] = t_ad
                st_deps.append(t_ad)
            dnf_free = tmax(*st_deps)
        t_st = p.dma("pool", row_ap(io["out"]), x3_ap, deps=[tmax(*st_deps)], sem="sto")
        x3_free = t_st
        acc_free = t_acc_all
        xf_free = scr_free
        out_toks.append(t_st)

    p.on("sp", lambda e: e.nop(), deps=out_toks, inc=0)
    return out_toks





SEQ = 4096
DM = 2048


def _mk_consts():
    i = np.arange(128, dtype=np.uint64)
    pc = np.array([bin(v).count("1") for v in range(128 * 2)], dtype=np.int64)
    popc = pc[(i[:, None] & i[None, :])]
    h128 = np.where(popc % 2 == 0, 1.0, -1.0).astype(np.float32)
    ident = np.eye(128, dtype=np.float32)
    r = np.arange(128)
    dmask = np.where(r[None, :] <= r[:, None], 0.0, -1e9).astype(np.float32)
    dmaskT = np.ascontiguousarray(dmask.T)
    amask = np.where(r[None, :] > r[:, None], 0.0, -1e9).astype(np.float32)
    amaskT = np.ascontiguousarray(amask.T)
    inv = 1.0 / 10000.0 ** (np.arange(0, 128, 2) / 128.0)
    fr = np.arange(SEQ)[:, None] * inv[None, :]
    cos_rep = np.tile(np.cos(fr), (1, 8)).astype(np.float32)
    sin_rep = np.tile(np.sin(fr), (1, 8)).astype(np.float32)
    return h128, ident, dmask, dmaskT, amask, amaskT, cos_rep, sin_rep


def _perm_gqa():
    perm = []
    for oc in range(8):
        for bhalf in (0, 1):
            g = (oc // 4) * 2 + bhalf
            h = g * 4 + (oc % 4)
            perm.extend(range(h * 64, (h + 1) * 64))
    return np.array(perm)


def _f32(a):
    return np.ascontiguousarray(a, dtype=np.float32)


PAIRS = [[0, 1], [2, 3], [4, 5], [6, 7]]


def kernel(**inp):
    h128, ident, dmask, dmaskT, amask, amaskT, cos_rep, sin_rep = _mk_consts()
    perm = _perm_gqa()
    x = _f32(inp["L0_attn_x"]).reshape(4, SEQ, DM)

    wq_t = [host_tile_T(_f32(inp["L0_attn_Wq"][1024 * m:1024 * (m + 1), :])[perm]) for m in (0, 1)]
    wk_t = [host_tile_T(_f32(inp["L0_attn_Wk"][256 * m:256 * (m + 1), :])) for m in (0, 1)]
    wvT = [_f32(inp["L0_attn_Wv"][256 * m:256 * (m + 1), :].T) for m in (0, 1)]
    wo_t = [host_tile_T(_f32(inp["L0_attn_Wo"][:, 1024 * m:1024 * (m + 1)][:, perm])) for m in (0, 1)]
    wupT = host_tile_T(_f32(inp["L0_ffn_W_up"]))
    wgT = host_tile_T(_f32(inp["L0_ffn_W_gate"]))
    wdT = host_tile_T(_f32(inp["L0_ffn_W_down"]))
    wqT1 = [_f32(inp["L1_attn_Wq"][1024 * m:1024 * (m + 1), :].T) for m in (0, 1)]
    wkT1 = [_f32(inp["L1_attn_Wk"][1024 * m:1024 * (m + 1), :].T) for m in (0, 1)]
    wvT1 = [_f32(inp["L1_attn_Wv"][1024 * m:1024 * (m + 1), :].T) for m in (0, 1)]
    wo_t1 = [host_tile_T(_f32(inp["L1_attn_Wo"][:, 1024 * m:1024 * (m + 1)])) for m in (0, 1)]
    gw_t = np.stack([host_tile_T(_f32(inp["L1_ffn_gate_w"][e])) for e in range(16)])
    uw_t = np.stack([host_tile_T(_f32(inp["L1_ffn_up_w"][e])) for e in range(16)])
    dw_t = np.stack([host_tile_T(_f32(inp["L1_ffn_down_w"][e])) for e in range(16)])
    shg_t = host_tile_T(_f32(inp["L1_ffn_sh_gate_w"]))
    shu_t = host_tile_T(_f32(inp["L1_ffn_sh_up_w"]))
    shd_t = host_tile_T(_f32(inp["L1_ffn_sh_down_w"]))
    wgT1 = _f32(inp["L1_ffn_gate_weight"].T)
    eb = np.tile(_f32(inp["L1_ffn_e_score_correction_bias"]), 8)
    su = _f32(inp["L0_ffn_SU_ug"])
    sd = _f32(inp["L0_ffn_SU_dp"])
    wsu = _f32(inp["L0_ffn_Wscale_ug"])
    wsd = _f32(inp["L0_ffn_Wscale_dp"])

    nc = bass.Bass("TRN2", target_bir_lowering=False)
    ext = {}
    especs = {
        "xA": (SEQ, DM), "xB": (2048, DM),
        "ident": (128, 128), "dmask": (128, 128), "dmaskt": (128, 128), "amask": (128, 128), "amaskt": (128, 128), "h128": (128, 128),
        "cos_rep": cos_rep.shape, "sin_rep": sin_rep.shape,
        "a_wq_t": wq_t[0].shape, "a_wk_t": wk_t[0].shape,
        "a_wvT": wvT[0].shape, "a_wo_t": wo_t[0].shape,
        "b_wupT": wupT.shape, "b_wgT": wgT.shape, "b_wdT": wdT.shape,
        "b_su_ug": su.shape, "b_su_dp": sd.shape,
        "b_wsc_ug": wsu.shape, "b_wsc_dp": wsd.shape,
        "c_wqT": wqT1[0].shape, "c_wkT": wkT1[0].shape,
        "c_wvT": wvT1[0].shape, "c_wo_t": wo_t1[0].shape,
        "d_wgT": wgT1.shape, "d_ebias": eb.shape,
        "d_gw_t": gw_t.shape, "d_uw_t": uw_t.shape, "d_dw_t": dw_t.shape,
        "d_shg_t": shg_t.shape, "d_shu_t": shu_t.shape, "d_shd_t": shd_t.shape,
    }
    BF16W = {"a_wq_t", "a_wk_t", "a_wvT", "a_wo_t",
             "b_wupT", "b_wgT", "b_wdT",
             "c_wqT", "c_wkT", "c_wvT", "c_wo_t",
             "d_gw_t", "d_uw_t", "d_dw_t", "d_shg_t", "d_shu_t", "d_shd_t"}
    for n, s in especs.items():
        dt = BF16 if n in BF16W else F32
        ext[n] = nc.dram_tensor(n, list(s), dt, kind="ExternalInput").ap()
    out_t = nc.dram_tensor("out", [2048, DM], F32, kind="ExternalOutput").ap()

    pA_loc = nc.dram_tensor("pA_loc", [SEQ, DM], BF16, kind="Internal")
    pA_red = nc.dram_tensor("pA_red", [2048, DM], BF16, kind="Internal")
    x2_half = nc.dram_tensor("x2_half", [2048, DM], BF16, kind="Internal")
    x2_full = nc.dram_tensor("x2_full", [SEQ, DM], BF16, kind="Internal")
    pC_loc = nc.dram_tensor("pC_loc", [SEQ, DM], BF16, kind="Internal")
    pC_red = nc.dram_tensor("pC_red", [2048, DM], BF16, kind="Internal")

    p = Prog(nc)

    p.begin_phase("a_")
    ioA = {"x": ext["xA"], "wq_t": ext["a_wq_t"], "wk_t": ext["a_wk_t"],
           "wvT": ext["a_wvT"], "wo_t": ext["a_wo_t"],
           "ident": ext["ident"], "dmaskt": ext["dmaskt"], "out": pA_loc.ap()}
    toksA = build_attn0(p, ioA, S=SEQ)
    p.cc("ReduceScatter", ALU.add, [pA_loc.ap().opt()], [pA_red.ap().opt()],
         PAIRS, deps=[tmax(*toksA)], name="cc1")
    p.fence()

    p.begin_phase("b_")
    ioB = {"x": ext["xB"], "pa": pA_red.ap(), "pb": None, "out": x2_half.ap(),
           "wupT": ext["b_wupT"], "wgT": ext["b_wgT"], "wdT": ext["b_wdT"],
           "su_ug": ext["b_su_ug"], "su_dp": ext["b_su_dp"],
           "wsc_ug": ext["b_wsc_ug"], "wsc_dp": ext["b_wsc_dp"],
           "h128": ext["h128"], "ident": ext["ident"]}
    toksB = build_ffn(p, ioB, T=2048, TT=512)
    p.cc("AllGather", ALU.bypass, [x2_half.ap().opt()], [x2_full.ap().opt()],
         PAIRS, deps=[tmax(*toksB)], name="cc2")
    p.fence()

    p.begin_phase("c_")
    ioC = {"x": x2_full.ap(), "wqT": ext["c_wqT"], "wkT": ext["c_wkT"],
           "wvT": ext["c_wvT"], "wo_t": ext["c_wo_t"],
           "cos_rep": ext["cos_rep"], "sin_rep": ext["sin_rep"],
           "ident": ext["ident"], "dmaskt": ext["dmaskt"], "amaskt": ext["amaskt"],
           "out": pC_loc.ap()}
    toksC = build_attn1(p, ioC, S=SEQ)
    p.cc("ReduceScatter", ALU.add, [pC_loc.ap().opt()], [pC_red.ap().opt()],
         PAIRS, deps=[tmax(*toksC)], name="cc3")
    p.fence()

    p.begin_phase("d_")
    ioD = {"x": x2_half.ap(), "pa": pC_red.ap(), "pb": None, "out": out_t,
           "wgT": ext["d_wgT"], "ebias_rep": ext["d_ebias"],
           "gw_t": ext["d_gw_t"], "uw_t": ext["d_uw_t"], "dw_t": ext["d_dw_t"],
           "shg_t": ext["d_shg_t"], "shu_t": ext["d_shu_t"], "shd_t": ext["d_shd_t"],
           "ident": ext["ident"]}
    build_moe(p, ioD, T=2048, ST=1024)
    p.finalize()

    def _bf(a):
        return np.ascontiguousarray(a.astype(ml_dtypes.bfloat16))
    wq_t = [_bf(a) for a in wq_t]
    wk_t = [_bf(a) for a in wk_t]
    wvT = [_bf(a) for a in wvT]
    wo_t = [_bf(a) for a in wo_t]
    wupT, wgT, wdT = _bf(wupT), _bf(wgT), _bf(wdT)
    wqT1 = [_bf(a) for a in wqT1]
    wkT1 = [_bf(a) for a in wkT1]
    wvT1 = [_bf(a) for a in wvT1]
    wo_t1 = [_bf(a) for a in wo_t1]
    gw_t, uw_t, dw_t = _bf(gw_t), _bf(uw_t), _bf(dw_t)
    shg_t, shu_t, shd_t = _bf(shg_t), _bf(shu_t), _bf(shd_t)

    in_maps = []
    for c in range(8):
        pq, m = c // 2, c % 2
        rows = slice(2048 * m, 2048 * (m + 1))
        in_maps.append({
            "xA": _f32(x[pq]), "xB": _f32(x[pq][rows]),
            "ident": ident, "dmask": dmask, "dmaskt": dmaskT, "amask": amask, "amaskt": amaskT, "h128": h128,
            "cos_rep": cos_rep, "sin_rep": sin_rep,
            "a_wq_t": wq_t[m], "a_wk_t": wk_t[m], "a_wvT": wvT[m], "a_wo_t": wo_t[m],
            "b_wupT": wupT, "b_wgT": wgT, "b_wdT": wdT,
            "b_su_ug": su, "b_su_dp": sd, "b_wsc_ug": wsu, "b_wsc_dp": wsd,
            "c_wqT": wqT1[m], "c_wkT": wkT1[m], "c_wvT": wvT1[m], "c_wo_t": wo_t1[m],
            "d_wgT": wgT1, "d_ebias": eb,
            "d_gw_t": gw_t, "d_uw_t": uw_t, "d_dw_t": dw_t,
            "d_shg_t": shg_t, "d_shu_t": shu_t, "d_shd_t": shd_t,
        })
    res = run_bass_kernel_spmd(nc, in_maps, core_ids=list(range(8)))
    oh = [res.results[c]["out"] for c in range(8)]
    p.close()
    out = np.stack([np.concatenate([oh[2 * pq], oh[2 * pq + 1]], axis=0) for pq in range(4)])
    return out.astype(np.float32)
